# version 02
# baseline (speedup 1.0000x reference)
import contextlib
import numpy as np
import concourse.bass as bass
import concourse.mybir as mybir
from concourse.bass_utils import run_bass_kernel_spmd

F32 = mybir.dt.float32
BF16 = mybir.dt.bfloat16
AF = mybir.ActivationFunctionType
ALU = mybir.AluOpType
AX = mybir.AxisListType

D = 1024
DIN = 2048
DFF = 2048
T = 4096
TC = 512
NCH = T // TC
NS = 16
DTR = 64
NE = 8
KD = D // 128
KI = DIN // 128
MD = D // 128
MI = DIN // 128

N_CORES = 8
DEBUG_CHUNKS = ()


class PB:

    ENGS = ("tensor", "vector", "scalar", "gpsimd", "sync")

    def __init__(self, nc, stack, self_chain=(), dedup=True):
        self.nc = nc
        self.stack = stack
        self.dedup = dedup
        self.order = {e: [] for e in self.ENGS}
        self.semh = {}
        self.semc = {}
        for name in ("pe", "ve", "se", "ge"):
            self.sem(name)
        self.waited = {e: {} for e in self.ENGS}
        self.self_chain = {"tensor": "pe", "vector": "ve", "scalar": "se",
                           "gpsimd": "ge"}
        self.self_chain = {k: v for k, v in self.self_chain.items()
                           if k in self_chain}

    def sem(self, name):
        if name not in self.semh:
            self.semh[name] = self.stack.enter_context(self.nc.semaphore(name))
            self.semc[name] = 0
        return name

    def emit(self, eng, build, waits=(), inc=None, inck=1):
        chain = self.self_chain.get(eng)
        if chain is not None:
            waits = list(waits) + [(chain, self.semc[chain])]
            if inc is None:
                inc = chain
        w = []
        for (s, v) in waits:
            if v is None or v <= 0:
                continue
            if self.dedup and self.waited[eng].get(s, 0) >= v:
                continue
            if self.waited[eng].get(s, 0) < v:
                self.waited[eng][s] = v
            if not any(h is self.semh[s] and vv >= v for (h, vv) in w):
                w.append((self.semh[s], v))
        val = None
        if inc is not None:
            self.semc[inc] += inck
            val = self.semc[inc]
            hinc = self.semh[inc]
        w = tuple(w)

        if inc is None:
            def thunk(e, _w=w, _b=build):
                for (h, v) in _w:
                    e.wait_ge(h, v)
                _b(e)
        else:
            def thunk(e, _w=w, _b=build, _h=hinc, _k=inck):
                for (h, v) in _w:
                    e.wait_ge(h, v)
                _b(e).then_inc(_h, _k)
        self.order[eng].append(thunk)
        if getattr(self, "trace", None) is not None:
            import traceback
            fr = traceback.extract_stack(limit=3)[0]
            self.trace.append((eng, fr.lineno, [(s.name if hasattr(s, "name") else s, v) for (s, v) in w], inc, val))
        return val

    def now(self, sem):
        return self.semc[sem]


def build_program(debug_chunks=(), sim_edges=()):
    nc = bass.Bass("TRN2", target_bir_lowering=False)

    dr = {}
    def din(name, shape, dtype=F32):
        dr[name] = nc.dram_tensor(name, list(shape), dtype, kind="ExternalInput").ap()
        return dr[name]

    xT = din("xT", [D, T])
    ipw = din("ipw", [D, 2 * DIN])
    opw = din("opw", [DIN, D])
    xpw = din("xpw", [DIN, 96])
    dtw = din("dtw", [DTR, DIN])
    w1 = din("w1", [NE, D, DFF])
    w2 = din("w2", [NE, DFF, D])
    gw = din("gw", [NE, D, DFF])
    uw = din("uw", [NE, D, DFF])
    dw = din("dw", [NE, DFF, D])
    rtw = din("rtw", [D, NE])
    frw = din("frw", [D, NE])
    norm1c = din("norm1c", [128, KD])
    norm2c = din("norm2c", [128, KD])
    convw_c = din("convw_c", [128, MI * 4])
    convb_c = din("convb_c", [128, MI])
    dtb_c = din("dtb_c", [128, MI])
    Dp_c = din("Dp_c", [128, MI])
    Alog_c = din("Alog_c", [128, MI * NS])
    b1_c = din("b1_c", [128, NE * MI])
    b2_c = din("b2_c", [128, NE * MD])
    gb_c = din("gb_c", [128, NE * MI])
    ub_c = din("ub_c", [128, NE * MI])
    db_c = din("db_c", [128, NE * MD])
    sel16_c = din("sel16_c", [16, 16 * 128], BF16)
    ident_c = din("ident_c", [128, 128])
    onescol_c = din("onescol_c", [128, 1], BF16)
    onesrow_c = din("onesrow_c", [1, 128], BF16)
    misc_c = din("misc_c", [128, 4])

    out = nc.dram_tensor("out", [D, T], F32, kind="ExternalOutput").ap()

    ipw_bf = nc.dram_tensor("ipw_bf", [D, 2 * DIN], BF16, kind="Internal").ap()
    opw_bf = nc.dram_tensor("opw_bf", [DIN, D], BF16, kind="Internal").ap()
    w1_bf = nc.dram_tensor("w1_bf", [NE, D, DFF], BF16, kind="Internal").ap()
    w2_bf = nc.dram_tensor("w2_bf", [NE, DFF, D], BF16, kind="Internal").ap()
    gw_bf = nc.dram_tensor("gw_bf", [NE, D, DFF], BF16, kind="Internal").ap()
    uw_bf = nc.dram_tensor("uw_bf", [NE, D, DFF], BF16, kind="Internal").ap()
    dw_bf = nc.dram_tensor("dw_bf", [NE, DFF, D], BF16, kind="Internal").ap()

    taps = {}

    with contextlib.ExitStack() as stack:
        def sb(name, shape, dtype=F32):
            return stack.enter_context(nc.sbuf_tensor(name, list(shape), dtype))
        def ps(name, shape, dtype=F32):
            return stack.enter_context(nc.psum_tensor(name, list(shape), dtype))

        c_norm1 = sb("c_norm1", [128, KD])
        c_norm2 = sb("c_norm2", [128, KD])
        c_convw = sb("c_convw", [128, MI * 4])
        c_convb = sb("c_convb", [128, MI])
        c_dtb = sb("c_dtb", [128, MI])
        c_Dp = sb("c_Dp", [128, MI])
        c_Alog = sb("c_Alog", [128, MI * NS])
        c_b1 = sb("c_b1", [128, NE * MI])
        c_b2 = sb("c_b2", [128, NE * MD])
        c_gb = sb("c_gb", [128, NE * MI])
        c_ub = sb("c_ub", [128, NE * MI])
        c_db = sb("c_db", [128, NE * MD])
        c_sel16 = sb("c_sel16", [16, 16 * 128], BF16)
        c_ident = sb("c_ident", [128, 128])
        c_onescol = sb("c_onescol", [128, 1], BF16)
        c_onesrow = sb("c_onesrow", [1, 128], BF16)
        c_misc = sb("c_misc", [128, 4])

        r_xpw = sb("r_xpw", [128, KI * 96], BF16)
        r_dtw = sb("r_dtw", [DTR, DIN], BF16)
        r_rtw = sb("r_rtw", [128, KD * NE], BF16)
        r_frw = sb("r_frw", [128, KD * NE], BF16)

        NRING = 2
        wring = [sb(f"wring{i}", [128, 4096], BF16) for i in range(NRING)]

        x0b = [sb(f"x0_{i}", [128, MD * TC]) for i in range(2)]
        t1b = sb("t1b", [128, MD * TC])
        h1b = sb("h1b", [128, MD * TC], BF16)
        htb = sb("htb", [128, MD * TC], BF16)
        zb = sb("zb", [128, MI * TC], BF16)
        xcb = sb("xcb", [128, MI * TC], BF16)
        tailb = sb("tailb", [128, MI * 3], BF16)
        dtrb = sb("dtrb", [DTR, TC], BF16)
        Bsb = sb("Bsb", [NS, TC], BF16)
        Csb = sb("Csb", [NS, TC], BF16)
        dtb2 = sb("dtb2", [128, TC])
        wb = sb("wb", [128, TC])
        dAb = sb("dAb", [128, 2 * TC])
        ub = sb("ub", [128, TC])
        hb = sb("hb", [128, TC])
        yb = sb("yb", [128, TC])
        carry = sb("carry", [128, MI * NS])
        hidb = sb("hidb", [128, KI * TC], BF16)
        accb = sb("accb", [128, MD * TC])
        bc = [sb(f"bc{i}", [128, TC]) for i in range(3)]
        Gsb = sb("Gsb", [128, TC])
        Get = sb("Get", [NE, TC], BF16)
        Pet = sb("Pet", [NE, TC], BF16)
        Gtb = sb("Gtb", [128, 32])
        RLb = sb("RLb", [NE, TC])
        strow = sb("strow", [1, 1024])
        stbf = sb("stbf", [1, 512], BF16)
        stsrc = sb("stsrc", [128, 2 * TC], BF16)
        pool = [sb(f"pool{i}", [128, TC]) for i in range(5)]

        pm = [ps(f"pm{i}", [128, TC]) for i in range(4)]
        pbA = ps("pbA", [128, TC])
        pbB = ps("pbB", [128, TC])
        ps6 = ps("ps6", [1, TC])
        ps7 = ps("ps7", [1, TC])

        pb = PB(nc, stack, self_chain=sim_edges,
                dedup=not sim_edges)
        E = pb.emit

        pm_edge = [None] * 4
        pm_i = [0]
        pbA_edge = [None]
        pbB_edge = [None]
        pool_edge = [None] * len(pool)
        pool_i = [0]

        def next_pm():
            i = pm_i[0] % 4
            pm_i[0] += 1
            return i, pm[i], (pm_edge[i],) if pm_edge[i] else ()

        def next_pool():
            i = pool_i[0] % len(pool)
            pool_i[0] += 1
            return i, pool[i], (pool_edge[i],) if pool_edge[i] else ()

        def gload(dst_ap, src_ap, sem="glc"):
            return E("gpsimd", lambda g, d=dst_ap, s=src_ap: g.dma_start(d, s),
                     inc=pb.sem(sem), inck=16)

        for dst, src in [
            (c_norm1[:], norm1c), (c_norm2[:], norm2c), (c_convw[:], convw_c),
            (c_convb[:], convb_c), (c_dtb[:], dtb_c), (c_Dp[:], Dp_c),
            (c_Alog[:], Alog_c), (c_b1[:], b1_c), (c_b2[:], b2_c),
            (c_gb[:], gb_c), (c_ub[:], ub_c), (c_db[:], db_c),
            (c_sel16[:], sel16_c), (c_ident[:], ident_c),
            (c_onescol[:], onescol_c), (c_onesrow[:], onesrow_c),
            (c_misc[:], misc_c),
        ]:
            gload(dst, src)
        gload(r_xpw[:].rearrange("p (kt m) -> p kt m", m=96),
              xpw.rearrange("(kt p) m -> p kt m", p=128))
        gload(r_dtw[:], dtw)
        gload(r_rtw[:].rearrange("p (kt m) -> p kt m", m=NE),
              rtw.rearrange("(kt p) m -> p kt m", p=128))
        gload(r_frw[:].rearrange("p (kt m) -> p kt m", m=NE),
              frw.rearrange("(kt p) m -> p kt m", p=128))
        def x0_load(c, waits=()):
            dst = x0b[c % 2][:].rearrange("p (kt t) -> p kt t", t=TC)
            src = xT.rearrange("(kt p) t -> p kt t", p=128)[:, :, c * TC:(c + 1) * TC]
            sem = pb.sem(f"glx{c % 2}")
            v = E("gpsimd", lambda g, d=dst, s=src: g.dma_start(d, s),
                  waits=waits, inc=sem, inck=16)
            return (sem, v)
        x0_ready = [None] * NCH
        x0_ready[0] = x0_load(0)
        x0_ready[1] = x0_load(1)
        consts_gl = pb.now("glc")

        prep = {}
        def prep_dma(key, dst_ap, src_ap):
            sem = pb.sem(f"gp{len(prep)}")
            v = E("gpsimd", lambda g, d=dst_ap, s=src_ap: g.dma_start(d, s),
                  inc=sem, inck=16)
            prep[key] = (sem, v)
        prep_dma("ipw", ipw_bf, ipw)
        prep_dma("opw", opw_bf, opw)
        for e in range(NE):
            prep_dma(("w1", e), w1_bf[e], w1[e])
            prep_dma(("w2", e), w2_bf[e], w2[e])
        for e in range(NE):
            prep_dma(("gw", e), gw_bf[e], gw[e])
            prep_dma(("uw", e), uw_bf[e], uw[e])
        for e in range(NE):
            prep_dma(("dw", e), dw_bf[e], dw[e])

        vA = E("scalar", lambda s: s.activation(c_Alog[:], c_Alog[:], AF.Exp),
               waits=[("glc", consts_gl)], inc="se")
        vA2 = E("vector", lambda v: v.tensor_scalar(c_Alog[:], c_Alog[:], -1.0, None, ALU.mult),
                waits=[("se", vA)], inc="ve")
        E("vector", lambda v: v.memset(carry[:], 0.0), inc="ve")
        vz2 = E("vector", lambda v: v.memset(tailb[:], 0.0), inc="ve")

        ring_state = {"n": 0, "slot_edge": [None] * NRING}
        for _i in range(NRING):
            pb.sem(f"wl{_i}")

        def ring_load(key, src_view):
            slot = ring_state["n"] % NRING
            sem = f"wl{slot}"
            waits = []
            if key in prep:
                waits.append(prep[key])
            if ring_state["slot_edge"][slot] is not None:
                waits.append(ring_state["slot_edge"][slot])
                waits.append((sem, pb.now(sem)))
            dst = wring[slot][:].rearrange(
                "p (kt m) -> p kt m", m=src_view.shape[2])
            wl = E("sync", lambda sy, d=dst, s=src_view: sy.dma_start(d, s),
                   waits=waits, inc=sem, inck=16)
            ring_state["n"] += 1
            return slot, (sem, wl)

        def ring_release(slot, edge):
            ring_state["slot_edge"][slot] = edge

        barrier = []

        def tap(name, c, src_ap, shape, dtype, wait):
            if c not in debug_chunks:
                return
            key = f"tap_{name}_{c}"
            t = nc.dram_tensor(key, list(shape), dtype, kind="ExternalOutput").ap()
            taps[key] = (name, c)
            v = E("gpsimd", lambda g, d=t, s=src_ap: g.dma_start(d, s),
                  waits=[wait] + list(barrier), inc=pb.sem("tl"), inck=16)
            for eng in ("tensor", "vector", "scalar", "gpsimd"):
                E(eng, lambda x: x.nop(), waits=[("tl", v)])

        def layer_norm_rows(xbuf, nkt, c, kind, src_waits=None):
            need_mean = kind in ("ln1", "ln2")
            sq_v = mu_v = None
            for kt in range(nkt):
                xs = xbuf[:, kt * TC:(kt + 1) * TC]
                w_ = []
                if kind == "ln1":
                    w_.append(x0_ready[c])
                if src_waits is not None:
                    w_.append(src_waits[kt])
                pe_prev = stats_edge[0]
                if pe_prev:
                    w_.append(pe_prev)
                if need_mean:
                    cv = E("vector", lambda v, a=xs: v.tensor_copy(
                        stsrc[:, 0:TC], a), waits=w_, inc="ve")
                sv = E("scalar", lambda s, a=xs: s.activation(
                    stsrc[:, TC:2 * TC], a, AF.Square), waits=w_, inc="se")
                st = (kt == 0)
                sp = (kt == nkt - 1)
                if need_mean:
                    mu_v = E("tensor", lambda t, st=st, sp=sp: t.matmul(
                        ps6[0:1, :], c_onescol[:, 0:1], stsrc[:, 0:TC],
                        start=st, stop=sp),
                        waits=[("ve", cv), ("glc", consts_gl)], inc="pe")
                sq_v = E("tensor", lambda t, st=st, sp=sp: t.matmul(
                    ps7[0:1, :], c_onescol[:, 0:1], stsrc[:, TC:2 * TC],
                    start=st, stop=sp),
                    waits=[("se", sv), ("glc", consts_gl)], inc="pe")
                stats_edge[0] = ("pe", sq_v)
            out = {}
            if need_mean:
                v1 = E("vector", lambda v: v.tensor_scalar(
                    strow[0:1, 0:TC], ps6[0:1, :], 1.0 / D, None, ALU.mult),
                    waits=[("pe", mu_v)], inc="ve")
                v2 = E("vector", lambda v: v.tensor_scalar(
                    strow[0:1, TC:2 * TC], ps7[0:1, :], 1.0 / D, None, ALU.mult),
                    waits=[("pe", sq_v)], inc="ve")
                v3 = E("vector", lambda v: v.tensor_tensor(
                    pool[4][0:1, 0:TC], strow[0:1, 0:TC], strow[0:1, 0:TC],
                    ALU.mult), waits=[("ve", v1)], inc="ve")
                v4 = E("vector", lambda v: v.tensor_tensor(
                    strow[0:1, TC:2 * TC], strow[0:1, TC:2 * TC],
                    pool[4][0:1, 0:TC], ALU.subtract),
                    waits=[("ve", v3)], inc="ve")
                s1 = E("scalar", lambda s: s.activation(
                    pool[4][0:1, 0:TC], strow[0:1, TC:2 * TC], AF.Ln,
                    bias=c_misc[0:1, 0:1]), waits=[("ve", v4)], inc="se")
                s2 = E("scalar", lambda s: s.activation(
                    pool[4][0:1, TC:2 * TC] if False else pool[4][0:1, 0:TC],
                    pool[4][0:1, 0:TC], AF.Exp, scale=-0.5),
                    waits=[("se", s1)], inc="se")
                cb1 = E("vector", lambda v: v.tensor_copy(
                    stbf[0:1, :], strow[0:1, 0:TC]), waits=[("ve", v4)], inc="ve")
                b1v = E("tensor", lambda t: t.matmul(
                    pbA[:], c_onesrow[0:1, :], stbf[0:1, :], start=True,
                    stop=True), waits=[("ve", cb1)] + ([pbA_edge[0]] if pbA_edge[0] else []), inc="pe")
                e1 = E("vector", lambda v: v.tensor_copy(bc[0][:], pbA[:]),
                       waits=[("pe", b1v)], inc="ve")
                pbA_edge[0] = ("ve", e1)
                cb2 = E("vector", lambda v: v.tensor_copy(
                    stbf[0:1, :], pool[4][0:1, 0:TC]),
                    waits=[("se", s2), ("ve", e1)], inc="ve")
                b2v = E("tensor", lambda t: t.matmul(
                    pbB[:], c_onesrow[0:1, :], stbf[0:1, :], start=True,
                    stop=True), waits=[("ve", cb2)] + ([pbB_edge[0]] if pbB_edge[0] else []), inc="pe")
                e2 = E("vector", lambda v: v.tensor_copy(bc[1][:], pbB[:]),
                       waits=[("pe", b2v)], inc="ve")
                pbB_edge[0] = ("ve", e2)
                out["mu"] = bc[0]
                out["r"] = bc[1]
                out["mu_done"] = ("ve", e1)
                out["r_done"] = ("ve", e2)
                if kind == "ln1":
                    w1v = E("vector", lambda v: v.tensor_scalar(
                        strow[0:1, 0:TC], strow[0:1, TC:2 * TC], 1e-5, None,
                        ALU.add), waits=[("ve", cb1)], inc="ve")
                    w2r = E("vector", lambda v: v.reciprocal(
                        strow[0:1, 0:TC], strow[0:1, 0:TC]),
                        waits=[("ve", w1v)], inc="ve")
                    w2v = E("vector", lambda v: v.tensor_tensor(
                        strow[0:1, 0:TC], strow[0:1, TC:2 * TC],
                        strow[0:1, 0:TC], ALU.mult),
                        waits=[("ve", w2r)], inc="ve")
                    s3 = E("scalar", lambda s: s.activation(
                        strow[0:1, 0:TC], strow[0:1, 0:TC], AF.Ln, bias=c_misc[0:1, 1:2]),
                        waits=[("ve", w2v)], inc="se")
                    s4 = E("scalar", lambda s: s.activation(
                        strow[0:1, 0:TC], strow[0:1, 0:TC], AF.Exp, scale=-0.5),
                        waits=[("se", s3)], inc="se")
                    w3v = E("vector", lambda v: v.tensor_tensor(
                        strow[0:1, 0:TC], strow[0:1, 0:TC],
                        pool[4][0:1, 0:TC], ALU.mult),
                        waits=[("se", s4), ("se", s2)], inc="ve")
                    cb3 = E("vector", lambda v: v.tensor_copy(
                        stbf[0:1, :], strow[0:1, 0:TC]),
                        waits=[("ve", w3v)], inc="ve")
                    b3v = E("tensor", lambda t: t.matmul(
                        pbA[:], c_onesrow[0:1, :], stbf[0:1, :], start=True,
                        stop=True), waits=[("ve", cb3)] + ([pbA_edge[0]] if pbA_edge[0] else []), inc="pe")
                    e3 = E("vector", lambda v: v.tensor_copy(bc[2][:], pbA[:]),
                           waits=[("pe", b3v)], inc="ve")
                    pbA_edge[0] = ("ve", e3)
                    out["rho"] = bc[2]
                    out["rho_done"] = ("ve", e3)
            else:
                v2 = E("vector", lambda v: v.tensor_scalar(
                    strow[0:1, 0:TC], ps7[0:1, :], 1.0 / D, None, ALU.mult),
                    waits=[("pe", sq_v)], inc="ve")
                s1 = E("scalar", lambda s: s.activation(
                    strow[0:1, 0:TC], strow[0:1, 0:TC], AF.Ln, bias=c_misc[0:1, 1:2]),
                    waits=[("ve", v2)], inc="se")
                s2_ = E("scalar", lambda s: s.activation(
                    strow[0:1, 0:TC], strow[0:1, 0:TC], AF.Exp, scale=-0.5),
                    waits=[("se", s1)], inc="se")
                cb = E("vector", lambda v: v.tensor_copy(
                    stbf[0:1, :], strow[0:1, 0:TC]), waits=[("se", s2_)],
                    inc="ve")
                bv = E("tensor", lambda t: t.matmul(
                    pbA[:], c_onesrow[0:1, :], stbf[0:1, :], start=True,
                    stop=True), waits=[("ve", cb)] + ([pbA_edge[0]] if pbA_edge[0] else []), inc="pe")
                ev = E("vector", lambda v: v.tensor_copy(bc[2][:], pbA[:]),
                       waits=[("pe", bv)], inc="ve")
                pbA_edge[0] = ("ve", ev)
                out["rho"] = bc[2]
                out["rho_done"] = ("ve", ev)
            return out

        stats_edge = [None]
        router_edge = [None]
        buf_edges = {}

        for c in range(NCH):
            if barrier:
                for eng in ("tensor", "vector", "scalar", "gpsimd"):
                    E(eng, lambda x: x.nop(), waits=barrier)
            cbar = []

            tap("x0", c, x0b[c % 2][:], [128, MD * TC], F32,
                x0_ready[c])
            rows = layer_norm_rows(x0b[c % 2], MD, c, "ln1")
            h1_v = a_v = None
            for kt in range(KD):
                xs = x0b[c % 2][:, kt * TC:(kt + 1) * TC]
                t1s = t1b[:, kt * TC:(kt + 1) * TC]
                h1s = h1b[:, kt * TC:(kt + 1) * TC]
                w_ = [rows["mu_done"], x0_ready[c]] + cbar
                if "t1b" in buf_edges:
                    w_.append(buf_edges["t1b"])
                E("vector", lambda v, a=t1s, b=xs: v.tensor_tensor(
                    a, b, bc[0][:], ALU.subtract), waits=w_, inc="ve")
                w2_ = [rows["rho_done"]]
                if "h1b" in buf_edges:
                    w2_.append(buf_edges["h1b"])
                h1_v = E("vector", lambda v, o=h1s, a=t1s, kt=kt: v.scalar_tensor_tensor(
                    o, a, c_norm1[:, kt:kt + 1], bc[2][:], ALU.mult, ALU.mult),
                    waits=w2_, inc="ve")
                a_v = E("vector", lambda v, a=t1s: v.tensor_tensor(
                    a, a, bc[1][:], ALU.mult), waits=[rows["r_done"]], inc="ve")
            tap("a", c, t1b[:], [128, MD * TC], F32, ("ve", a_v))
            tap("h1", c, h1b[:], [128, MD * TC], BF16, ("ve", h1_v))

            conv_done = [None] * MI
            z_done = [None] * MI
            for slab in range(8):
                src = ipw_bf.rearrange("(kt p) m -> p kt m", p=128)[
                    :, :, slab * 512:(slab + 1) * 512]
                slot, wl_v = ring_load("ipw", src)
                last_pe = None
                for mt_in in range(4):
                    m = slab * 4 + mt_in
                    pi, pmt, pmw = next_pm()
                    for kt in range(KD):
                        st = (kt == 0)
                        sp = (kt == KD - 1)
                        lhs = wring[slot][:, kt * 512 + mt_in * 128:
                                          kt * 512 + mt_in * 128 + 128]
                        rhs = h1b[:, kt * TC:(kt + 1) * TC]
                        w_ = [wl_v, ("ve", h1_v)] + list(pmw) + cbar
                        last_pe = E("tensor", lambda t, o=pmt, l=lhs, r=rhs,
                                    st=st, sp=sp: t.matmul(o[:], l, r, start=st,
                                                           stop=sp),
                                    waits=w_, inc="pe" if sp else None)
                    if m < MI:
                        mi = m
                        _, acc, accw = next_pool()
                        w_ = [("pe", last_pe), ("glc", consts_gl)] + list(accw)
                        if c == 0:
                            w_.append(("ve", vz2))
                        E("vector", lambda v, a=acc, p=pmt, mi=mi: v.tensor_scalar(
                            a[:], p[:], c_convw[:, mi * 4 + 3:mi * 4 + 4], None,
                            ALU.mult), waits=w_, inc="ve")
                        for k in (2, 1, 0):
                            n = 3 - k
                            E("vector", lambda v, a=acc, p=pmt, mi=mi, k=k, n=n:
                              v.scalar_tensor_tensor(
                                  a[:, n:TC], p[:, 0:TC - n],
                                  c_convw[:, mi * 4 + k:mi * 4 + k + 1],
                                  a[:, n:TC], ALU.mult, ALU.add), inc="ve")
                            E("vector", lambda v, a=acc, mi=mi, k=k, n=n:
                              v.scalar_tensor_tensor(
                                  a[:, 0:n], tailb[:, mi * 3 + k:mi * 3 + 3],
                                  c_convw[:, mi * 4 + k:mi * 4 + k + 1],
                                  a[:, 0:n], ALU.mult, ALU.add), inc="ve")
                        tl_v = E("vector", lambda v, p=pmt, mi=mi: v.tensor_copy(
                            tailb[:, mi * 3:mi * 3 + 3], p[:, TC - 3:TC]),
                            inc="ve")
                        w3_ = []
                        if "xcb" in buf_edges:
                            w3_.append(buf_edges["xcb"])
                        cd = E("scalar", lambda s, a=acc, mi=mi: s.activation(
                            xcb[:, mi * TC:(mi + 1) * TC], a[:], AF.Silu,
                            bias=c_convb[:, mi:mi + 1]),
                            waits=[("ve", pb.now("ve"))] + w3_, inc="se")
                        conv_done[mi] = cd
                        pm_edge[pi] = ("se", cd)
                        pool_edge[pool.index(acc)] = ("se", cd)
                    else:
                        mi = m - MI
                        w_ = [("pe", last_pe)]
                        if "zb" in buf_edges:
                            w_.append(buf_edges["zb"])
                        zd = E("scalar", lambda s, p=pmt, mi=mi: s.activation(
                            zb[:, mi * TC:(mi + 1) * TC], p[:], AF.Copy),
                            waits=w_, inc="se")
                        z_done[mi] = zd
                        pm_edge[pi] = ("se", zd)
                ring_release(slot, ("pe", last_pe))
            buf_edges["h1b"] = ("pe", pb.now("pe"))
            xc_all = ("se", max(conv_done))
            z_all = ("se", max(z_done))
            tap("xc", c, xcb[:], [128, MI * TC], BF16, xc_all)
            tap("z", c, zb[:], [128, MI * TC], BF16, z_all)

            def xproj_part(col0, mcols, out_emit):
                pi, pmt, pmw = next_pm()
                last = None
                for kt in range(KI):
                    st = (kt == 0)
                    sp = (kt == KI - 1)
                    lhs = r_xpw[:, kt * 96 + col0:kt * 96 + col0 + mcols]
                    rhs = xcb[:, kt * TC:(kt + 1) * TC]
                    w_ = [("se", conv_done[kt]), ("glc", consts_gl)] + list(pmw)
                    last = E("tensor", lambda t, o=pmt, l=lhs, r=rhs, st=st,
                             sp=sp, mc=mcols: t.matmul(o[0:mc, :], l, r,
                                                       start=st, stop=sp),
                             waits=w_, inc="pe" if sp else None)
                ev = out_emit(pmt, last)
                pm_edge[pi] = ev
                return ev

            def dtr_evict(pmt, last):
                w_ = [("pe", last)]
                if "dtrb" in buf_edges:
                    w_.append(buf_edges["dtrb"])
                v = E("scalar", lambda s, p=pmt: s.activation(
                    dtrb[:], p[0:DTR, :], AF.Copy), waits=w_, inc="se")
                return ("se", v)

            def b_evict(pmt, last):
                w_ = [("pe", last)]
                if "Bsb" in buf_edges:
                    w_.append(buf_edges["Bsb"])
                v = E("scalar", lambda s, p=pmt: s.activation(
                    Bsb[:], p[0:NS, :], AF.Copy), waits=w_, inc="se")
                return ("se", v)

            def c_evict(pmt, last):
                w_ = [("pe", last)]
                if "Csb" in buf_edges:
                    w_.append(buf_edges["Csb"])
                v = E("scalar", lambda s, p=pmt: s.activation(
                    Csb[:], p[0:NS, :], AF.Copy), waits=w_, inc="se")
                return ("se", v)

            dtr_done = xproj_part(0, DTR, dtr_evict)
            B_done = xproj_part(DTR, NS, b_evict)
            C_done = xproj_part(DTR + NS, NS, c_evict)
            buf_edges["dtrb"] = None
            tap("B", c, Bsb[:], [NS, TC], BF16, B_done)
            tap("C", c, Csb[:], [NS, TC], BF16, C_done)

            yz_done = [None] * MI
            dt_edge = [None, None]
            dA_edge = [None, None]
            for mi in range(MI):
                par = 0
                dts = dtb2[:, 0:TC]
                pi, pmt, pmw = next_pm()
                last = E("tensor", lambda t, o=pmt, mi=mi: t.matmul(
                    o[:], r_dtw[:, mi * 128:(mi + 1) * 128], dtrb[:],
                    start=True, stop=True),
                    waits=[dtr_done, ("glc", consts_gl)] + list(pmw), inc="pe")
                _, p_av, w_av = next_pool()
                _, p_m, w_m = next_pool()
                E("vector", lambda v, a=p_av, p=pmt, mi=mi: v.tensor_scalar(
                    a[:], p[:], c_dtb[:, mi:mi + 1], 0.0, ALU.add, ALU.min),
                    waits=[("pe", last)] + list(w_av) + list(w_m), inc="ve")
                E("vector", lambda v, a=p_m, p=pmt, mi=mi: v.tensor_scalar(
                    a[:], p[:], c_dtb[:, mi:mi + 1], 0.0, ALU.add, ALU.max),
                    inc="ve")
                E("vector", lambda v, a=p_av, b=p_m: v.tensor_tensor(
                    a[:], a[:], b[:], ALU.subtract), inc="ve")
                pm_edge[pi] = ("ve", pb.now("ve"))
                s_e = E("scalar", lambda s, a=p_av: s.activation(
                    a[:], a[:], AF.Exp),
                    waits=[("ve", pb.now("ve"))], inc="se")
                s_l = E("scalar", lambda s, a=p_av: s.activation(
                    a[:], a[:], AF.Ln, bias=c_misc[:, 2:3]), waits=[("se", s_e)], inc="se")
                w_dt = [("se", s_l)]
                if dt_edge[par]:
                    w_dt.append(dt_edge[par])
                dt_v = E("vector", lambda v, o=dts, a=p_m, b=p_av: v.tensor_tensor(
                    o, a[:], b[:], ALU.add), waits=w_dt, inc="ve")
                pool_edge[pool.index(p_av)] = ("ve", dt_v)
                pool_edge[pool.index(p_m)] = ("ve", dt_v)
                if mi == 0:
                    tap("dt0", c, dts, [128, TC], F32, ("ve", dt_v))
                w_v = E("vector", lambda v, o=wb, a=dts, mi=mi: v.tensor_tensor(
                    o[:], a, xcb[:, mi * TC:(mi + 1) * TC], ALU.mult),
                    waits=[("se", conv_done[mi])], inc="ve")
                y_v = None
                last_dA = None
                for s_i in range(NS):
                    spar = s_i % 2
                    dAs = dAb[:, spar * TC:(spar + 1) * TC]
                    wA = (pbA_edge[0],) if pbA_edge[0] else ()
                    wB = (pbB_edge[0],) if pbB_edge[0] else ()
                    bA = E("tensor", lambda t, s_i=s_i: t.matmul(
                        pbA[:], c_sel16[:, s_i * 128:(s_i + 1) * 128], Bsb[:],
                        start=True, stop=True), waits=[B_done] + list(wA),
                        inc="pe")
                    bB = E("tensor", lambda t, s_i=s_i: t.matmul(
                        pbB[:], c_sel16[:, s_i * 128:(s_i + 1) * 128], Csb[:],
                        start=True, stop=True), waits=[C_done] + list(wB),
                        inc="pe")
                    w_da = [("ve", dt_v), ("ve", vA2)]
                    if dA_edge[spar]:
                        w_da.append(dA_edge[spar])
                    dA_v = E("scalar", lambda s, o=dAs, i=dts, mi=mi, s_i=s_i:
                             s.activation(o, i, AF.Exp,
                                          scale=c_Alog[:, mi * NS + s_i:
                                                    mi * NS + s_i + 1]),
                             waits=w_da, inc="se")
                    last_dA = dA_v
                    E("vector", lambda v: v.tensor_tensor(
                        ub[:], wb[:], pbA[:], ALU.mult),
                        waits=[("pe", bA), ("ve", w_v)], inc="ve")
                    idx = mi * NS + s_i
                    sc_v = E("vector", lambda v, o=hb, a=dAs, idx=idx:
                             v.tensor_tensor_scan(
                                 o[:], a, ub[:], carry[:, idx:idx + 1],
                                 ALU.mult, ALU.add),
                             waits=[("se", dA_v)], inc="ve")
                    dA_edge[spar] = ("ve", sc_v)
                    E("vector", lambda v, idx=idx: v.tensor_copy(
                        carry[:, idx:idx + 1], hb[:, TC - 1:TC]), inc="ve")
                    if s_i == 0:
                        y_v = E("vector", lambda v: v.tensor_tensor(
                            yb[:], hb[:], pbB[:], ALU.mult),
                            waits=[("pe", bB)], inc="ve")
                    else:
                        E("vector", lambda v: v.tensor_tensor(
                            hb[:], hb[:], pbB[:], ALU.mult),
                            waits=[("pe", bB)], inc="ve")
                        y_v = E("vector", lambda v: v.tensor_tensor(
                            yb[:], yb[:], hb[:], ALU.add), inc="ve")
                    pbA_edge[0] = ("ve", y_v)
                    pbB_edge[0] = ("ve", y_v)
                dt_edge[par] = ("se", last_dA)
                if mi == 0:
                    tap("y0", c, yb[:], [128, TC], F32, ("ve", y_v))
                _, p_sz, w_sz = next_pool()
                sz_v = E("scalar", lambda s, a=p_sz, mi=mi: s.activation(
                    a[:], zb[:, mi * TC:(mi + 1) * TC], AF.Silu),
                    waits=[("se", z_done[mi])] + list(w_sz), inc="se")
                _, p_t, w_t = next_pool()
                E("vector", lambda v, a=p_t, mi=mi: v.scalar_tensor_tensor(
                    a[:], xcb[:, mi * TC:(mi + 1) * TC], c_Dp[:, mi:mi + 1],
                    yb[:], ALU.mult, ALU.add), waits=list(w_t), inc="ve")
                yzv = E("vector", lambda v, a=p_t, b=p_sz, mi=mi: v.tensor_tensor(
                    xcb[:, mi * TC:(mi + 1) * TC], a[:], b[:], ALU.mult),
                    waits=[("se", sz_v)], inc="ve")
                yz_done[mi] = yzv
                pool_edge[pool.index(p_sz)] = ("ve", yzv)
                pool_edge[pool.index(p_t)] = ("ve", yzv)
            buf_edges["Bsb"] = ("ve", pb.now("ve"))
            buf_edges["Csb"] = buf_edges["Bsb"]
            buf_edges["dtrb"] = ("pe", pb.now("pe"))
            tap("yz", c, xcb[:], [128, MI * TC], BF16, ("ve", max(yz_done)))

            xa_done = [None] * MD
            for slab in range(4):
                src = opw_bf.rearrange("(kt p) m -> p kt m", p=128)[
                    :, :, slab * 256:(slab + 1) * 256]
                slot, wl_v = ring_load("opw", src)
                last_pe = None
                for mt_in in range(2):
                    m = slab * 2 + mt_in
                    pi, pmt, pmw = next_pm()
                    for kt in range(KI):
                        st = (kt == 0)
                        sp = (kt == KI - 1)
                        lhs = wring[slot][:, kt * 256 + mt_in * 128:
                                          kt * 256 + mt_in * 128 + 128]
                        rhs = xcb[:, kt * TC:(kt + 1) * TC]
                        w_ = [wl_v, ("ve", yz_done[kt])] + list(pmw)
                        last_pe = E("tensor", lambda t, o=pmt, l=lhs, r=rhs,
                                    st=st, sp=sp: t.matmul(o[:], l, r,
                                                           start=st, stop=sp),
                                    waits=w_, inc="pe" if sp else None)
                    xv = E("vector", lambda v, p=pmt, m=m: v.tensor_tensor(
                        t1b[:, m * TC:(m + 1) * TC],
                        t1b[:, m * TC:(m + 1) * TC], p[:], ALU.add),
                        waits=[("pe", last_pe)], inc="ve")
                    xa_done[m] = xv
                    pm_edge[pi] = ("ve", xv)
                ring_release(slot, ("pe", last_pe))
            tap("xa", c, t1b[:], [128, MD * TC], F32, ("ve", max(xa_done)))

            rows2 = layer_norm_rows(t1b, MD, c, "rms",
                                    src_waits=[("ve", v) for v in xa_done])
            ht_v = None
            for kt in range(KD):
                w_ = [rows2["rho_done"], ("ve", xa_done[kt])]
                if "htb" in buf_edges:
                    w_.append(buf_edges["htb"])
                ht_v = E("vector", lambda v, kt=kt: v.scalar_tensor_tensor(
                    htb[:, kt * TC:(kt + 1) * TC],
                    t1b[:, kt * TC:(kt + 1) * TC], c_norm2[:, kt:kt + 1],
                    bc[2][:], ALU.mult, ALU.mult), waits=w_, inc="ve")
            tap("ht", c, htb[:], [128, MD * TC], BF16, ("ve", ht_v))

            def router_gates(rhs_buf, rhs_done, wtile, out_et, eps, tapname):
                pi, pmt, pmw = next_pm()
                last = None
                for kt in range(KD):
                    st = (kt == 0)
                    sp = (kt == KD - 1)
                    lhs = wtile[:, kt * NE:(kt + 1) * NE]
                    rhs = rhs_buf[:, kt * TC:(kt + 1) * TC]
                    last = E("tensor", lambda t, o=pmt, l=lhs, r=rhs, st=st,
                             sp=sp: t.matmul(o[0:NE, :], l, r, start=st,
                                             stop=sp),
                             waits=[rhs_done, ("glc", consts_gl)] + list(pmw),
                             inc="pe" if sp else None)
                w_rl = [("pe", last)]
                if router_edge[0]:
                    w_rl.append(router_edge[0])
                rl_v = E("scalar", lambda s, p=pmt: s.activation(
                    RLb[:], p[0:NE, :], AF.Copy), waits=w_rl, inc="se")
                pm_edge[pi] = ("se", rl_v)
                wA = (pbA_edge[0],) if pbA_edge[0] else ()
                tp = None
                for ti in range(4):
                    tp = E("tensor", lambda t, ti=ti: t.transpose(
                        pbA[:, ti * 8:(ti + 1) * 8],
                        RLb[:, ti * 128:(ti + 1) * 128], c_ident[0:8, 0:8]),
                        waits=[("se", rl_v)] + list(wA), inc="pe")
                gt_v = E("vector", lambda v: v.tensor_copy(
                    Gtb[:], pbA[:, 0:32]), waits=[("pe", tp)], inc="ve")
                pbA_edge[0] = ("ve", gt_v)
                tap("RL" + tapname, c, RLb[:], [NE, TC], F32, ("se", rl_v))
                tap("GtA" + tapname, c, Gtb[:], [128, 32], F32, ("ve", gt_v))
                gv = gt_v
                for ti in range(4):
                    g8 = Gtb[:, ti * 8:(ti + 1) * 8]
                    m1 = pool[4][:, 0:1]
                    sm = pool[4][:, 8:16]
                    def VS(build, extra=(), _pb=pb):
                        nonlocal gv
                        gv = E("vector", build,
                               waits=[("ve", gv)] + list(extra), inc="ve")
                        return gv
                    VS(lambda v, g8=g8, m1=m1: v.tensor_reduce(
                        m1, g8, AX.X, ALU.max))
                    VS(lambda v, g8=g8, m1=m1, sm=sm: v.tensor_scalar(
                        sm, g8, m1, None, ALU.subtract))
                    se_v = E("scalar", lambda s, sm=sm: s.activation(
                        sm, sm, AF.Exp), waits=[("ve", gv)], inc="se")
                    VS(lambda v, sm=sm, m1=m1: v.tensor_reduce(
                        m1, sm, AX.X, ALU.add), extra=[("se", se_v)])
                    VS(lambda v, m1=m1: v.reciprocal(m1, m1))
                    VS(lambda v, g8=g8, sm=sm, m1=m1: v.tensor_scalar(
                        g8, sm, m1, None, ALU.mult))
                    VS(lambda v, g8=g8, m1=m1: v.tensor_reduce(
                        m1, g8, AX.X, ALU.max))
                    VS(lambda v, g8=g8, m1=m1, sm=sm: v.tensor_scalar(
                        sm, g8, m1, None, ALU.is_ge))
                    p2 = pool[4][:, 16:24]
                    VS(lambda v, g8=g8, sm=sm, p2=p2: v.tensor_tensor(
                        p2, g8, sm, ALU.mult))
                    VS(lambda v, g8=g8, p2=p2: v.tensor_tensor(
                        p2, g8, p2, ALU.subtract))
                    m2 = pool[4][:, 1:2]
                    VS(lambda v, p2=p2, m2=m2: v.tensor_reduce(
                        m2, p2, AX.X, ALU.max))
                    VS(lambda v, p2=p2, m2=m2: v.tensor_scalar(
                        p2, p2, m2, None, ALU.is_ge))
                    VS(lambda v, sm=sm, p2=p2: v.tensor_tensor(
                        sm, sm, p2, ALU.add))
                    VS(lambda v, m1=m1, m2=m2: v.tensor_tensor(
                        m1, m1, m2, ALU.add))
                    if eps:
                        VS(lambda v, m1=m1: v.tensor_scalar(
                            m1, m1, eps, None, ALU.add))
                    VS(lambda v, g8=g8, sm=sm: v.tensor_tensor(
                        g8, g8, sm, ALU.mult))
                    VS(lambda v, m1=m1: v.reciprocal(m1, m1))
                    VS(lambda v, g8=g8, m1=m1: v.tensor_scalar(
                        g8, g8, m1, None, ALU.mult))
                tap("GtB" + tapname, c, Gtb[:], [128, 32], F32, ("ve", gv))
                wB = (pbB_edge[0],) if pbB_edge[0] else ()
                tp2 = None
                for ti in range(4):
                    tp2 = E("tensor", lambda t, ti=ti: t.transpose(
                        pbB[0:8, ti * 128:(ti + 1) * 128],
                        Gtb[:, ti * 8:(ti + 1) * 8], c_ident[:]),
                        waits=[("ve", gv)] + list(wB), inc="pe")
                et_v = E("vector", lambda v, o=out_et: v.tensor_copy(
                    o[:], pbB[0:8, :]), waits=[("pe", tp2)], inc="ve")
                pbB_edge[0] = ("ve", et_v)
                router_edge[0] = ("pe", tp2)
                tap(tapname, c, out_et[:], [NE, TC], BF16, ("ve", et_v))
                return ("ve", et_v)

            w_get = []
            if "Get" in buf_edges:
                w_get.append(buf_edges["Get"])
            get_done = router_gates(htb, ("ve", ht_v), r_rtw, Get, 0.0, "gates")

            acc_done = [None] * MD
            hid_edge = buf_edges.get("hidb")
            for e in range(NE):
                wA = (pbA_edge[0],) if pbA_edge[0] else ()
                gb_v = E("tensor", lambda t, e=e: t.matmul(
                    pbA[:], c_sel16[0:8, e * 128:(e + 1) * 128], Get[:],
                    start=True, stop=True), waits=[get_done] + list(wA),
                    inc="pe")
                w_g = [("pe", gb_v)]
                if "Gsb" in buf_edges:
                    w_g.append(buf_edges["Gsb"])
                gs_v = E("vector", lambda v: v.tensor_copy(Gsb[:], pbA[:]),
                         waits=w_g, inc="ve")
                pbA_edge[0] = ("ve", gs_v)
                hid_done = [None] * KI
                for slab in range(4):
                    src = w1_bf.rearrange("e (kt p) m -> e p kt m", p=128)[
                        e][:, :, slab * 512:(slab + 1) * 512]
                    slot, wl_v = ring_load(("w1", e), src)
                    last_pe = None
                    for mt_in in range(4):
                        hi = slab * 4 + mt_in
                        pi, pmt, pmw = next_pm()
                        for kt in range(KD):
                            st = (kt == 0)
                            sp = (kt == KD - 1)
                            lhs = wring[slot][:, kt * 512 + mt_in * 128:
                                              kt * 512 + mt_in * 128 + 128]
                            rhs = htb[:, kt * TC:(kt + 1) * TC]
                            w_ = [wl_v, ("ve", ht_v)] + list(pmw)
                            last_pe = E("tensor", lambda t, o=pmt, l=lhs,
                                        r=rhs, st=st, sp=sp: t.matmul(
                                            o[:], l, r, start=st, stop=sp),
                                        waits=w_, inc="pe" if sp else None)
                        w_ = [("pe", last_pe)]
                        if hid_edge:
                            w_.append(hid_edge)
                        hv = E("scalar", lambda s, p=pmt, hi=hi, e=e:
                               s.activation(hidb[:, hi * TC:(hi + 1) * TC],
                                            p[:], AF.Gelu_apprx_tanh,
                                            bias=c_b1[:, e * MI + hi:
                                                      e * MI + hi + 1]),
                               waits=w_, inc="se")
                        hid_done[hi] = hv
                        pm_edge[pi] = ("se", hv)
                    ring_release(slot, ("pe", last_pe))
                if e == 0:
                    tap("hid0", c, hidb[:], [128, KI * TC], BF16,
                        ("se", max(hid_done)))
                    tap("gsb0", c, Gsb[:], [128, TC], F32, ("ve", gs_v))
                for slab in range(4):
                    src = w2_bf.rearrange("e (kt p) m -> e p kt m", p=128)[
                        e][:, :, slab * 256:(slab + 1) * 256]
                    slot, wl_v = ring_load(("w2", e), src)
                    last_pe = None
                    for mt_in in range(2):
                        m = slab * 2 + mt_in
                        pi, pmt, pmw = next_pm()
                        for kt in range(KI):
                            st = (kt == 0)
                            sp = (kt == KI - 1)
                            lhs = wring[slot][:, kt * 256 + mt_in * 128:
                                              kt * 256 + mt_in * 128 + 128]
                            rhs = hidb[:, kt * TC:(kt + 1) * TC]
                            w_ = [wl_v, ("se", hid_done[kt])] + list(pmw)
                            last_pe = E("tensor", lambda t, o=pmt, l=lhs,
                                        r=rhs, st=st, sp=sp: t.matmul(
                                            o[:], l, r, start=st, stop=sp),
                                        waits=w_, inc="pe" if sp else None)
                        if e == 0:
                            w_ = [("pe", last_pe), ("ve", gs_v)]
                            if "accb" in buf_edges:
                                w_.append(buf_edges["accb"])
                            av = E("vector", lambda v, p=pmt, m=m, e=e:
                                   v.scalar_tensor_tensor(
                                       accb[:, m * TC:(m + 1) * TC], p[:],
                                       c_b2[:, e * MD + m:e * MD + m + 1],
                                       Gsb[:], ALU.add, ALU.mult),
                                   waits=w_, inc="ve")
                            acc_done[m] = ("ve", av)
                            pm_edge[pi] = ("ve", av)
                        else:
                            _, p_t, w_t = next_pool()
                            sv = E("vector", lambda v, a=p_t, p=pmt, m=m, e=e:
                                   v.scalar_tensor_tensor(
                                       a[:], p[:],
                                       c_b2[:, e * MD + m:e * MD + m + 1],
                                       Gsb[:], ALU.add, ALU.mult),
                                   waits=[("pe", last_pe), ("ve", gs_v)]
                                   + list(w_t), inc="ve")
                            pm_edge[pi] = ("ve", sv)
                            gv2 = E("gpsimd", lambda g, a=p_t, m=m:
                                    g.tensor_tensor(
                                        accb[:, m * TC:(m + 1) * TC],
                                        accb[:, m * TC:(m + 1) * TC], a[:],
                                        ALU.add),
                                    waits=[("ve", sv), acc_done[m]] + cbar,
                                    inc="ge")
                            acc_done[m] = ("ge", gv2)
                            pool_edge[pool.index(p_t)] = ("ge", gv2)
                    ring_release(slot, ("pe", last_pe))
                if e == 0:
                    tap("acc0", c, accb[:], [128, MD * TC], F32, acc_done[-1])
                buf_edges["Gsb"] = ("ve", pb.now("ve"))
            buf_edges["htb"] = ("pe", pb.now("pe"))
            buf_edges["Get"] = acc_done[0]
            tap("moe", c, accb[:], [128, MD * TC], F32, ("ge", pb.now("ge")))

            x1_done = [None] * MD
            x0cur = x0b[c % 2]
            for m in range(MD):
                xsl = x0cur[:, m * TC:(m + 1) * TC]
                E("gpsimd", lambda g, a=xsl, b=t1b[:, m * TC:(m + 1) * TC]:
                  g.tensor_tensor(a, a, b, ALU.add),
                  waits=[("ve", xa_done[m]), acc_done[m], x0_ready[c]]
                  + cbar, inc="ge")
                xv = E("gpsimd", lambda g, a=xsl, b=accb[:, m * TC:(m + 1) * TC]:
                       g.tensor_tensor(a, a, b, ALU.add),
                       waits=[x0_ready[c]], inc="ge")
                x1_done[m] = ("ge", xv)
            buf_edges["t1b"] = x1_done[-1]
            tap("x1", c, x0b[c % 2][:], [128, MD * TC], F32, x1_done[-1])

            sq_v = mu_v = None
            ln2_sq_se = None
            for kt in range(MD):
                xs = x0b[c % 2][:, kt * TC:(kt + 1) * TC]
                w_ = [x1_done[kt]]
                pe_prev = stats_edge[0]
                if pe_prev:
                    w_.append(pe_prev)
                cv = E("vector", lambda v, a=xs: v.tensor_copy(
                    stsrc[:, 0:TC], a), waits=w_, inc="ve")
                sv = E("scalar", lambda s, a=xs: s.activation(
                    stsrc[:, TC:2 * TC], a, AF.Square), waits=w_, inc="se")
                ln2_sq_se = sv
                st = (kt == 0)
                sp = (kt == MD - 1)
                mu_v = E("tensor", lambda t, st=st, sp=sp: t.matmul(
                    ps6[0:1, :], c_onescol[:, 0:1], stsrc[:, 0:TC],
                    start=st, stop=sp), waits=[("ve", cv)], inc="pe")
                sq_v = E("tensor", lambda t, st=st, sp=sp: t.matmul(
                    ps7[0:1, :], c_onescol[:, 0:1], stsrc[:, TC:2 * TC],
                    start=st, stop=sp), waits=[("se", sv)], inc="pe")
                stats_edge[0] = ("pe", sq_v)
            v1 = E("vector", lambda v: v.tensor_scalar(
                strow[0:1, 0:TC], ps6[0:1, :], 1.0 / D, None, ALU.mult),
                waits=[("pe", mu_v)], inc="ve")
            v2 = E("vector", lambda v: v.tensor_scalar(
                strow[0:1, TC:2 * TC], ps7[0:1, :], 1.0 / D, None, ALU.mult),
                waits=[("pe", sq_v)], inc="ve")
            v3 = E("vector", lambda v: v.tensor_tensor(
                pool[4][0:1, 0:TC], strow[0:1, 0:TC], strow[0:1, 0:TC],
                ALU.mult), waits=[("ve", v1)], inc="ve")
            v4 = E("vector", lambda v: v.tensor_tensor(
                strow[0:1, TC:2 * TC], strow[0:1, TC:2 * TC],
                pool[4][0:1, 0:TC], ALU.subtract), waits=[("ve", v3)],
                inc="ve")
            s1 = E("scalar", lambda s: s.activation(
                pool[4][0:1, 0:TC], strow[0:1, TC:2 * TC], AF.Ln, bias=c_misc[0:1, 0:1]),
                waits=[("ve", v4)], inc="se")
            s2 = E("scalar", lambda s: s.activation(
                pool[4][0:1, 0:TC], pool[4][0:1, 0:TC], AF.Exp, scale=-0.5),
                waits=[("se", s1)], inc="se")
            cb1 = E("vector", lambda v: v.tensor_copy(
                stbf[0:1, :], strow[0:1, 0:TC]), waits=[("ve", v4)], inc="ve")
            wA = (pbA_edge[0],) if pbA_edge[0] else ()
            b1v = E("tensor", lambda t: t.matmul(
                pbA[:], c_onesrow[0:1, :], stbf[0:1, :], start=True,
                stop=True), waits=[("ve", cb1)] + list(wA), inc="pe")
            e1 = E("vector", lambda v: v.tensor_copy(bc[0][:], pbA[:]),
                   waits=[("pe", b1v)], inc="ve")
            pbA_edge[0] = ("ve", e1)
            cb2 = E("vector", lambda v: v.tensor_copy(
                stbf[0:1, :], pool[4][0:1, 0:TC]),
                waits=[("se", s2), ("ve", e1)], inc="ve")
            wB = (pbB_edge[0],) if pbB_edge[0] else ()
            b2v = E("tensor", lambda t: t.matmul(
                pbB[:], c_onesrow[0:1, :], stbf[0:1, :], start=True,
                stop=True), waits=[("ve", cb2)] + list(wB), inc="pe")
            e2 = E("vector", lambda v: v.tensor_copy(bc[1][:], pbB[:]),
                   waits=[("pe", b2v)], inc="ve")
            pbB_edge[0] = ("ve", e2)
            b2_v = None
            for kt in range(MD):
                _, p_t, w_t = next_pool()
                E("vector", lambda v, a=p_t,
                  b=x0b[c % 2][:, kt * TC:(kt + 1) * TC]: v.tensor_tensor(
                    a[:], b, bc[0][:],
                    ALU.subtract), waits=[("ve", e1), x1_done[kt]] + list(w_t),
                    inc="ve")
                b2_v = E("vector", lambda v, a=p_t, kt=kt: v.tensor_tensor(
                    htb[:, kt * TC:(kt + 1) * TC], a[:], bc[1][:], ALU.mult),
                    waits=[("ve", e2)], inc="ve")
                pool_edge[pool.index(p_t)] = ("ve", b2_v)
            tap("b2nd", c, htb[:], [128, MD * TC], BF16, ("ve", b2_v))

            pet_done = router_gates(htb, ("ve", b2_v), r_frw, Pet, 1e-9,
                                    "pgates")

            facc_done = [None] * MD
            h2_edge = ("pe", pb.now("pe"))
            for e in range(NE):
                wA = (pbA_edge[0],) if pbA_edge[0] else ()
                pb_v = E("tensor", lambda t, e=e: t.matmul(
                    pbA[:], c_sel16[0:8, e * 128:(e + 1) * 128], Pet[:],
                    start=True, stop=True), waits=[pet_done] + list(wA),
                    inc="pe")
                w_g = [("pe", pb_v), buf_edges.get("Gsb")]
                gs_v = E("vector", lambda v: v.tensor_copy(Gsb[:], pbA[:]),
                         waits=[x for x in w_g if x], inc="ve")
                pbA_edge[0] = ("ve", gs_v)
                h2_done = [None] * KI
                for slab in range(4):
                    gsrc = gw_bf.rearrange("e (kt p) m -> e p kt m", p=128)[
                        e][:, :, slab * 512:(slab + 1) * 512]
                    gslot, gwl = ring_load(("gw", e), gsrc)
                    usrc = uw_bf.rearrange("e (kt p) m -> e p kt m", p=128)[
                        e][:, :, slab * 512:(slab + 1) * 512]
                    uslot, uwl = ring_load(("uw", e), usrc)
                    g_last = u_last = None
                    for mt_in in range(4):
                        fi = slab * 4 + mt_in
                        pgi, pgt, pgw = next_pm()
                        for kt in range(KD):
                            st = (kt == 0)
                            sp = (kt == KD - 1)
                            lhs = wring[gslot][:, kt * 512 + mt_in * 128:
                                               kt * 512 + mt_in * 128 + 128]
                            rhs = htb[:, kt * TC:(kt + 1) * TC]
                            w_ = [gwl, ("ve", b2_v)] + list(pgw)
                            g_last = E("tensor", lambda t, o=pgt, l=lhs, r=rhs,
                                       st=st, sp=sp: t.matmul(o[:], l, r,
                                                              start=st,
                                                              stop=sp),
                                       waits=w_, inc="pe" if sp else None)
                        pui, put, puw = next_pm()
                        for kt in range(KD):
                            st = (kt == 0)
                            sp = (kt == KD - 1)
                            lhs = wring[uslot][:, kt * 512 + mt_in * 128:
                                               kt * 512 + mt_in * 128 + 128]
                            rhs = htb[:, kt * TC:(kt + 1) * TC]
                            w_ = [uwl, ("ve", b2_v)] + list(puw)
                            u_last = E("tensor", lambda t, o=put, l=lhs, r=rhs,
                                       st=st, sp=sp: t.matmul(o[:], l, r,
                                                              start=st,
                                                              stop=sp),
                                       waits=w_, inc="pe" if sp else None)
                        _, p_sg, w_sg = next_pool()
                        sg_v = E("scalar", lambda s, a=p_sg, p=pgt, fi=fi, e=e:
                                 s.activation(a[:], p[:], AF.Silu,
                                              bias=c_gb[:, e * MI + fi:
                                                        e * MI + fi + 1]),
                                 waits=[("pe", g_last)] + list(w_sg), inc="se")
                        pm_edge[pgi] = ("se", sg_v)
                        w_ = [("pe", u_last), ("se", sg_v)]
                        if h2_edge:
                            w_.append(h2_edge)
                        hv = E("vector", lambda v, a=p_sg, p=put, fi=fi, e=e:
                               v.scalar_tensor_tensor(
                                   hidb[:, fi * TC:(fi + 1) * TC], p[:],
                                   c_ub[:, e * MI + fi:e * MI + fi + 1],
                                   a[:], ALU.add, ALU.mult), waits=w_,
                               inc="ve")
                        h2_done[fi] = hv
                        pm_edge[pui] = ("ve", hv)
                        pool_edge[pool.index(p_sg)] = ("ve", hv)
                    ring_release(gslot, ("pe", g_last))
                    ring_release(uslot, ("pe", u_last))
                for slab in range(4):
                    src = dw_bf.rearrange("e (kt p) m -> e p kt m", p=128)[
                        e][:, :, slab * 256:(slab + 1) * 256]
                    slot, wl_v = ring_load(("dw", e), src)
                    last_pe = None
                    for mt_in in range(2):
                        m = slab * 2 + mt_in
                        pi, pmt, pmw = next_pm()
                        for kt in range(KI):
                            st = (kt == 0)
                            sp = (kt == KI - 1)
                            lhs = wring[slot][:, kt * 256 + mt_in * 128:
                                              kt * 256 + mt_in * 128 + 128]
                            rhs = hidb[:, kt * TC:(kt + 1) * TC]
                            w_ = [wl_v, ("ve", h2_done[kt])] + list(pmw)
                            last_pe = E("tensor", lambda t, o=pmt, l=lhs,
                                        r=rhs, st=st, sp=sp: t.matmul(
                                            o[:], l, r, start=st, stop=sp),
                                        waits=w_, inc="pe" if sp else None)
                        if e == 0:
                            w_ = [("pe", last_pe), ("ve", gs_v), x1_done[m]]
                            av = E("vector", lambda v, p=pmt, m=m, e=e:
                                   v.scalar_tensor_tensor(
                                       accb[:, m * TC:(m + 1) * TC], p[:],
                                       c_db[:, e * MD + m:e * MD + m + 1],
                                       Gsb[:], ALU.add, ALU.mult),
                                   waits=w_, inc="ve")
                            facc_done[m] = ("ve", av)
                            pm_edge[pi] = ("ve", av)
                        else:
                            _, p_t, w_t = next_pool()
                            sv = E("vector", lambda v, a=p_t, p=pmt, m=m, e=e:
                                   v.scalar_tensor_tensor(
                                       a[:], p[:],
                                       c_db[:, e * MD + m:e * MD + m + 1],
                                       Gsb[:], ALU.add, ALU.mult),
                                   waits=[("pe", last_pe), ("ve", gs_v)]
                                   + list(w_t), inc="ve")
                            pm_edge[pi] = ("ve", sv)
                            gv2 = E("gpsimd", lambda g, a=p_t, m=m:
                                    g.tensor_tensor(
                                        accb[:, m * TC:(m + 1) * TC],
                                        accb[:, m * TC:(m + 1) * TC], a[:],
                                        ALU.add),
                                    waits=[("ve", sv), facc_done[m]],
                                    inc="ge")
                            facc_done[m] = ("ge", gv2)
                            pool_edge[pool.index(p_t)] = ("ge", gv2)
                    ring_release(slot, ("pe", last_pe))
                buf_edges["Gsb"] = ("ve", pb.now("ve"))
            buf_edges["accb"] = facc_done[-1]
            buf_edges["hidb"] = ("pe", pb.now("pe"))
            buf_edges["Pet"] = facc_done[0]

            for m in range(MD):
                E("gpsimd", lambda g, a=x0b[c % 2][:, m * TC:(m + 1) * TC],
                  b=accb[:, m * TC:(m + 1) * TC]:
                  g.tensor_tensor(a, a, b, ALU.add),
                  waits=[facc_done[m], ("ve", b2_v), ("se", ln2_sq_se),
                         x0_ready[c]],
                  inc="ge")
            out_done = pb.now("ge")
            dst = out.rearrange("(kt p) t -> p kt t", p=128)[
                :, :, c * TC:(c + 1) * TC]
            srcv = x0b[c % 2][:].rearrange("p (kt t) -> p kt t", t=TC)
            st_sem = pb.sem(f"sl{c % 2}")
            st_v = E("gpsimd", lambda g, d=dst, s=srcv: g.dma_start(d, s),
                     waits=[("ge", out_done), x0_ready[c]], inc=st_sem, inck=16)

            if c + 2 < NCH:
                x0_ready[c + 2] = x0_load(c + 2, waits=[(st_sem, st_v)])

            barrier = [("pe", pb.now("pe")), ("ve", pb.now("ve")),
                       ("se", pb.now("se")), ("ge", out_done)]
            buf_edges["zb"] = ("ve", pb.now("ve"))
            buf_edges["xcb"] = ("pe", pb.now("pe"))

        fin = [(s, pb.now(s)) for s in ("sl0", "sl1", "tl") if s in pb.semc]
        E("gpsimd", lambda g: g.nop(), waits=fin)

        with nc.Block() as block:
            @block.sync
            def _(e):
                for f in pb.order["sync"]:
                    f(e)

            @block.tensor
            def _(e):
                for f in pb.order["tensor"]:
                    f(e)

            @block.vector
            def _(e):
                for f in pb.order["vector"]:
                    f(e)

            @block.scalar
            def _(e):
                for f in pb.order["scalar"]:
                    f(e)

            @block.gpsimd
            def _(e):
                for f in pb.order["gpsimd"]:
                    f(e)

    return nc, taps


def _host_inputs(inputs, b):
    f = np.ascontiguousarray
    ip = inputs
    m = {
        "xT": f(np.asarray(ip["L0_attn_x"][b]).T),
        "ipw": f(ip["L0_attn_in_proj_w"]),
        "opw": f(ip["L0_attn_out_proj_w"]),
        "xpw": f(ip["L0_attn_x_proj_w"]),
        "dtw": f(ip["L0_attn_dt_proj_w"]),
        "w1": f(ip["L0_attn_w1"]),
        "w2": f(ip["L0_attn_w2"]),
        "gw": f(np.asarray(ip["L0_ffn_gate_w"])),
        "uw": f(np.asarray(ip["L0_ffn_up_w"])),
        "dw": f(np.asarray(ip["L0_ffn_down_w"])),
        "rtw": f(ip["L0_attn_router_w"]),
        "frw": f(np.asarray(ip["L0_ffn_router_w"]).T),
        "norm1c": f(np.asarray(ip["L0_attn_norm1_w"]).reshape(8, 128).T),
        "norm2c": f(np.asarray(ip["L0_attn_norm2_w"]).reshape(8, 128).T),
        "convw_c": f(np.asarray(ip["L0_attn_conv_w"]).reshape(16, 128, 4)
                     .transpose(1, 0, 2).reshape(128, 64)),
        "convb_c": f(np.asarray(ip["L0_attn_conv_b"]).reshape(16, 128).T),
        "dtb_c": f(np.asarray(ip["L0_attn_dt_proj_b"]).reshape(16, 128).T),
        "Dp_c": f(np.asarray(ip["L0_attn_D_param"]).reshape(16, 128).T),
        "Alog_c": f(np.asarray(ip["L0_attn_A_log"]).reshape(16, 128, 16)
                    .transpose(1, 0, 2).reshape(128, 16 * 16)),
        "b1_c": f(np.asarray(ip["L0_attn_b1"]).reshape(8, 16, 128)
                  .transpose(2, 0, 1).reshape(128, 8 * 16)),
        "b2_c": f(np.asarray(ip["L0_attn_b2"]).reshape(8, 8, 128)
                  .transpose(2, 0, 1).reshape(128, 8 * 8)),
        "gb_c": f(np.asarray(ip["L0_ffn_gate_b"]).reshape(8, 16, 128)
                  .transpose(2, 0, 1).reshape(128, 8 * 16)),
        "ub_c": f(np.asarray(ip["L0_ffn_up_b"]).reshape(8, 16, 128)
                  .transpose(2, 0, 1).reshape(128, 8 * 16)),
        "db_c": f(np.asarray(ip["L0_ffn_down_b"]).reshape(8, 8, 128)
                  .transpose(2, 0, 1).reshape(128, 8 * 8)),
        "ident_c": np.eye(128, dtype=np.float32),
        "misc_c": np.tile(np.array([[1e-5, 1e-6, 1.0, 0.0]], np.float32),
                          (128, 1)),
        "onescol_c": np.ones((128, 1), np.float32),
        "onesrow_c": np.ones((1, 128), np.float32),
    }
    import ml_dtypes
    sel = np.zeros((16, 16, 128), np.float32)
    for s in range(16):
        sel[s, s, :] = 1.0
    m["sel16_c"] = f(sel.transpose(1, 0, 2).reshape(16, 16 * 128)
                     .astype(ml_dtypes.bfloat16))
    m["onescol_c"] = m["onescol_c"].astype(ml_dtypes.bfloat16)
    m["onesrow_c"] = m["onesrow_c"].astype(ml_dtypes.bfloat16)
    for k in m:
        if m[k].dtype == np.float64:
            m[k] = m[k].astype(np.float32)
    return m


def kernel(**inputs):
    nc, _ = build_program(DEBUG_CHUNKS)
    in_maps = [_host_inputs(inputs, b) for b in range(N_CORES)]
    res = run_bass_kernel_spmd(nc, in_maps, core_ids=list(range(N_CORES)))
    outs = [np.ascontiguousarray(np.asarray(r["out"]).T)
            for r in res.results]
    return np.stack(outs, 0)


# version 03
# speedup vs baseline: 1.1552x; 1.1552x over previous
import contextlib
import numpy as np
import concourse.bass as bass
import concourse.mybir as mybir
from concourse.bass_utils import run_bass_kernel_spmd

F32 = mybir.dt.float32
BF16 = mybir.dt.bfloat16
AF = mybir.ActivationFunctionType
ALU = mybir.AluOpType
AX = mybir.AxisListType

D = 1024
DIN = 2048
T = 4096
TC = 512
NCH = T // TC
NS = 16
DTR = 64
NE = 8
KD = D // 128
KI = DIN // 128
MD = D // 128
MI = DIN // 128

N_CORES = 8
DEBUG_CHUNKS = ()


class PB:

    ENGS = ("tensor", "vector", "scalar", "gpsimd", "sync")

    def __init__(self, nc, stack, self_chain=(), dedup=True):
        self.nc = nc
        self.stack = stack
        self.dedup = dedup
        self.order = {e: [] for e in self.ENGS}
        self.semh = {}
        self.semc = {}
        for name in ("pe", "ve", "se", "ge"):
            self.sem(name)
        self.waited = {e: {} for e in self.ENGS}
        self.self_chain = {"tensor": "pe", "vector": "ve", "scalar": "se",
                           "gpsimd": "ge"}
        self.self_chain = {k: v for k, v in self.self_chain.items()
                           if k in self_chain}

    def sem(self, name):
        if name not in self.semh:
            self.semh[name] = self.stack.enter_context(self.nc.semaphore(name))
            self.semc[name] = 0
        return name

    def emit(self, eng, build, waits=(), inc=None, inck=1):
        chain = self.self_chain.get(eng)
        if chain is not None:
            waits = list(waits) + [(chain, self.semc[chain])]
            if inc is None:
                inc = chain
        w = []
        for (s, v) in waits:
            if v is None or v <= 0:
                continue
            if self.dedup and self.waited[eng].get(s, 0) >= v:
                continue
            if self.waited[eng].get(s, 0) < v:
                self.waited[eng][s] = v
            if not any(h is self.semh[s] and vv >= v for (h, vv) in w):
                w.append((self.semh[s], v))
        val = None
        if inc is not None:
            self.semc[inc] += inck
            val = self.semc[inc]
            hinc = self.semh[inc]
        w = tuple(w)
        if inc is None:
            def thunk(e, _w=w, _b=build):
                for (h, v) in _w:
                    e.wait_ge(h, v)
                _b(e)
        else:
            def thunk(e, _w=w, _b=build, _h=hinc, _k=inck):
                for (h, v) in _w:
                    e.wait_ge(h, v)
                _b(e).then_inc(_h, _k)
        self.order[eng].append(thunk)
        return val

    def now(self, sem):
        return self.semc.get(sem, 0)


def build_program(debug_chunks=(), sim_edges=(), interleave=True):
    nc = bass.Bass("TRN2", target_bir_lowering=False)

    def din(name, shape, dtype=F32):
        return nc.dram_tensor(name, list(shape), dtype,
                              kind="ExternalInput").ap()

    xT = din("xT", [D, T])
    ipw = din("ipw", [D, 2 * DIN])
    opw = din("opw", [DIN, D])
    xpw = din("xpw", [DIN, 96])
    dtw = din("dtw", [DTR, DIN])
    w1 = din("w1", [NE, D, DIN])
    w2 = din("w2", [NE, DIN, D])
    gw = din("gw", [NE, D, DIN])
    uw = din("uw", [NE, D, DIN])
    dw = din("dw", [NE, DIN, D])
    rtw = din("rtw", [D, NE])
    frw = din("frw", [D, NE])
    norm1c = din("norm1c", [128, KD])
    norm2c = din("norm2c", [128, KD])
    convw_c = din("convw_c", [128, MI * 4])
    convb_c = din("convb_c", [128, MI])
    dtb_c = din("dtb_c", [128, MI])
    Dp_c = din("Dp_c", [128, MI])
    Alog_c = din("Alog_c", [128, MI * NS])
    b1_c = din("b1_c", [128, NE * MI])
    b2_c = din("b2_c", [128, NE * MD])
    gb_c = din("gb_c", [128, NE * MI])
    ub_c = din("ub_c", [128, NE * MI])
    db_c = din("db_c", [128, NE * MD])
    sel16_c = din("sel16_c", [16, 16 * 128], BF16)
    ident_c = din("ident_c", [128, 128])
    onescol_c = din("onescol_c", [128, 1], BF16)
    onesrow_c = din("onesrow_c", [1, 128], BF16)
    misc_c = din("misc_c", [128, 4])

    out = nc.dram_tensor("out", [D, T], F32, kind="ExternalOutput").ap()

    ipw_bf = nc.dram_tensor("ipw_bf", [D, 2 * DIN], BF16, kind="Internal").ap()
    opw_bf = nc.dram_tensor("opw_bf", [DIN, D], BF16, kind="Internal").ap()
    w1_bf = nc.dram_tensor("w1_bf", [NE, D, DIN], BF16, kind="Internal").ap()
    w2_bf = nc.dram_tensor("w2_bf", [NE, DIN, D], BF16, kind="Internal").ap()
    gw_bf = nc.dram_tensor("gw_bf", [NE, D, DIN], BF16, kind="Internal").ap()
    uw_bf = nc.dram_tensor("uw_bf", [NE, D, DIN], BF16, kind="Internal").ap()
    dw_bf = nc.dram_tensor("dw_bf", [NE, DIN, D], BF16, kind="Internal").ap()

    taps = {}

    with contextlib.ExitStack() as stack:
        def sb(name, shape, dtype=F32):
            return stack.enter_context(nc.sbuf_tensor(name, list(shape),
                                                      dtype))
        def psum(name, shape, dtype=F32):
            return stack.enter_context(nc.psum_tensor(name, list(shape),
                                                      dtype))

        c_norm1 = sb("c_norm1", [128, KD])
        c_norm2 = sb("c_norm2", [128, KD])
        c_convw = sb("c_convw", [128, MI * 4])
        c_convb = sb("c_convb", [128, MI])
        c_dtb = sb("c_dtb", [128, MI])
        c_Dp = sb("c_Dp", [128, MI])
        c_Alog = sb("c_Alog", [128, MI * NS])
        c_b1 = sb("c_b1", [128, NE * MI])
        c_b2 = sb("c_b2", [128, NE * MD])
        c_gb = sb("c_gb", [128, NE * MI])
        c_ub = sb("c_ub", [128, NE * MI])
        c_db = sb("c_db", [128, NE * MD])
        c_sel16 = sb("c_sel16", [16, 16 * 128], BF16)
        c_ident = sb("c_ident", [128, 128])
        c_onescol = sb("c_onescol", [128, 1], BF16)
        c_onesrow = sb("c_onesrow", [1, 128], BF16)
        c_misc = sb("c_misc", [128, 4])

        r_xpw = sb("r_xpw", [128, KI * 96], BF16)
        r_dtw = sb("r_dtw", [DTR, DIN], BF16)
        r_rtw = sb("r_rtw", [128, KD * NE], BF16)
        r_frw = sb("r_frw", [128, KD * NE], BF16)

        NRING = 2
        wring = [sb(f"wring{i}", [128, 2048], BF16) for i in range(NRING)]

        x0b = [sb(f"x0_{i}", [128, MD * TC]) for i in range(2)]
        t1b = [sb(f"t1_{i}", [128, MD * TC]) for i in range(2)]
        h1b = sb("h1b", [128, MD * TC], BF16)
        htb = sb("htb", [128, MD * TC], BF16)
        zb = sb("zb", [128, MI * TC], BF16)
        xcb = [sb(f"xc_{i}", [128, MI * TC], BF16) for i in range(2)]
        tailb = sb("tailb", [128, MI * 3], BF16)
        dtrb = sb("dtrb", [DTR, TC], BF16)
        Bsb = sb("Bsb", [NS, TC], BF16)
        Csb = sb("Csb", [NS, TC], BF16)
        dtb2 = sb("dtb2", [128, TC])
        wbuf = sb("wbuf", [128, TC])
        dAb = sb("dAb", [128, TC])
        ubuf = sb("ubuf", [128, TC])
        hbuf = sb("hbuf", [128, TC])
        ybuf = sb("ybuf", [128, TC])
        carry = sb("carry", [128, MI * NS])
        hidb = sb("hidb", [128, KI * TC], BF16)
        Gsb = sb("Gsb", [128, TC], BF16)
        Get = sb("Get", [NE, TC], BF16)
        Pet = sb("Pet", [NE, TC], BF16)
        Gtb = sb("Gtb", [128, 32])
        RLb = sb("RLb", [NE, TC])
        rsc = sb("rsc", [128, 32])
        strow = sb("strow", [1, TC])
        stbf = sb("stbf", [1, TC], BF16)
        stsrc = sb("stsrc", [128, 2 * TC], BF16)
        bcA = sb("bcA", [128, TC])
        bcB = sb("bcB", [128, TC])
        pool = [sb(f"pool{i}", [128, TC]) for i in range(4)]

        pm = [psum(f"pm{i}", [128, TC]) for i in range(4)]
        pbA = psum("pbA", [128, TC])
        pbB = psum("pbB", [128, TC])
        ps6 = psum("ps6", [1, TC])
        ps7 = psum("ps7", [1, TC])

        pb = PB(nc, stack, self_chain=sim_edges, dedup=not sim_edges)
        E = pb.emit

        pm_edge = [None] * 4
        pm_i = [0]
        pbA_edge = [None]
        pbB_edge = [None]
        pool_edge = [None] * len(pool)
        pool_i = [0]
        stats_edge = [None]
        router_edge = [None]
        dA_edge = [None, None]
        dt_edge = [None]
        edges = {}

        def next_pm():
            i = pm_i[0] % 4
            pm_i[0] += 1
            return i, pm[i], ([pm_edge[i]] if pm_edge[i] else [])

        def next_pool():
            i = pool_i[0] % len(pool)
            pool_i[0] += 1
            return i, pool[i], ([pool_edge[i]] if pool_edge[i] else [])

        def gload(dst_ap, src_ap, sem="glc"):
            return E("gpsimd", lambda g, d=dst_ap, s=src_ap: g.dma_start(d, s),
                     inc=pb.sem(sem), inck=16)

        for dst, src in [
            (c_norm1[:], norm1c), (c_norm2[:], norm2c), (c_convw[:], convw_c),
            (c_convb[:], convb_c), (c_dtb[:], dtb_c), (c_Dp[:], Dp_c),
            (c_Alog[:], Alog_c), (c_b1[:], b1_c), (c_b2[:], b2_c),
            (c_gb[:], gb_c), (c_ub[:], ub_c), (c_db[:], db_c),
            (c_sel16[:], sel16_c), (c_ident[:], ident_c),
            (c_onescol[:], onescol_c), (c_onesrow[:], onesrow_c),
            (c_misc[:], misc_c),
        ]:
            gload(dst, src)
        gload(r_xpw[:].rearrange("p (kt m) -> p kt m", m=96),
              xpw.rearrange("(kt p) m -> p kt m", p=128))
        gload(r_dtw[:], dtw)
        gload(r_rtw[:].rearrange("p (kt m) -> p kt m", m=NE),
              rtw.rearrange("(kt p) m -> p kt m", p=128))
        gload(r_frw[:].rearrange("p (kt m) -> p kt m", m=NE),
              frw.rearrange("(kt p) m -> p kt m", p=128))

        def x0_load(c, waits=()):
            dst = x0b[c % 2][:].rearrange("p (kt t) -> p kt t", t=TC)
            src = xT.rearrange("(kt p) t -> p kt t", p=128)[
                :, :, c * TC:(c + 1) * TC]
            sem = pb.sem(f"glx{c % 2}")
            v = E("gpsimd", lambda g, d=dst, s=src: g.dma_start(d, s),
                  waits=waits, inc=sem, inck=16)
            return (sem, v)
        x0_ready = [None] * NCH
        x0_ready[0] = x0_load(0)
        x0_ready[1] = x0_load(1)
        consts_gl = pb.now("glc")

        prep = {}
        def prep_dma(key, dst_ap, src_ap):
            sem = pb.sem(f"gp{len(prep)}")
            v = E("gpsimd", lambda g, d=dst_ap, s=src_ap: g.dma_start(d, s),
                  inc=sem, inck=16)
            prep[key] = (sem, v)
        prep_dma("ipw", ipw_bf, ipw)
        prep_dma("opw", opw_bf, opw)
        for e in range(NE):
            prep_dma(("w1", e), w1_bf[e], w1[e])
            prep_dma(("w2", e), w2_bf[e], w2[e])
        for e in range(NE):
            prep_dma(("gw", e), gw_bf[e], gw[e])
            prep_dma(("uw", e), uw_bf[e], uw[e])
        for e in range(NE):
            prep_dma(("dw", e), dw_bf[e], dw[e])

        vA = E("scalar", lambda s: s.activation(c_Alog[:], c_Alog[:], AF.Exp),
               waits=[("glc", consts_gl)], inc="se")
        vA2 = E("vector", lambda v: v.tensor_scalar(
            c_Alog[:], c_Alog[:], -1.0, None, ALU.mult),
            waits=[("se", vA)], inc="ve")
        E("vector", lambda v: v.memset(carry[:], 0.0), inc="ve")
        vz2 = E("vector", lambda v: v.memset(tailb[:], 0.0), inc="ve")
        edges["tail0"] = ("ve", vz2)

        ring_state = {"n": 0, "slot_edge": [None] * NRING}
        for _i in range(NRING):
            pb.sem(f"wl{_i}")

        def ring_load(key, src_view):
            slot = ring_state["n"] % NRING
            sem = f"wl{slot}"
            waits = []
            if key in prep:
                waits.append(prep[key])
            if ring_state["slot_edge"][slot] is not None:
                waits.append(ring_state["slot_edge"][slot])
                waits.append((sem, pb.now(sem)))
            dst = wring[slot][:].rearrange(
                "p (kt m) -> p kt m", m=src_view.shape[2])
            wl = E("sync", lambda sy, d=dst, s=src_view: sy.dma_start(d, s),
                   waits=waits, inc=sem, inck=16)
            ring_state["n"] += 1
            return slot, (sem, wl)

        def ring_release(slot, edge):
            ring_state["slot_edge"][slot] = edge

        def tap(name, c, src_ap, shape, dtype, wait):
            if c not in debug_chunks:
                return
            key = f"tap_{name}_{c}"
            t = nc.dram_tensor(key, list(shape), dtype,
                               kind="ExternalOutput").ap()
            taps[key] = (name, c)
            v = E("gpsimd", lambda g, d=t, s=src_ap: g.dma_start(d, s),
                  waits=[wait], inc=pb.sem("tl"), inck=16)
            for eng in ("tensor", "vector", "scalar", "gpsimd"):
                E(eng, lambda x: x.nop(), waits=[("tl", v)])

        def stats(xbuf, nkt, need_mean, src_waits):
            sv = None
            for kt in range(nkt):
                xs = xbuf[:, kt * TC:(kt + 1) * TC]
                w_ = list(src_waits[kt]) if src_waits else []
                if stats_edge[0]:
                    w_.append(stats_edge[0])
                cv = None
                if need_mean:
                    cv = E("vector", lambda v, a=xs: v.tensor_copy(
                        stsrc[:, 0:TC], a), waits=w_, inc="ve")
                sv = E("scalar", lambda s, a=xs: s.activation(
                    stsrc[:, TC:2 * TC], a, AF.Square), waits=w_, inc="se")
                st_ = (kt == 0)
                sp = (kt == nkt - 1)
                if need_mean:
                    E("tensor", lambda t, st_=st_, sp=sp: t.matmul(
                        ps6[0:1, :], c_onescol[:, 0:1], stsrc[:, 0:TC],
                        start=st_, stop=sp),
                        waits=[("ve", cv), ("glc", consts_gl)], inc="pe")
                sq_v = E("tensor", lambda t, st_=st_, sp=sp: t.matmul(
                    ps7[0:1, :], c_onescol[:, 0:1], stsrc[:, TC:2 * TC],
                    start=st_, stop=sp),
                    waits=[("se", sv), ("glc", consts_gl)], inc="pe")
                stats_edge[0] = ("pe", sq_v)
            return sv

        stbf_edge = [None]

        def bcast_row_ap(src_row, target, extra_waits):
            w_ = list(extra_waits)
            if stbf_edge[0]:
                w_.append(stbf_edge[0])
            cb = E("vector", lambda v, s=src_row: v.tensor_copy(
                stbf[0:1, :], s), waits=w_, inc="ve")
            bv = E("tensor", lambda t: t.matmul(
                pbA[:], c_onesrow[0:1, :], stbf[0:1, :], start=True,
                stop=True), waits=[("ve", cb)]
                + ([pbA_edge[0]] if pbA_edge[0] else []), inc="pe")
            stbf_edge[0] = ("pe", bv)
            ev = E("vector", lambda v, o=target: v.tensor_copy(o[:], pbA[:]),
                   waits=[("pe", bv)], inc="ve")
            pbA_edge[0] = ("ve", ev)
            return ("ve", ev)

        def bcast_row(src_is_strow_off, target, extra_waits):
            w_ = list(extra_waits)
            if stbf_edge[0]:
                w_.append(stbf_edge[0])
            cb = E("vector", lambda v: v.tensor_copy(
                stbf[0:1, :], strow[0:1, src_is_strow_off:
                                    src_is_strow_off + TC]),
                waits=w_, inc="ve")
            bv = E("tensor", lambda t: t.matmul(
                pbA[:], c_onesrow[0:1, :], stbf[0:1, :], start=True,
                stop=True), waits=[("ve", cb)]
                + ([pbA_edge[0]] if pbA_edge[0] else []), inc="pe")
            stbf_edge[0] = ("pe", bv)
            ev = E("vector", lambda v, o=target: v.tensor_copy(o[:], pbA[:]),
                   waits=[("pe", bv)], inc="ve")
            pbA_edge[0] = ("ve", ev)
            return ("ve", ev)

        def ln_rows(kind):
            out = {}
            prow = prow2 = None
            if kind in ("ln1", "ln2"):
                _, prow, wrow = next_pool()
                _, prow2, wrow2 = next_pool()
                tmp = prow[0:1, 0:TC]
                var = prow2[0:1, 0:TC]
                v1 = E("vector", lambda v: v.tensor_scalar(
                    strow[0:1, 0:TC], ps6[0:1, :], 1.0 / D, None, ALU.mult),
                    waits=[stats_edge[0]], inc="ve")
                E("vector", lambda v, var=var: v.tensor_scalar(
                    var, ps7[0:1, :], 1.0 / D, None,
                    ALU.mult), waits=wrow2, inc="ve")
                v3 = E("vector", lambda v, tmp=tmp: v.tensor_tensor(
                    tmp, strow[0:1, 0:TC],
                    strow[0:1, 0:TC], ALU.mult),
                    waits=[("ve", v1)] + wrow, inc="ve")
                v4 = E("vector", lambda v, tmp=tmp, var=var: v.tensor_tensor(
                    var, var,
                    tmp, ALU.subtract),
                    waits=[("ve", v3)], inc="ve")
                s1 = E("scalar", lambda s, tmp=tmp, var=var: s.activation(
                    tmp, var, AF.Ln,
                    bias=c_misc[0:1, 0:1]), waits=[("ve", v4)], inc="se")
                s2 = E("scalar", lambda s, tmp=tmp: s.activation(
                    tmp, tmp,
                    AF.Exp, scale=-0.5), waits=[("se", s1)], inc="se")
                out["mu"] = bcast_row(0, bcA, [("ve", v4)])
                out["r"] = bcast_row_ap(tmp, bcB, [("se", s2)])
            if kind == "ln1":
                var = prow2[0:1, 0:TC]
                w1v = E("vector", lambda v, var=var: v.tensor_scalar(
                    strow[0:1, 0:TC], var, 1e-5, None,
                    ALU.add), waits=[out["mu"]], inc="ve")
                w2r = E("vector", lambda v: v.reciprocal(
                    strow[0:1, 0:TC], strow[0:1, 0:TC]),
                    waits=[("ve", w1v)], inc="ve")
                w2v = E("vector", lambda v, var=var: v.tensor_tensor(
                    strow[0:1, 0:TC], var,
                    strow[0:1, 0:TC], ALU.mult), waits=[("ve", w2r)],
                    inc="ve")
                s3 = E("scalar", lambda s: s.activation(
                    strow[0:1, 0:TC], strow[0:1, 0:TC], AF.Ln,
                    bias=c_misc[0:1, 1:2]), waits=[("ve", w2v)], inc="se")
                s4 = E("scalar", lambda s: s.activation(
                    strow[0:1, 0:TC], strow[0:1, 0:TC], AF.Exp, scale=-0.5),
                    waits=[("se", s3)], inc="se")
                out["rq_row"] = ("se", s4)
                pool_edge[pool.index(prow)] = out["r"]
                pool_edge[pool.index(prow2)] = ("se", s3)
            elif prow is not None:
                pool_edge[pool.index(prow)] = out["r"]
                pool_edge[pool.index(prow2)] = ("se", s1)
            elif kind == "rms":
                v2 = E("vector", lambda v: v.tensor_scalar(
                    strow[0:1, 0:TC], ps7[0:1, :], 1.0 / D, None, ALU.mult),
                    waits=[stats_edge[0]], inc="ve")
                s1 = E("scalar", lambda s: s.activation(
                    strow[0:1, 0:TC], strow[0:1, 0:TC], AF.Ln,
                    bias=c_misc[0:1, 1:2]), waits=[("ve", v2)], inc="se")
                s2_ = E("scalar", lambda s: s.activation(
                    strow[0:1, 0:TC], strow[0:1, 0:TC], AF.Exp, scale=-0.5),
                    waits=[("se", s1)], inc="se")
                out["rho"] = bcast_row(0, bcB, [("se", s2_)])
            return out

        front_state = {}

        def emit_front1(c):
            st = {}
            cpar = c % 2
            ln1_sq = stats(x0b[cpar], MD, True,
                           [[x0_ready[c]] + edges.get(f"x0_{cpar}", [])
                            for _ in range(MD)])
            st["ln1_sq"] = ln1_sq
            rows = ln_rows("ln1")
            a_v = None
            for kt in range(KD):
                xs = x0b[cpar][:, kt * TC:(kt + 1) * TC]
                t1s = t1b[cpar][:, kt * TC:(kt + 1) * TC]
                E("vector", lambda v, a=t1s, b=xs: v.tensor_tensor(
                    a, b, bcA[:], ALU.subtract),
                    waits=[rows["mu"], x0_ready[c]]
                    + edges.get(f"t1_{cpar}", []), inc="ve")
                a_v = E("vector", lambda v, a=t1s: v.tensor_tensor(
                    a, a, bcB[:], ALU.mult), waits=[rows["r"]], inc="ve")
            rq_done = bcast_row(0, bcB, [rows["rq_row"], ("ve", a_v)])
            h1_v = None
            for kt in range(KD):
                t1s = t1b[cpar][:, kt * TC:(kt + 1) * TC]
                h1s = h1b[:, kt * TC:(kt + 1) * TC]
                h1_v = E("vector", lambda v, o=h1s, a=t1s, kt=kt:
                         v.scalar_tensor_tensor(
                             o, a, c_norm1[:, kt:kt + 1], bcB[:], ALU.mult,
                             ALU.mult),
                         waits=[rq_done] + edges.get("h1b", []),
                         inc="ve")
            st["h1"] = ("ve", h1_v)
            tap("a", c, t1b[cpar][:], [128, MD * TC], F32, ("ve", a_v))
            tap("h1", c, h1b[:], [128, MD * TC], BF16, ("ve", h1_v))
            conv_done = [None] * MI
            last_pe = None
            for hs in range(8):
                src = ipw_bf.rearrange("(kt p) m -> p kt m", p=128)[
                    :, :, hs * 256:(hs + 1) * 256]
                slot, wl_v = ring_load("ipw", src)
                for mt_in in range(2):
                    mi = hs * 2 + mt_in
                    pi, pmt, pmw = next_pm()
                    for kt in range(KD):
                        lhs = wring[slot][:, kt * 256 + mt_in * 128:
                                          kt * 256 + mt_in * 128 + 128]
                        rhs = h1b[:, kt * TC:(kt + 1) * TC]
                        last_pe = E("tensor", lambda t, o=pmt, l=lhs, r=rhs,
                                    st_=(kt == 0), sp=(kt == KD - 1):
                                    t.matmul(o[:], l, r, start=st_, stop=sp),
                                    waits=[wl_v, ("ve", h1_v)] + pmw,
                                    inc="pe" if kt == KD - 1 else None)
                    _, acc, accw = next_pool()
                    w_ = [("pe", last_pe), ("glc", consts_gl),
                          edges["tail0"]] + accw
                    E("vector", lambda v, a=acc, p=pmt, mi=mi:
                      v.tensor_scalar(
                          a[:], p[:], c_convw[:, mi * 4 + 3:mi * 4 + 4],
                          None, ALU.mult), waits=w_, inc="ve")
                    for k in (2, 1, 0):
                        n = 3 - k
                        E("vector", lambda v, a=acc, p=pmt, mi=mi, k=k, n=n:
                          v.scalar_tensor_tensor(
                              a[:, n:TC], p[:, 0:TC - n],
                              c_convw[:, mi * 4 + k:mi * 4 + k + 1],
                              a[:, n:TC], ALU.mult, ALU.add), inc="ve")
                        E("vector", lambda v, a=acc, mi=mi, k=k, n=n:
                          v.scalar_tensor_tensor(
                              a[:, 0:n], tailb[:, mi * 3 + k:mi * 3 + 3],
                              c_convw[:, mi * 4 + k:mi * 4 + k + 1],
                              a[:, 0:n], ALU.mult, ALU.add), inc="ve")
                    tl_v = E("vector", lambda v, p=pmt, mi=mi: v.tensor_copy(
                        tailb[:, mi * 3:mi * 3 + 3], p[:, TC - 3:TC]),
                        inc="ve")
                    cd = E("scalar", lambda s, a=acc, mi=mi: s.activation(
                        xcb[cpar][:, mi * TC:(mi + 1) * TC], a[:], AF.Silu,
                        bias=c_convb[:, mi:mi + 1]),
                        waits=[("ve", tl_v)] + edges.get(f"xc_{cpar}", []),
                        inc="se")
                    conv_done[mi] = cd
                    pm_edge[pi] = ("se", cd)
                    pool_edge[pool.index(acc)] = ("se", cd)
                ring_release(slot, ("pe", last_pe))
            edges["tail0"] = ("ve", pb.now("ve"))
            st["conv"] = conv_done
            front_state[c] = st
            tap("xc", c, xcb[cpar][:], [128, MI * TC], BF16,
                ("se", max(conv_done)))

        def emit_front2(c):
            st = front_state[c]
            cpar = c % 2
            z_done = [None] * MI
            last_pe = None
            for hs in range(8, 16):
                src = ipw_bf.rearrange("(kt p) m -> p kt m", p=128)[
                    :, :, hs * 256:(hs + 1) * 256]
                slot, wl_v = ring_load("ipw", src)
                for mt_in in range(2):
                    mi = (hs - 8) * 2 + mt_in
                    pi, pmt, pmw = next_pm()
                    for kt in range(KD):
                        lhs = wring[slot][:, kt * 256 + mt_in * 128:
                                          kt * 256 + mt_in * 128 + 128]
                        rhs = h1b[:, kt * TC:(kt + 1) * TC]
                        last_pe = E("tensor", lambda t, o=pmt, l=lhs, r=rhs,
                                    st_=(kt == 0), sp=(kt == KD - 1):
                                    t.matmul(o[:], l, r, start=st_, stop=sp),
                                    waits=[wl_v, st["h1"]] + pmw,
                                    inc="pe" if kt == KD - 1 else None)
                    zd = E("scalar", lambda s, p=pmt, mi=mi: s.activation(
                        zb[:, mi * TC:(mi + 1) * TC], p[:], AF.Silu),
                        waits=[("pe", last_pe)] + edges.get("zb", []),
                        inc="se")
                    z_done[mi] = zd
                    pm_edge[pi] = ("se", zd)
                ring_release(slot, ("pe", last_pe))
            st["z"] = z_done
            edges["h1b"] = [("pe", last_pe)]
            tap("z", c, zb[:], [128, MI * TC], BF16, ("se", max(z_done)))

            def xp_part(col0, mc, evict):
                pi, pmt, pmw = next_pm()
                last = None
                for kt in range(KI):
                    lhs = r_xpw[:, kt * 96 + col0:kt * 96 + col0 + mc]
                    rhs = xcb[cpar][:, kt * TC:(kt + 1) * TC]
                    last = E("tensor", lambda t, o=pmt, l=lhs, r=rhs,
                             st_=(kt == 0), sp=(kt == KI - 1), mc=mc:
                             t.matmul(o[0:mc, :], l, r, start=st_, stop=sp),
                             waits=[("se", st["conv"][kt]),
                                    ("glc", consts_gl)] + pmw,
                             inc="pe" if kt == KI - 1 else None)
                ev = evict(pmt, last)
                pm_edge[pi] = ev
                return ev
            st["dtr"] = xp_part(0, DTR, lambda p, l: (
                "se", E("scalar", lambda s, p=p: s.activation(
                    dtrb[:], p[0:DTR, :], AF.Copy),
                    waits=[("pe", l)] + edges.get("dtrb", []), inc="se")))
            st["B"] = xp_part(DTR, NS, lambda p, l: (
                "se", E("scalar", lambda s, p=p: s.activation(
                    Bsb[:], p[0:NS, :], AF.Copy),
                    waits=[("pe", l)] + edges.get("Bsb", []), inc="se")))
            st["C"] = xp_part(DTR + NS, NS, lambda p, l: (
                "se", E("scalar", lambda s, p=p: s.activation(
                    Csb[:], p[0:NS, :], AF.Copy),
                    waits=[("pe", l)] + edges.get("Bsb", []), inc="se")))
            tap("B", c, Bsb[:], [NS, TC], BF16, st["B"])
            tap("C", c, Csb[:], [NS, TC], BF16, st["C"])

        def emit_scan(c, pull):
            st = front_state[c]
            cpar = c % 2
            yz_done = [None] * MI
            for mi in range(MI):
                dts = dtb2[:, 0:TC]
                pi, pmt, pmw = next_pm()
                last = E("tensor", lambda t, o=pmt, mi=mi: t.matmul(
                    o[:], r_dtw[:, mi * 128:(mi + 1) * 128], dtrb[:],
                    start=True, stop=True),
                    waits=[st["dtr"], ("glc", consts_gl)] + pmw, inc="pe")
                _, p_av, w_av = next_pool()
                _, p_m, w_m = next_pool()
                E("vector", lambda v, a=p_av, p=pmt, mi=mi: v.tensor_scalar(
                    a[:], p[:], c_dtb[:, mi:mi + 1], 0.0, ALU.add, ALU.min),
                    waits=[("pe", last)] + w_av + w_m, inc="ve")
                E("vector", lambda v, a=p_m, p=pmt, mi=mi: v.tensor_scalar(
                    a[:], p[:], c_dtb[:, mi:mi + 1], 0.0, ALU.add, ALU.max),
                    inc="ve")
                E("vector", lambda v, a=p_av, b=p_m: v.tensor_tensor(
                    a[:], a[:], b[:], ALU.subtract), inc="ve")
                pm_edge[pi] = ("ve", pb.now("ve"))
                s_e = E("scalar", lambda s, a=p_av: s.activation(
                    a[:], a[:], AF.Exp), waits=[("ve", pb.now("ve"))],
                    inc="se")
                s_l = E("scalar", lambda s, a=p_av: s.activation(
                    a[:], a[:], AF.Ln, bias=c_misc[:, 2:3]),
                    waits=[("se", s_e)], inc="se")
                w_dt = [("se", s_l)]
                if dt_edge[0]:
                    w_dt.append(dt_edge[0])
                dt_v = E("vector", lambda v, o=dts, a=p_m, b=p_av:
                         v.tensor_tensor(o, a[:], b[:], ALU.add),
                         waits=w_dt, inc="ve")
                pool_edge[pool.index(p_av)] = ("ve", dt_v)
                pool_edge[pool.index(p_m)] = ("ve", dt_v)
                if mi == 0:
                    tap("dt0", c, dts, [128, TC], F32, ("ve", dt_v))
                w_v = E("vector", lambda v, o=wbuf, a=dts, mi=mi:
                        v.tensor_tensor(
                            o[:], a, xcb[cpar][:, mi * TC:(mi + 1) * TC],
                            ALU.mult),
                        waits=[("se", st["conv"][mi])], inc="ve")
                y_v = None
                last_dA = None
                for s_i in range(NS):
                    spar = 0
                    dAs = dAb[:, 0:TC]
                    wA = [pbA_edge[0]] if pbA_edge[0] else []
                    wB = [pbB_edge[0]] if pbB_edge[0] else []
                    bA = E("tensor", lambda t, s_i=s_i: t.matmul(
                        pbA[:], c_sel16[:, s_i * 128:(s_i + 1) * 128],
                        Bsb[:], start=True, stop=True),
                        waits=[st["B"]] + wA, inc="pe")
                    bB = E("tensor", lambda t, s_i=s_i: t.matmul(
                        pbB[:], c_sel16[:, s_i * 128:(s_i + 1) * 128],
                        Csb[:], start=True, stop=True),
                        waits=[st["C"]] + wB, inc="pe")
                    w_da = [("ve", dt_v), ("ve", vA2)]
                    if dA_edge[spar]:
                        w_da.append(dA_edge[spar])
                    dA_v = E("scalar", lambda s, o=dAs, i=dts, mi=mi,
                             s_i=s_i: s.activation(
                                 o, i, AF.Exp,
                                 scale=c_Alog[:, mi * NS + s_i:
                                              mi * NS + s_i + 1]),
                             waits=w_da, inc="se")
                    last_dA = dA_v
                    E("vector", lambda v: v.tensor_tensor(
                        ubuf[:], wbuf[:], pbA[:], ALU.mult),
                        waits=[("pe", bA), ("ve", w_v)], inc="ve")
                    idx = mi * NS + s_i
                    sc_v = E("vector", lambda v, a=dAs, idx=idx:
                             v.tensor_tensor_scan(
                                 hbuf[:], a, ubuf[:], carry[:, idx:idx + 1],
                                 ALU.mult, ALU.add),
                             waits=[("se", dA_v)], inc="ve")
                    dA_edge[spar] = ("ve", sc_v)
                    E("vector", lambda v, idx=idx: v.tensor_copy(
                        carry[:, idx:idx + 1], hbuf[:, TC - 1:TC]),
                        waits=[("ve", sc_v)], inc="ve")
                    if s_i == 0:
                        y_v = E("vector", lambda v: v.tensor_tensor(
                            ybuf[:], hbuf[:], pbB[:], ALU.mult),
                            waits=[("pe", bB), ("ve", sc_v)], inc="ve")
                    else:
                        E("vector", lambda v: v.tensor_tensor(
                            hbuf[:], hbuf[:], pbB[:], ALU.mult),
                            waits=[("pe", bB), ("ve", sc_v)], inc="ve")
                        y_v = E("vector", lambda v: v.tensor_tensor(
                            ybuf[:], ybuf[:], hbuf[:], ALU.add), inc="ve")
                    pbA_edge[0] = ("ve", y_v)
                    pbB_edge[0] = ("ve", y_v)
                    pull(2)
                dt_edge[0] = ("se", last_dA)
                if mi == 0:
                    tap("y0", c, ybuf[:], [128, TC], F32, ("ve", y_v))
                _, p_t, w_t = next_pool()
                E("vector", lambda v, a=p_t, mi=mi: v.scalar_tensor_tensor(
                    a[:], xcb[cpar][:, mi * TC:(mi + 1) * TC],
                    c_Dp[:, mi:mi + 1], ybuf[:], ALU.mult, ALU.add),
                    waits=w_t, inc="ve")
                yzv = E("vector", lambda v, a=p_t, mi=mi: v.tensor_tensor(
                    xcb[cpar][:, mi * TC:(mi + 1) * TC], a[:],
                    zb[:, mi * TC:(mi + 1) * TC], ALU.mult),
                    waits=[("se", st["z"][mi])], inc="ve")
                yz_done[mi] = yzv
                pool_edge[pool.index(p_t)] = ("ve", yzv)
            tap("carry", c, carry[:], [128, MI * NS], F32,
                ("ve", pb.now("ve")))
            st["yz"] = yz_done
            edges["zb"] = [("ve", max(yz_done))]
            edges["dtrb"] = [("pe", pb.now("pe"))]
            edges["Bsb"] = [("ve", pb.now("ve"))]
            tap("yz", c, xcb[cpar][:], [128, MI * TC], BF16,
                ("ve", max(yz_done)))

        def router_gates(c, rhs_buf, rhs_done, wtile, out_et, eps, tapname):
            pi, pmt, pmw = next_pm()
            last = None
            for kt in range(KD):
                lhs = wtile[:, kt * NE:(kt + 1) * NE]
                rhs = rhs_buf[:, kt * TC:(kt + 1) * TC]
                last = E("tensor", lambda t, o=pmt, l=lhs, r=rhs,
                         st_=(kt == 0), sp=(kt == KD - 1): t.matmul(
                             o[0:NE, :], l, r, start=st_, stop=sp),
                         waits=[rhs_done, ("glc", consts_gl)] + pmw,
                         inc="pe" if kt == KD - 1 else None)
            w_rl = [("pe", last)]
            if router_edge[0]:
                w_rl.append(router_edge[0])
            rl_v = E("scalar", lambda s, p=pmt: s.activation(
                RLb[:], p[0:NE, :], AF.Copy), waits=w_rl, inc="se")
            pm_edge[pi] = ("se", rl_v)
            yield
            wA = [pbA_edge[0]] if pbA_edge[0] else []
            tp = None
            for ti in range(4):
                tp = E("tensor", lambda t, ti=ti: t.transpose(
                    pbA[:, ti * 8:(ti + 1) * 8],
                    RLb[:, ti * 128:(ti + 1) * 128], c_ident[0:8, 0:8]),
                    waits=[("se", rl_v)] + wA, inc="pe")
            gt_v = E("vector", lambda v: v.tensor_copy(
                Gtb[:], pbA[:, 0:32]), waits=[("pe", tp)], inc="ve")
            pbA_edge[0] = ("ve", gt_v)
            yield
            gv = gt_v
            for ti in range(4):
                g8 = Gtb[:, ti * 8:(ti + 1) * 8]
                m1 = rsc[:, 0:1]
                m2 = rsc[:, 1:2]
                sm = rsc[:, 8:16]
                p2 = rsc[:, 16:24]
                def VS(build, extra=()):
                    nonlocal gv
                    gv = E("vector", build,
                           waits=[("ve", gv)] + list(extra), inc="ve")
                    return gv
                VS(lambda v, g8=g8, m1=m1: v.tensor_reduce(
                    m1, g8, AX.X, ALU.max))
                VS(lambda v, g8=g8, m1=m1, sm=sm: v.tensor_scalar(
                    sm, g8, m1, None, ALU.subtract))
                se_v = E("scalar", lambda s, sm=sm: s.activation(
                    sm, sm, AF.Exp), waits=[("ve", gv)], inc="se")
                VS(lambda v, sm=sm, m1=m1: v.tensor_reduce(
                    m1, sm, AX.X, ALU.add), extra=[("se", se_v)])
                VS(lambda v, m1=m1: v.reciprocal(m1, m1))
                VS(lambda v, g8=g8, sm=sm, m1=m1: v.tensor_scalar(
                    g8, sm, m1, None, ALU.mult))
                VS(lambda v, g8=g8, m1=m1: v.tensor_reduce(
                    m1, g8, AX.X, ALU.max))
                VS(lambda v, g8=g8, m1=m1, sm=sm: v.tensor_scalar(
                    sm, g8, m1, None, ALU.is_ge))
                VS(lambda v, g8=g8, sm=sm, p2=p2: v.tensor_tensor(
                    p2, g8, sm, ALU.mult))
                VS(lambda v, g8=g8, p2=p2: v.tensor_tensor(
                    p2, g8, p2, ALU.subtract))
                VS(lambda v, p2=p2, m2=m2: v.tensor_reduce(
                    m2, p2, AX.X, ALU.max))
                VS(lambda v, p2=p2, m2=m2: v.tensor_scalar(
                    p2, p2, m2, None, ALU.is_ge))
                VS(lambda v, sm=sm, p2=p2: v.tensor_tensor(
                    sm, sm, p2, ALU.add))
                VS(lambda v, m1=m1, m2=m2: v.tensor_tensor(
                    m1, m1, m2, ALU.add))
                if eps:
                    VS(lambda v, m1=m1: v.tensor_scalar(
                        m1, m1, eps, None, ALU.add))
                VS(lambda v, g8=g8, sm=sm: v.tensor_tensor(
                    g8, g8, sm, ALU.mult))
                VS(lambda v, m1=m1: v.reciprocal(m1, m1))
                VS(lambda v, g8=g8, m1=m1: v.tensor_scalar(
                    g8, g8, m1, None, ALU.mult))
            yield
            wB = [pbB_edge[0]] if pbB_edge[0] else []
            tp2 = None
            for ti in range(4):
                tp2 = E("tensor", lambda t, ti=ti: t.transpose(
                    pbB[0:8, ti * 128:(ti + 1) * 128],
                    Gtb[:, ti * 8:(ti + 1) * 8], c_ident[:]),
                    waits=[("ve", gv)] + wB, inc="pe")
            et_v = E("vector", lambda v, o=out_et: v.tensor_copy(
                o[:], pbB[0:8, :]),
                waits=[("pe", tp2)] + edges.get(out_et.name, []), inc="ve")
            pbB_edge[0] = ("ve", et_v)
            router_edge[0] = ("pe", tp2)
            tap(tapname, c, out_et[:], [NE, TC], BF16, ("ve", et_v))
            front_state[c][tapname] = ("ve", et_v)

        def tail_units(c):
            st = front_state[c]
            cpar = c % 2
            xa_done = [None] * MD
            for hs in range(8):
                src = opw_bf.rearrange("(kt p) m -> p kt m", p=128)[
                    :, :, hs * 128:(hs + 1) * 128]
                slot, wl_v = ring_load("opw", src)
                m = hs
                pi, pmt, pmw = next_pm()
                last_pe = None
                for kt in range(KI):
                    lhs = wring[slot][:, kt * 128:kt * 128 + 128]
                    rhs = xcb[cpar][:, kt * TC:(kt + 1) * TC]
                    last_pe = E("tensor", lambda t, o=pmt, l=lhs, r=rhs,
                                st_=(kt == 0), sp=(kt == KI - 1): t.matmul(
                                    o[:], l, r, start=st_, stop=sp),
                                waits=[wl_v, ("ve", st["yz"][kt])] + pmw,
                                inc="pe" if kt == KI - 1 else None)
                xv = E("vector", lambda v, p=pmt, m=m: v.tensor_tensor(
                    t1b[cpar][:, m * TC:(m + 1) * TC],
                    t1b[cpar][:, m * TC:(m + 1) * TC], p[:], ALU.add),
                    waits=[("pe", last_pe)], inc="ve")
                xa_done[m] = xv
                pm_edge[pi] = ("ve", xv)
                ring_release(slot, ("pe", last_pe))
                yield
            tap("xa", c, t1b[cpar][:], [128, MD * TC], F32,
                ("ve", max(xa_done)))
            acc_done = [None] * MD
            for m in range(MD):
                acc_done[m] = E("gpsimd", lambda g,
                                a=x0b[cpar][:, m * TC:(m + 1) * TC],
                                b=t1b[cpar][:, m * TC:(m + 1) * TC]:
                                g.tensor_tensor(a, a, b, ALU.add),
                                waits=[("ve", xa_done[m]), x0_ready[c],
                                       ("se", st["ln1_sq"])], inc="ge")
            yield
            stats(t1b[cpar], MD, False, [[("ve", xa_done[kt])]
                                         for kt in range(MD)])
            rows = ln_rows("rms")
            ht_v = None
            for kt in range(KD):
                ht_v = E("vector", lambda v, kt=kt: v.scalar_tensor_tensor(
                    htb[:, kt * TC:(kt + 1) * TC],
                    t1b[cpar][:, kt * TC:(kt + 1) * TC],
                    c_norm2[:, kt:kt + 1], bcB[:], ALU.mult, ALU.mult),
                    waits=[rows["rho"], ("ve", xa_done[kt])]
                    + edges.get("htb", []), inc="ve")
            st["ht"] = ("ve", ht_v)
            tap("ht", c, htb[:], [128, MD * TC], BF16, ("ve", ht_v))
            yield
            for _ in router_gates(c, htb, ("ve", ht_v), r_rtw, Get, 0.0,
                                  "gates"):
                yield
            get_done = st["gates"]
            for e in range(NE):
                wA = [pbA_edge[0]] if pbA_edge[0] else []
                gb_v = E("tensor", lambda t, e=e: t.matmul(
                    pbA[:], c_sel16[0:8, e * 128:(e + 1) * 128], Get[:],
                    start=True, stop=True), waits=[get_done] + wA, inc="pe")
                gs_v = E("vector", lambda v: v.tensor_copy(Gsb[:], pbA[:]),
                         waits=[("pe", gb_v)] + edges.get("Gsb", []),
                         inc="ve")
                pbA_edge[0] = ("ve", gs_v)
                edges["Gsb"] = [("ve", gs_v)]
                hid_done = [None] * KI
                for hs in range(8):
                    src = w1_bf.rearrange("e (kt p) m -> e p kt m", p=128)[
                        e][:, :, hs * 256:(hs + 1) * 256]
                    slot, wl_v = ring_load(("w1", e), src)
                    last_pe = None
                    for mt_in in range(2):
                        hi = hs * 2 + mt_in
                        pi, pmt, pmw = next_pm()
                        for kt in range(KD):
                            lhs = wring[slot][:, kt * 256 + mt_in * 128:
                                              kt * 256 + mt_in * 128 + 128]
                            rhs = htb[:, kt * TC:(kt + 1) * TC]
                            last_pe = E("tensor", lambda t, o=pmt, l=lhs,
                                        r=rhs, st_=(kt == 0),
                                        sp=(kt == KD - 1): t.matmul(
                                            o[:], l, r, start=st_, stop=sp),
                                        waits=[wl_v, ("ve", ht_v)] + pmw,
                                        inc="pe" if kt == KD - 1 else None)
                        hv = E("scalar", lambda s, p=pmt, hi=hi, e=e:
                               s.activation(
                                   hidb[:, hi * TC:(hi + 1) * TC], p[:],
                                   AF.Identity,
                                   bias=c_b1[:, e * MI + hi:
                                             e * MI + hi + 1]),
                               waits=[("pe", last_pe)]
                               + edges.get("hidb", []), inc="se")
                        hid_done[hi] = hv
                        pm_edge[pi] = ("se", hv)
                    ring_release(slot, ("pe", last_pe))
                    yield
                gelu_v = E("scalar", lambda s: s.activation(
                    hidb[:], hidb[:], AF.Gelu_apprx_tanh),
                    waits=[("se", max(hid_done))], inc="se")
                yield
                for hs in range(8):
                    src = w2_bf.rearrange("e (kt p) m -> e p kt m", p=128)[
                        e][:, :, hs * 128:(hs + 1) * 128]
                    slot, wl_v = ring_load(("w2", e), src)
                    m = hs
                    pi, pmt, pmw = next_pm()
                    last_pe = None
                    for kt in range(KI):
                        lhs = wring[slot][:, kt * 128:kt * 128 + 128]
                        rhs = hidb[:, kt * TC:(kt + 1) * TC]
                        last_pe = E("tensor", lambda t, o=pmt, l=lhs, r=rhs,
                                    st_=(kt == 0), sp=(kt == KI - 1):
                                    t.matmul(o[:], l, r, start=st_, stop=sp),
                                    waits=[wl_v, ("se", gelu_v)] + pmw,
                                    inc="pe" if kt == KI - 1 else None)
                    _, p_t, w_t = next_pool()
                    sv = E("vector", lambda v, a=p_t, p=pmt, m=m, e=e:
                           v.scalar_tensor_tensor(
                               a[:], p[:],
                               c_b2[:, e * MD + m:e * MD + m + 1],
                               Gsb[:], ALU.add, ALU.mult),
                           waits=[("pe", last_pe), ("ve", gs_v)] + w_t,
                           inc="ve")
                    pm_edge[pi] = ("ve", sv)
                    gv2 = E("gpsimd", lambda g, a=p_t,
                            o=x0b[cpar][:, m * TC:(m + 1) * TC]:
                            g.tensor_tensor(o, o, a[:], ALU.add),
                            waits=[("ve", sv), ("ge", acc_done[m])],
                            inc="ge")
                    acc_done[m] = gv2
                    pool_edge[pool.index(p_t)] = ("ge", gv2)
                    ring_release(slot, ("pe", last_pe))
                    yield
            edges["htb"] = [("pe", pb.now("pe"))]
            x1_done = [("ge", v) for v in acc_done]
            tap("x1", c, x0b[cpar][:], [128, MD * TC], F32, x1_done[-1])
            ln2_sq = stats(x0b[cpar], MD, True, [[x1_done[kt]]
                                                 for kt in range(MD)])
            rows2 = ln_rows("ln2")
            b2_v = None
            for kt in range(MD):
                _, p_t, w_t = next_pool()
                E("vector", lambda v, a=p_t,
                  b=x0b[cpar][:, kt * TC:(kt + 1) * TC]: v.tensor_tensor(
                      a[:], b, bcA[:], ALU.subtract),
                  waits=[rows2["mu"], x1_done[kt]] + w_t, inc="ve")
                b2_v = E("vector", lambda v, a=p_t, kt=kt: v.tensor_tensor(
                    htb[:, kt * TC:(kt + 1) * TC], a[:], bcB[:], ALU.mult),
                    waits=[rows2["r"]] + edges.get("htb", []), inc="ve")
                pool_edge[pool.index(p_t)] = ("ve", b2_v)
            tap("b2nd", c, htb[:], [128, MD * TC], BF16, ("ve", b2_v))
            yield
            for _ in router_gates(c, htb, ("ve", b2_v), r_frw, Pet, 1e-9,
                                  "pgates"):
                yield
            pet_done = st["pgates"]
            facc_done = list(x1_done)
            for e in range(NE):
                wA = [pbA_edge[0]] if pbA_edge[0] else []
                pb_v = E("tensor", lambda t, e=e: t.matmul(
                    pbA[:], c_sel16[0:8, e * 128:(e + 1) * 128], Pet[:],
                    start=True, stop=True), waits=[pet_done] + wA, inc="pe")
                gs_v = E("vector", lambda v: v.tensor_copy(Gsb[:], pbA[:]),
                         waits=[("pe", pb_v)] + edges.get("Gsb", []),
                         inc="ve")
                pbA_edge[0] = ("ve", gs_v)
                edges["Gsb"] = [("ve", gs_v)]
                sg_done = [None] * KI
                for hs in range(8):
                    src = gw_bf.rearrange("e (kt p) m -> e p kt m", p=128)[
                        e][:, :, hs * 256:(hs + 1) * 256]
                    slot, wl_v = ring_load(("gw", e), src)
                    last_pe = None
                    for mt_in in range(2):
                        fi = hs * 2 + mt_in
                        pi, pmt, pmw = next_pm()
                        for kt in range(KD):
                            lhs = wring[slot][:, kt * 256 + mt_in * 128:
                                              kt * 256 + mt_in * 128 + 128]
                            rhs = htb[:, kt * TC:(kt + 1) * TC]
                            last_pe = E("tensor", lambda t, o=pmt, l=lhs,
                                        r=rhs, st_=(kt == 0),
                                        sp=(kt == KD - 1): t.matmul(
                                            o[:], l, r, start=st_, stop=sp),
                                        waits=[wl_v, ("ve", b2_v)] + pmw,
                                        inc="pe" if kt == KD - 1 else None)
                        sgv = E("scalar", lambda s, p=pmt, fi=fi, e=e:
                                s.activation(
                                    hidb[:, fi * TC:(fi + 1) * TC], p[:],
                                    AF.Identity,
                                    bias=c_gb[:, e * MI + fi:
                                              e * MI + fi + 1]),
                                waits=[("pe", last_pe)]
                                + edges.get("hidb", []), inc="se")
                        sg_done[fi] = sgv
                        pm_edge[pi] = ("se", sgv)
                    ring_release(slot, ("pe", last_pe))
                    yield
                silu_v = E("scalar", lambda s: s.activation(
                    hidb[:], hidb[:], AF.Silu),
                    waits=[("se", max(sg_done))], inc="se")
                yield
                h2_done = [None] * KI
                for hs in range(8):
                    src = uw_bf.rearrange("e (kt p) m -> e p kt m", p=128)[
                        e][:, :, hs * 256:(hs + 1) * 256]
                    slot, wl_v = ring_load(("uw", e), src)
                    last_pe = None
                    for mt_in in range(2):
                        fi = hs * 2 + mt_in
                        pi, pmt, pmw = next_pm()
                        for kt in range(KD):
                            lhs = wring[slot][:, kt * 256 + mt_in * 128:
                                              kt * 256 + mt_in * 128 + 128]
                            rhs = htb[:, kt * TC:(kt + 1) * TC]
                            last_pe = E("tensor", lambda t, o=pmt, l=lhs,
                                        r=rhs, st_=(kt == 0),
                                        sp=(kt == KD - 1): t.matmul(
                                            o[:], l, r, start=st_, stop=sp),
                                        waits=[wl_v, ("se", silu_v)] + pmw,
                                        inc="pe" if kt == KD - 1 else None)
                        hv = E("vector", lambda v, p=pmt, fi=fi, e=e:
                               v.scalar_tensor_tensor(
                                   hidb[:, fi * TC:(fi + 1) * TC], p[:],
                                   c_ub[:, e * MI + fi:e * MI + fi + 1],
                                   hidb[:, fi * TC:(fi + 1) * TC],
                                   ALU.add, ALU.mult),
                               waits=[("pe", last_pe), ("se", silu_v)],
                               inc="ve")
                        h2_done[fi] = hv
                        pm_edge[pi] = ("ve", hv)
                    ring_release(slot, ("pe", last_pe))
                    yield
                for hs in range(8):
                    src = dw_bf.rearrange("e (kt p) m -> e p kt m", p=128)[
                        e][:, :, hs * 128:(hs + 1) * 128]
                    slot, wl_v = ring_load(("dw", e), src)
                    m = hs
                    pi, pmt, pmw = next_pm()
                    last_pe = None
                    for kt in range(KI):
                        lhs = wring[slot][:, kt * 128:kt * 128 + 128]
                        rhs = hidb[:, kt * TC:(kt + 1) * TC]
                        last_pe = E("tensor", lambda t, o=pmt, l=lhs, r=rhs,
                                    st_=(kt == 0), sp=(kt == KI - 1):
                                    t.matmul(o[:], l, r, start=st_, stop=sp),
                                    waits=[wl_v, ("ve", h2_done[kt])] + pmw,
                                    inc="pe" if kt == KI - 1 else None)
                    _, p_t, w_t = next_pool()
                    sv = E("vector", lambda v, a=p_t, p=pmt, m=m, e=e:
                           v.scalar_tensor_tensor(
                               a[:], p[:],
                               c_db[:, e * MD + m:e * MD + m + 1],
                               Gsb[:], ALU.add, ALU.mult),
                           waits=[("pe", last_pe), ("ve", gs_v)] + w_t,
                           inc="ve")
                    pm_edge[pi] = ("ve", sv)
                    gv2 = E("gpsimd", lambda g, a=p_t,
                            o=x0b[cpar][:, m * TC:(m + 1) * TC]:
                            g.tensor_tensor(o, o, a[:], ALU.add),
                            waits=[("ve", sv), facc_done[m], ("ve", b2_v),
                                   ("se", ln2_sq)], inc="ge")
                    facc_done[m] = ("ge", gv2)
                    pool_edge[pool.index(p_t)] = ("ge", gv2)
                    ring_release(slot, ("pe", last_pe))
                    yield
            edges["hidb"] = [("pe", pb.now("pe"))]
            edges["htb"] = [("pe", pb.now("pe"))]
            out_done = pb.now("ge")
            dst = out.rearrange("(kt p) t -> p kt t", p=128)[
                :, :, c * TC:(c + 1) * TC]
            srcv = x0b[cpar][:].rearrange("p (kt t) -> p kt t", t=TC)
            st_sem = pb.sem(f"sl{cpar}")
            st_v = E("gpsimd", lambda g, d=dst, s=srcv: g.dma_start(d, s),
                     waits=[("ge", out_done), x0_ready[c]], inc=st_sem,
                     inck=16)
            if c + 2 < NCH:
                x0_ready[c + 2] = x0_load(c + 2, waits=[(st_sem, st_v)])
            edges[f"x0_{cpar}"] = [(st_sem, st_v)]
            edges[f"t1_{cpar}"] = [("se", ln2_sq), ("ve", b2_v),
                                   x1_done[-1]]
            edges[f"xc_{cpar}"] = [("pe", pb.now("pe"))]
            edges[Get.name] = [("pe", pb.now("pe"))]
            edges[Pet.name] = [("pe", pb.now("pe"))]
            del front_state[c]
            yield

        gens = {}
        emit_front1(0)
        for it in range(NCH + 1):
            if it < NCH:
                emit_front2(it)
                g = gens.pop(it - 1, None)
                def pull(k, g=g):
                    if g is None or not interleave:
                        return
                    for _ in range(k):
                        if next(g, StopIteration) is StopIteration:
                            break
                emit_scan(it, pull)
                if g is not None:
                    for _ in g:
                        pass
                gens[it] = tail_units(it)
                if it + 1 < NCH:
                    emit_front1(it + 1)
            else:
                for _ in gens.pop(it - 1):
                    pass

        fin = [(s, pb.now(s)) for s in ("sl0", "sl1", "tl") if s in pb.semc]
        E("gpsimd", lambda g: g.nop(), waits=fin)

        with nc.Block() as block:
            @block.sync
            def _(e):
                for f in pb.order["sync"]:
                    f(e)

            @block.tensor
            def _(e):
                for f in pb.order["tensor"]:
                    f(e)

            @block.vector
            def _(e):
                for f in pb.order["vector"]:
                    f(e)

            @block.scalar
            def _(e):
                for f in pb.order["scalar"]:
                    f(e)

            @block.gpsimd
            def _(e):
                for f in pb.order["gpsimd"]:
                    f(e)

    return nc, taps


def _host_inputs(inputs, b):
    f = np.ascontiguousarray
    ip = inputs
    m = {
        "xT": f(np.asarray(ip["L0_attn_x"][b]).T),
        "ipw": f(ip["L0_attn_in_proj_w"]),
        "opw": f(ip["L0_attn_out_proj_w"]),
        "xpw": f(ip["L0_attn_x_proj_w"]),
        "dtw": f(ip["L0_attn_dt_proj_w"]),
        "w1": f(ip["L0_attn_w1"]),
        "w2": f(ip["L0_attn_w2"]),
        "gw": f(np.asarray(ip["L0_ffn_gate_w"])),
        "uw": f(np.asarray(ip["L0_ffn_up_w"])),
        "dw": f(np.asarray(ip["L0_ffn_down_w"])),
        "rtw": f(ip["L0_attn_router_w"]),
        "frw": f(np.asarray(ip["L0_ffn_router_w"]).T),
        "norm1c": f(np.asarray(ip["L0_attn_norm1_w"]).reshape(8, 128).T),
        "norm2c": f(np.asarray(ip["L0_attn_norm2_w"]).reshape(8, 128).T),
        "convw_c": f(np.asarray(ip["L0_attn_conv_w"]).reshape(16, 128, 4)
                     .transpose(1, 0, 2).reshape(128, 64)),
        "convb_c": f(np.asarray(ip["L0_attn_conv_b"]).reshape(16, 128).T),
        "dtb_c": f(np.asarray(ip["L0_attn_dt_proj_b"]).reshape(16, 128).T),
        "Dp_c": f(np.asarray(ip["L0_attn_D_param"]).reshape(16, 128).T),
        "Alog_c": f(np.asarray(ip["L0_attn_A_log"]).reshape(16, 128, 16)
                    .transpose(1, 0, 2).reshape(128, 16 * 16)),
        "b1_c": f(np.asarray(ip["L0_attn_b1"]).reshape(8, 16, 128)
                  .transpose(2, 0, 1).reshape(128, 8 * 16)),
        "b2_c": f(np.asarray(ip["L0_attn_b2"]).reshape(8, 8, 128)
                  .transpose(2, 0, 1).reshape(128, 8 * 8)),
        "gb_c": f(np.asarray(ip["L0_ffn_gate_b"]).reshape(8, 16, 128)
                  .transpose(2, 0, 1).reshape(128, 8 * 16)),
        "ub_c": f(np.asarray(ip["L0_ffn_up_b"]).reshape(8, 16, 128)
                  .transpose(2, 0, 1).reshape(128, 8 * 16)),
        "db_c": f(np.asarray(ip["L0_ffn_down_b"]).reshape(8, 8, 128)
                  .transpose(2, 0, 1).reshape(128, 8 * 8)),
        "ident_c": np.eye(128, dtype=np.float32),
        "misc_c": np.tile(np.array([[1e-5, 1e-6, 1.0, 0.0]], np.float32),
                          (128, 1)),
    }
    import ml_dtypes
    sel = np.zeros((16, 16, 128), np.float32)
    for s in range(16):
        sel[s, s, :] = 1.0
    m["sel16_c"] = f(sel.transpose(1, 0, 2).reshape(16, 16 * 128)
                     .astype(ml_dtypes.bfloat16))
    m["onescol_c"] = np.ones((128, 1), np.float32).astype(ml_dtypes.bfloat16)
    m["onesrow_c"] = np.ones((1, 128), np.float32).astype(ml_dtypes.bfloat16)
    for k in m:
        if m[k].dtype == np.float64:
            m[k] = m[k].astype(np.float32)
    return m


def kernel(**inputs):
    nc, _ = build_program(DEBUG_CHUNKS)
    in_maps = [_host_inputs(inputs, b) for b in range(N_CORES)]
    res = run_bass_kernel_spmd(nc, in_maps, core_ids=list(range(N_CORES)))
    outs = [np.ascontiguousarray(np.asarray(r["out"]).T)
            for r in res.results]
    return np.stack(outs, 0)


# version 04
# speedup vs baseline: 1.2034x; 1.0418x over previous
import contextlib
import numpy as np
import concourse.bass as bass
import concourse.mybir as mybir
from concourse.bass_utils import run_bass_kernel_spmd

F32 = mybir.dt.float32
BF16 = mybir.dt.bfloat16
AF = mybir.ActivationFunctionType
ALU = mybir.AluOpType
AX = mybir.AxisListType

D = 1024
DIN = 2048
T = 4096
TC = 512
NCH = T // TC
NS = 16
DTR = 64
NE = 8
KD = D // 128
KI = DIN // 128
MD = D // 128
MI = DIN // 128

N_CORES = 8
DEBUG_CHUNKS = ()


class PB:

    ENGS = ("tensor", "vector", "scalar", "gpsimd", "sync")

    def __init__(self, nc, stack, self_chain=(), dedup=True):
        self.nc = nc
        self.stack = stack
        self.dedup = dedup
        self.order = {e: [] for e in self.ENGS}
        self.semh = {}
        self.semc = {}
        for name in ("pe", "ve", "se", "ge"):
            self.sem(name)
        self.waited = {e: {} for e in self.ENGS}
        self.self_chain = {"tensor": "pe", "vector": "ve", "scalar": "se",
                           "gpsimd": "ge"}
        self.self_chain = {k: v for k, v in self.self_chain.items()
                           if k in self_chain}

    def sem(self, name):
        if name not in self.semh:
            self.semh[name] = self.stack.enter_context(self.nc.semaphore(name))
            self.semc[name] = 0
        return name

    def emit(self, eng, build, waits=(), inc=None, inck=1):
        chain = self.self_chain.get(eng)
        if chain is not None:
            waits = list(waits) + [(chain, self.semc[chain])]
            if inc is None:
                inc = chain
        w = []
        for (s, v) in waits:
            if v is None or v <= 0:
                continue
            if self.dedup and self.waited[eng].get(s, 0) >= v:
                continue
            if self.waited[eng].get(s, 0) < v:
                self.waited[eng][s] = v
            if not any(h is self.semh[s] and vv >= v for (h, vv) in w):
                w.append((self.semh[s], v))
        val = None
        if inc is not None:
            self.semc[inc] += inck
            val = self.semc[inc]
            hinc = self.semh[inc]
        w = tuple(w)
        if inc is None:
            def thunk(e, _w=w, _b=build):
                for (h, v) in _w:
                    e.wait_ge(h, v)
                _b(e)
        else:
            def thunk(e, _w=w, _b=build, _h=hinc, _k=inck):
                for (h, v) in _w:
                    e.wait_ge(h, v)
                _b(e).then_inc(_h, _k)
        self.order[eng].append(thunk)
        return val

    def now(self, sem):
        return self.semc.get(sem, 0)


def build_program(debug_chunks=(), sim_edges=(), interleave=True):
    nc = bass.Bass("TRN2", target_bir_lowering=False)

    def din(name, shape, dtype=F32):
        return nc.dram_tensor(name, list(shape), dtype,
                              kind="ExternalInput").ap()

    xT = din("xT", [D, T])
    ipw = din("ipw", [D, 2 * DIN])
    opw = din("opw", [DIN, D])
    xpw = din("xpw", [DIN, 96])
    dtw = din("dtw", [DTR, DIN])
    w1 = din("w1", [NE, D, DIN])
    w2 = din("w2", [NE, DIN, D])
    gw = din("gw", [NE, D, DIN])
    uw = din("uw", [NE, D, DIN])
    dw = din("dw", [NE, DIN, D])
    rtw = din("rtw", [D, NE])
    frw = din("frw", [D, NE])
    norm1c = din("norm1c", [128, KD])
    norm2c = din("norm2c", [128, KD])
    convw_c = din("convw_c", [128, MI * 4])
    convb_c = din("convb_c", [128, MI])
    dtb_c = din("dtb_c", [128, MI])
    Dp_c = din("Dp_c", [128, MI])
    Alog_c = din("Alog_c", [128, MI * NS])
    b1_c = din("b1_c", [128, NE * MI])
    b2_c = din("b2_c", [128, NE * MD])
    gb_c = din("gb_c", [128, NE * MI])
    ub_c = din("ub_c", [128, NE * MI])
    db_c = din("db_c", [128, NE * MD])
    sel16_c = din("sel16_c", [16, 16 * 128], BF16)
    ident_c = din("ident_c", [128, 128])
    onescol_c = din("onescol_c", [128, 1], BF16)
    onesrow_c = din("onesrow_c", [1, 128], BF16)
    misc_c = din("misc_c", [128, 4])

    out = nc.dram_tensor("out", [D, T], F32, kind="ExternalOutput").ap()

    ipw_bf = nc.dram_tensor("ipw_bf", [D, 2 * DIN], BF16, kind="Internal").ap()
    opw_bf = nc.dram_tensor("opw_bf", [DIN, D], BF16, kind="Internal").ap()
    w1_bf = nc.dram_tensor("w1_bf", [NE, D, DIN], BF16, kind="Internal").ap()
    w2_bf = nc.dram_tensor("w2_bf", [NE, DIN, D], BF16, kind="Internal").ap()
    gw_bf = nc.dram_tensor("gw_bf", [NE, D, DIN], BF16, kind="Internal").ap()
    uw_bf = nc.dram_tensor("uw_bf", [NE, D, DIN], BF16, kind="Internal").ap()
    dw_bf = nc.dram_tensor("dw_bf", [NE, DIN, D], BF16, kind="Internal").ap()

    taps = {}

    with contextlib.ExitStack() as stack:
        def sb(name, shape, dtype=F32):
            return stack.enter_context(nc.sbuf_tensor(name, list(shape),
                                                      dtype))
        def psum(name, shape, dtype=F32):
            return stack.enter_context(nc.psum_tensor(name, list(shape),
                                                      dtype))

        c_norm1 = sb("c_norm1", [128, KD])
        c_norm2 = sb("c_norm2", [128, KD])
        c_convw = sb("c_convw", [128, MI * 4])
        c_convb = sb("c_convb", [128, MI])
        c_dtb = sb("c_dtb", [128, MI])
        c_Dp = sb("c_Dp", [128, MI])
        c_Alog = sb("c_Alog", [128, MI * NS])
        c_b1 = sb("c_b1", [128, NE * MI])
        c_b2 = sb("c_b2", [128, NE * MD])
        c_gb = sb("c_gb", [128, NE * MI])
        c_ub = sb("c_ub", [128, NE * MI])
        c_db = sb("c_db", [128, NE * MD])
        c_sel16 = sb("c_sel16", [16, 16 * 128], BF16)
        c_ident = sb("c_ident", [128, 128])
        c_onescol = sb("c_onescol", [128, 1], BF16)
        c_onesrow = sb("c_onesrow", [1, 128], BF16)
        c_misc = sb("c_misc", [128, 4])

        r_xpw = sb("r_xpw", [128, KI * 96], BF16)
        r_dtw = sb("r_dtw", [DTR, DIN], BF16)
        r_rtw = sb("r_rtw", [128, KD * NE], BF16)
        r_frw = sb("r_frw", [128, KD * NE], BF16)

        NRING = 3
        wring = [sb(f"wring{i}", [128, 2048], BF16) for i in range(NRING)]

        x0b = [sb(f"x0_{i}", [128, MD * TC]) for i in range(2)]
        t1b = [sb(f"t1_{i}", [128, MD * TC]) for i in range(2)]
        h1b = sb("h1b", [128, MD * TC], BF16)
        htb = sb("htb", [128, MD * TC], BF16)
        zb = sb("zb", [128, MI * TC], BF16)
        xcb = [sb(f"xc_{i}", [128, MI * TC], BF16) for i in range(2)]
        tailb = sb("tailb", [128, MI * 3], BF16)
        dtrb = sb("dtrb", [DTR, TC], BF16)
        Bsb = sb("Bsb", [NS, TC], BF16)
        Csb = sb("Csb", [NS, TC], BF16)
        dtb2 = sb("dtb2", [128, TC])
        wbuf = sb("wbuf", [128, TC])
        dAb = sb("dAb", [128, TC])
        ubuf = sb("ubuf", [128, TC], BF16)
        hbuf = sb("hbuf", [128, TC], BF16)
        ybuf = sb("ybuf", [128, TC])
        carry = sb("carry", [128, MI * NS])
        hidb = sb("hidb", [128, KI * TC], BF16)
        Gsb = sb("Gsb", [128, TC], BF16)
        Get = sb("Get", [NE, TC], BF16)
        Pet = sb("Pet", [NE, TC], BF16)
        Gtb = sb("Gtb", [128, 32])
        RLb = sb("RLb", [NE, TC])
        rsc = sb("rsc", [128, 32])
        strow = sb("strow", [1, TC])
        stbf = sb("stbf", [1, TC], BF16)
        stsrc = sb("stsrc", [128, 2 * TC], BF16)
        bcA = sb("bcA", [128, TC], BF16)
        bcB = sb("bcB", [128, TC], BF16)
        pool = [sb(f"pool{i}", [128, TC]) for i in range(4)]

        pm = [psum(f"pm{i}", [128, TC]) for i in range(4)]
        pbA = psum("pbA", [128, TC])
        pbB = psum("pbB", [128, TC])
        ps6 = psum("ps6", [1, TC])
        ps7 = psum("ps7", [1, TC])

        pb = PB(nc, stack, self_chain=sim_edges, dedup=not sim_edges)
        E = pb.emit

        pm_edge = [None] * 4
        pm_i = [0]
        pbA_edge = [None]
        pbB_edge = [None]
        pool_edge = [None] * len(pool)
        pool_i = [0]
        stats_edge = [None]
        router_edge = [None]
        dA_edge = [None, None]
        dt_edge = [None]
        edges = {}

        def next_pm():
            i = pm_i[0] % 4
            pm_i[0] += 1
            return i, pm[i], ([pm_edge[i]] if pm_edge[i] else [])

        def next_pool():
            i = pool_i[0] % len(pool)
            pool_i[0] += 1
            return i, pool[i], ([pool_edge[i]] if pool_edge[i] else [])

        def gload(dst_ap, src_ap, sem="glc"):
            return E("gpsimd", lambda g, d=dst_ap, s=src_ap: g.dma_start(d, s),
                     inc=pb.sem(sem), inck=16)

        for dst, src in [
            (c_norm1[:], norm1c), (c_norm2[:], norm2c), (c_convw[:], convw_c),
            (c_convb[:], convb_c), (c_dtb[:], dtb_c), (c_Dp[:], Dp_c),
            (c_Alog[:], Alog_c), (c_b1[:], b1_c), (c_b2[:], b2_c),
            (c_gb[:], gb_c), (c_ub[:], ub_c), (c_db[:], db_c),
            (c_sel16[:], sel16_c), (c_ident[:], ident_c),
            (c_onescol[:], onescol_c), (c_onesrow[:], onesrow_c),
            (c_misc[:], misc_c),
        ]:
            gload(dst, src)
        gload(r_xpw[:].rearrange("p (kt m) -> p kt m", m=96),
              xpw.rearrange("(kt p) m -> p kt m", p=128))
        gload(r_dtw[:], dtw)
        gload(r_rtw[:].rearrange("p (kt m) -> p kt m", m=NE),
              rtw.rearrange("(kt p) m -> p kt m", p=128))
        gload(r_frw[:].rearrange("p (kt m) -> p kt m", m=NE),
              frw.rearrange("(kt p) m -> p kt m", p=128))

        def x0_load(c, waits=()):
            dst = x0b[c % 2][:].rearrange("p (kt t) -> p kt t", t=TC)
            src = xT.rearrange("(kt p) t -> p kt t", p=128)[
                :, :, c * TC:(c + 1) * TC]
            sem = pb.sem(f"glx{c % 2}")
            v = E("gpsimd", lambda g, d=dst, s=src: g.dma_start(d, s),
                  waits=waits, inc=sem, inck=16)
            return (sem, v)
        x0_ready = [None] * NCH
        x0_ready[0] = x0_load(0)
        x0_ready[1] = x0_load(1)
        consts_gl = pb.now("glc")

        prep = {}
        def prep_dma(key, dst_ap, src_ap):
            sem = pb.sem(f"gp{len(prep)}")
            v = E("gpsimd", lambda g, d=dst_ap, s=src_ap: g.dma_start(d, s),
                  inc=sem, inck=16)
            prep[key] = (sem, v)
        prep_dma("ipw", ipw_bf, ipw)
        prep_dma("opw", opw_bf, opw)
        for e in range(NE):
            prep_dma(("w1", e), w1_bf[e], w1[e])
            prep_dma(("w2", e), w2_bf[e], w2[e])
        for e in range(NE):
            prep_dma(("gw", e), gw_bf[e], gw[e])
            prep_dma(("uw", e), uw_bf[e], uw[e])
        for e in range(NE):
            prep_dma(("dw", e), dw_bf[e], dw[e])

        vA = E("scalar", lambda s: s.activation(c_Alog[:], c_Alog[:], AF.Exp),
               waits=[("glc", consts_gl)], inc="se")
        vA2 = E("vector", lambda v: v.tensor_scalar(
            c_Alog[:], c_Alog[:], -1.0, None, ALU.mult),
            waits=[("se", vA)], inc="ve")
        E("vector", lambda v: v.memset(carry[:], 0.0), inc="ve")
        vz2 = E("vector", lambda v: v.memset(tailb[:], 0.0), inc="ve")
        edges["tail0"] = ("ve", vz2)

        ring_state = {"n": 0, "slot_edge": [None] * NRING}
        for _i in range(NRING):
            pb.sem(f"wl{_i}")

        def ring_load(key, src_view):
            slot = ring_state["n"] % NRING
            sem = f"wl{slot}"
            waits = []
            if key in prep:
                waits.append(prep[key])
            if ring_state["slot_edge"][slot] is not None:
                waits.append(ring_state["slot_edge"][slot])
                waits.append((sem, pb.now(sem)))
            dst = wring[slot][:].rearrange(
                "p (kt m) -> p kt m", m=src_view.shape[2])
            wl = E("sync", lambda sy, d=dst, s=src_view: sy.dma_start(d, s),
                   waits=waits, inc=sem, inck=16)
            ring_state["n"] += 1
            return slot, (sem, wl)

        def ring_release(slot, edge):
            ring_state["slot_edge"][slot] = edge

        def tap(name, c, src_ap, shape, dtype, wait):
            if c not in debug_chunks:
                return
            key = f"tap_{name}_{c}"
            t = nc.dram_tensor(key, list(shape), dtype,
                               kind="ExternalOutput").ap()
            taps[key] = (name, c)
            v = E("gpsimd", lambda g, d=t, s=src_ap: g.dma_start(d, s),
                  waits=[wait], inc=pb.sem("tl"), inck=16)
            for eng in ("tensor", "vector", "scalar", "gpsimd"):
                E(eng, lambda x: x.nop(), waits=[("tl", v)])

        def stats(xbuf, nkt, need_mean, src_waits):
            sv = None
            for kt in range(nkt):
                xs = xbuf[:, kt * TC:(kt + 1) * TC]
                w_ = list(src_waits[kt]) if src_waits else []
                if stats_edge[0]:
                    w_.append(stats_edge[0])
                cv = None
                if need_mean:
                    cv = E("vector", lambda v, a=xs: v.tensor_copy(
                        stsrc[:, 0:TC], a), waits=w_, inc="ve")
                sv = E("scalar", lambda s, a=xs: s.activation(
                    stsrc[:, TC:2 * TC], a, AF.Square), waits=w_, inc="se")
                st_ = (kt == 0)
                sp = (kt == nkt - 1)
                if need_mean:
                    E("tensor", lambda t, st_=st_, sp=sp: t.matmul(
                        ps6[0:1, :], c_onescol[:, 0:1], stsrc[:, 0:TC],
                        start=st_, stop=sp),
                        waits=[("ve", cv), ("glc", consts_gl)], inc="pe")
                sq_v = E("tensor", lambda t, st_=st_, sp=sp: t.matmul(
                    ps7[0:1, :], c_onescol[:, 0:1], stsrc[:, TC:2 * TC],
                    start=st_, stop=sp),
                    waits=[("se", sv), ("glc", consts_gl)], inc="pe")
                stats_edge[0] = ("pe", sq_v)
            return sv

        stbf_edge = [None]

        def bcast_row_ap(src_row, target, extra_waits):
            w_ = list(extra_waits)
            if stbf_edge[0]:
                w_.append(stbf_edge[0])
            cb = E("vector", lambda v, s=src_row: v.tensor_copy(
                stbf[0:1, :], s), waits=w_, inc="ve")
            bv = E("tensor", lambda t: t.matmul(
                pbA[:], c_onesrow[0:1, :], stbf[0:1, :], start=True,
                stop=True), waits=[("ve", cb)]
                + ([pbA_edge[0]] if pbA_edge[0] else []), inc="pe")
            stbf_edge[0] = ("pe", bv)
            ev = E("vector", lambda v, o=target: v.tensor_copy(o[:], pbA[:]),
                   waits=[("pe", bv)], inc="ve")
            pbA_edge[0] = ("ve", ev)
            return ("ve", ev)

        def bcast_row(src_is_strow_off, target, extra_waits):
            w_ = list(extra_waits)
            if stbf_edge[0]:
                w_.append(stbf_edge[0])
            cb = E("vector", lambda v: v.tensor_copy(
                stbf[0:1, :], strow[0:1, src_is_strow_off:
                                    src_is_strow_off + TC]),
                waits=w_, inc="ve")
            bv = E("tensor", lambda t: t.matmul(
                pbA[:], c_onesrow[0:1, :], stbf[0:1, :], start=True,
                stop=True), waits=[("ve", cb)]
                + ([pbA_edge[0]] if pbA_edge[0] else []), inc="pe")
            stbf_edge[0] = ("pe", bv)
            ev = E("vector", lambda v, o=target: v.tensor_copy(o[:], pbA[:]),
                   waits=[("pe", bv)], inc="ve")
            pbA_edge[0] = ("ve", ev)
            return ("ve", ev)

        def ln_rows(kind):
            out = {}
            prow = prow2 = None
            if kind in ("ln1", "ln2"):
                _, prow, wrow = next_pool()
                _, prow2, wrow2 = next_pool()
                tmp = prow[0:1, 0:TC]
                var = prow2[0:1, 0:TC]
                v1 = E("vector", lambda v: v.tensor_scalar(
                    strow[0:1, 0:TC], ps6[0:1, :], 1.0 / D, None, ALU.mult),
                    waits=[stats_edge[0]], inc="ve")
                E("vector", lambda v, var=var: v.tensor_scalar(
                    var, ps7[0:1, :], 1.0 / D, None,
                    ALU.mult), waits=wrow2, inc="ve")
                v3 = E("vector", lambda v, tmp=tmp: v.tensor_tensor(
                    tmp, strow[0:1, 0:TC],
                    strow[0:1, 0:TC], ALU.mult),
                    waits=[("ve", v1)] + wrow, inc="ve")
                v4 = E("vector", lambda v, tmp=tmp, var=var: v.tensor_tensor(
                    var, var,
                    tmp, ALU.subtract),
                    waits=[("ve", v3)], inc="ve")
                s1 = E("scalar", lambda s, tmp=tmp, var=var: s.activation(
                    tmp, var, AF.Ln,
                    bias=c_misc[0:1, 0:1]), waits=[("ve", v4)], inc="se")
                s2 = E("scalar", lambda s, tmp=tmp: s.activation(
                    tmp, tmp,
                    AF.Exp, scale=-0.5), waits=[("se", s1)], inc="se")
                out["mu"] = bcast_row(0, bcA, [("ve", v4)])
                out["r"] = bcast_row_ap(tmp, bcB, [("se", s2)])
            if kind == "ln1":
                var = prow2[0:1, 0:TC]
                w1v = E("vector", lambda v, var=var: v.tensor_scalar(
                    strow[0:1, 0:TC], var, 1e-5, None,
                    ALU.add), waits=[out["mu"]], inc="ve")
                w2r = E("vector", lambda v: v.reciprocal(
                    strow[0:1, 0:TC], strow[0:1, 0:TC]),
                    waits=[("ve", w1v)], inc="ve")
                w2v = E("vector", lambda v, var=var: v.tensor_tensor(
                    strow[0:1, 0:TC], var,
                    strow[0:1, 0:TC], ALU.mult), waits=[("ve", w2r)],
                    inc="ve")
                s3 = E("scalar", lambda s: s.activation(
                    strow[0:1, 0:TC], strow[0:1, 0:TC], AF.Ln,
                    bias=c_misc[0:1, 1:2]), waits=[("ve", w2v)], inc="se")
                s4 = E("scalar", lambda s: s.activation(
                    strow[0:1, 0:TC], strow[0:1, 0:TC], AF.Exp, scale=-0.5),
                    waits=[("se", s3)], inc="se")
                out["rq_row"] = ("se", s4)
                pool_edge[pool.index(prow)] = out["r"]
                pool_edge[pool.index(prow2)] = ("se", s3)
            elif prow is not None:
                pool_edge[pool.index(prow)] = out["r"]
                pool_edge[pool.index(prow2)] = ("se", s1)
            elif kind == "rms":
                v2 = E("vector", lambda v: v.tensor_scalar(
                    strow[0:1, 0:TC], ps7[0:1, :], 1.0 / D, None, ALU.mult),
                    waits=[stats_edge[0]], inc="ve")
                s1 = E("scalar", lambda s: s.activation(
                    strow[0:1, 0:TC], strow[0:1, 0:TC], AF.Ln,
                    bias=c_misc[0:1, 1:2]), waits=[("ve", v2)], inc="se")
                s2_ = E("scalar", lambda s: s.activation(
                    strow[0:1, 0:TC], strow[0:1, 0:TC], AF.Exp, scale=-0.5),
                    waits=[("se", s1)], inc="se")
                out["rho"] = bcast_row(0, bcB, [("se", s2_)])
            return out

        front_state = {}

        def emit_front1(c):
            st = {}
            cpar = c % 2
            ln1_sq = stats(x0b[cpar], MD, True,
                           [[x0_ready[c]] + edges.get(f"x0_{cpar}", [])
                            for _ in range(MD)])
            st["ln1_sq"] = ln1_sq
            rows = ln_rows("ln1")
            a_v = None
            for kt in range(KD):
                xs = x0b[cpar][:, kt * TC:(kt + 1) * TC]
                t1s = t1b[cpar][:, kt * TC:(kt + 1) * TC]
                E("vector", lambda v, a=t1s, b=xs: v.tensor_tensor(
                    a, b, bcA[:], ALU.subtract),
                    waits=[rows["mu"], x0_ready[c]]
                    + edges.get(f"t1_{cpar}", []), inc="ve")
                a_v = E("vector", lambda v, a=t1s: v.tensor_tensor(
                    a, a, bcB[:], ALU.mult), waits=[rows["r"]], inc="ve")
            rq_done = bcast_row(0, bcB, [rows["rq_row"], ("ve", a_v)])
            h1_v = None
            for kt in range(KD):
                t1s = t1b[cpar][:, kt * TC:(kt + 1) * TC]
                h1s = h1b[:, kt * TC:(kt + 1) * TC]
                h1_v = E("vector", lambda v, o=h1s, a=t1s, kt=kt:
                         v.scalar_tensor_tensor(
                             o, a, c_norm1[:, kt:kt + 1], bcB[:], ALU.mult,
                             ALU.mult),
                         waits=[rq_done] + edges.get("h1b", []),
                         inc="ve")
            st["h1"] = ("ve", h1_v)
            tap("a", c, t1b[cpar][:], [128, MD * TC], F32, ("ve", a_v))
            tap("h1", c, h1b[:], [128, MD * TC], BF16, ("ve", h1_v))
            conv_done = [None] * MI
            last_pe = None
            for hs in range(8):
                src = ipw_bf.rearrange("(kt p) m -> p kt m", p=128)[
                    :, :, hs * 256:(hs + 1) * 256]
                slot, wl_v = ring_load("ipw", src)
                for mt_in in range(2):
                    mi = hs * 2 + mt_in
                    pi, pmt, pmw = next_pm()
                    for kt in range(KD):
                        lhs = wring[slot][:, kt * 256 + mt_in * 128:
                                          kt * 256 + mt_in * 128 + 128]
                        rhs = h1b[:, kt * TC:(kt + 1) * TC]
                        last_pe = E("tensor", lambda t, o=pmt, l=lhs, r=rhs,
                                    st_=(kt == 0), sp=(kt == KD - 1):
                                    t.matmul(o[:], l, r, start=st_, stop=sp),
                                    waits=[wl_v, ("ve", h1_v)] + pmw,
                                    inc="pe" if kt == KD - 1 else None)
                    _, acc, accw = next_pool()
                    w_ = [("pe", last_pe), ("glc", consts_gl),
                          edges["tail0"]] + accw
                    E("vector", lambda v, a=acc, p=pmt, mi=mi:
                      v.tensor_scalar(
                          a[:], p[:], c_convw[:, mi * 4 + 3:mi * 4 + 4],
                          None, ALU.mult), waits=w_, inc="ve")
                    for k in (2, 1, 0):
                        n = 3 - k
                        E("vector", lambda v, a=acc, p=pmt, mi=mi, k=k, n=n:
                          v.scalar_tensor_tensor(
                              a[:, n:TC], p[:, 0:TC - n],
                              c_convw[:, mi * 4 + k:mi * 4 + k + 1],
                              a[:, n:TC], ALU.mult, ALU.add), inc="ve")
                        E("vector", lambda v, a=acc, mi=mi, k=k, n=n:
                          v.scalar_tensor_tensor(
                              a[:, 0:n], tailb[:, mi * 3 + k:mi * 3 + 3],
                              c_convw[:, mi * 4 + k:mi * 4 + k + 1],
                              a[:, 0:n], ALU.mult, ALU.add), inc="ve")
                    tl_v = E("vector", lambda v, p=pmt, mi=mi: v.tensor_copy(
                        tailb[:, mi * 3:mi * 3 + 3], p[:, TC - 3:TC]),
                        inc="ve")
                    cd = E("scalar", lambda s, a=acc, mi=mi: s.activation(
                        xcb[cpar][:, mi * TC:(mi + 1) * TC], a[:], AF.Silu,
                        bias=c_convb[:, mi:mi + 1]),
                        waits=[("ve", tl_v)] + edges.get(f"xc_{cpar}", []),
                        inc="se")
                    conv_done[mi] = cd
                    pm_edge[pi] = ("se", cd)
                    pool_edge[pool.index(acc)] = ("se", cd)
                ring_release(slot, ("pe", last_pe))
            edges["tail0"] = ("ve", pb.now("ve"))
            st["conv"] = conv_done
            front_state[c] = st
            tap("xc", c, xcb[cpar][:], [128, MI * TC], BF16,
                ("se", max(conv_done)))

        def emit_front2(c):
            st = front_state[c]
            cpar = c % 2
            z_done = [None] * MI
            last_pe = None
            for hs in range(8, 16):
                src = ipw_bf.rearrange("(kt p) m -> p kt m", p=128)[
                    :, :, hs * 256:(hs + 1) * 256]
                slot, wl_v = ring_load("ipw", src)
                for mt_in in range(2):
                    mi = (hs - 8) * 2 + mt_in
                    pi, pmt, pmw = next_pm()
                    for kt in range(KD):
                        lhs = wring[slot][:, kt * 256 + mt_in * 128:
                                          kt * 256 + mt_in * 128 + 128]
                        rhs = h1b[:, kt * TC:(kt + 1) * TC]
                        last_pe = E("tensor", lambda t, o=pmt, l=lhs, r=rhs,
                                    st_=(kt == 0), sp=(kt == KD - 1):
                                    t.matmul(o[:], l, r, start=st_, stop=sp),
                                    waits=[wl_v, st["h1"]] + pmw,
                                    inc="pe" if kt == KD - 1 else None)
                    zd = E("scalar", lambda s, p=pmt, mi=mi: s.activation(
                        zb[:, mi * TC:(mi + 1) * TC], p[:], AF.Silu),
                        waits=[("pe", last_pe)] + edges.get("zb", []),
                        inc="se")
                    z_done[mi] = zd
                    pm_edge[pi] = ("se", zd)
                ring_release(slot, ("pe", last_pe))
            st["z"] = z_done
            edges["h1b"] = [("pe", last_pe)]
            tap("z", c, zb[:], [128, MI * TC], BF16, ("se", max(z_done)))

            def xp_part(col0, mc, evict):
                pi, pmt, pmw = next_pm()
                last = None
                for kt in range(KI):
                    lhs = r_xpw[:, kt * 96 + col0:kt * 96 + col0 + mc]
                    rhs = xcb[cpar][:, kt * TC:(kt + 1) * TC]
                    last = E("tensor", lambda t, o=pmt, l=lhs, r=rhs,
                             st_=(kt == 0), sp=(kt == KI - 1), mc=mc:
                             t.matmul(o[0:mc, :], l, r, start=st_, stop=sp),
                             waits=[("se", st["conv"][kt]),
                                    ("glc", consts_gl)] + pmw,
                             inc="pe" if kt == KI - 1 else None)
                ev = evict(pmt, last)
                pm_edge[pi] = ev
                return ev
            st["dtr"] = xp_part(0, DTR, lambda p, l: (
                "se", E("scalar", lambda s, p=p: s.activation(
                    dtrb[:], p[0:DTR, :], AF.Copy),
                    waits=[("pe", l)] + edges.get("dtrb", []), inc="se")))
            st["B"] = xp_part(DTR, NS, lambda p, l: (
                "se", E("scalar", lambda s, p=p: s.activation(
                    Bsb[:], p[0:NS, :], AF.Copy),
                    waits=[("pe", l)] + edges.get("Bsb", []), inc="se")))
            st["C"] = xp_part(DTR + NS, NS, lambda p, l: (
                "se", E("scalar", lambda s, p=p: s.activation(
                    Csb[:], p[0:NS, :], AF.Copy),
                    waits=[("pe", l)] + edges.get("Bsb", []), inc="se")))
            tap("B", c, Bsb[:], [NS, TC], BF16, st["B"])
            tap("C", c, Csb[:], [NS, TC], BF16, st["C"])

        def emit_scan(c, pull):
            st = front_state[c]
            cpar = c % 2
            yz_done = [None] * MI
            for mi in range(MI):
                dts = dtb2[:, 0:TC]
                pi, pmt, pmw = next_pm()
                last = E("tensor", lambda t, o=pmt, mi=mi: t.matmul(
                    o[:], r_dtw[:, mi * 128:(mi + 1) * 128], dtrb[:],
                    start=True, stop=True),
                    waits=[st["dtr"], ("glc", consts_gl)] + pmw, inc="pe")
                _, p_av, w_av = next_pool()
                _, p_m, w_m = next_pool()
                E("vector", lambda v, a=p_av, p=pmt, mi=mi: v.tensor_scalar(
                    a[:], p[:], c_dtb[:, mi:mi + 1], 0.0, ALU.add, ALU.min),
                    waits=[("pe", last)] + w_av + w_m, inc="ve")
                E("vector", lambda v, a=p_m, p=pmt, mi=mi: v.tensor_scalar(
                    a[:], p[:], c_dtb[:, mi:mi + 1], 0.0, ALU.add, ALU.max),
                    inc="ve")
                E("vector", lambda v, a=p_av, b=p_m: v.tensor_tensor(
                    a[:], a[:], b[:], ALU.subtract), inc="ve")
                pm_edge[pi] = ("ve", pb.now("ve"))
                s_e = E("scalar", lambda s, a=p_av: s.activation(
                    a[:], a[:], AF.Exp), waits=[("ve", pb.now("ve"))],
                    inc="se")
                s_l = E("scalar", lambda s, a=p_av: s.activation(
                    a[:], a[:], AF.Ln, bias=c_misc[:, 2:3]),
                    waits=[("se", s_e)], inc="se")
                w_dt = [("se", s_l)]
                if dt_edge[0]:
                    w_dt.append(dt_edge[0])
                dt_v = E("vector", lambda v, o=dts, a=p_m, b=p_av:
                         v.tensor_tensor(o, a[:], b[:], ALU.add),
                         waits=w_dt, inc="ve")
                pool_edge[pool.index(p_av)] = ("ve", dt_v)
                pool_edge[pool.index(p_m)] = ("ve", dt_v)
                if mi == 0:
                    tap("dt0", c, dts, [128, TC], F32, ("ve", dt_v))
                w_v = E("vector", lambda v, o=wbuf, a=dts, mi=mi:
                        v.tensor_tensor(
                            o[:], a, xcb[cpar][:, mi * TC:(mi + 1) * TC],
                            ALU.mult),
                        waits=[("se", st["conv"][mi])], inc="ve")
                y_v = None
                last_dA = None
                for s_i in range(NS):
                    spar = 0
                    dAs = dAb[:, 0:TC]
                    wA = [pbA_edge[0]] if pbA_edge[0] else []
                    wB = [pbB_edge[0]] if pbB_edge[0] else []
                    bA = E("tensor", lambda t, s_i=s_i: t.matmul(
                        pbA[:], c_sel16[:, s_i * 128:(s_i + 1) * 128],
                        Bsb[:], start=True, stop=True),
                        waits=[st["B"]] + wA, inc="pe")
                    bB = E("tensor", lambda t, s_i=s_i: t.matmul(
                        pbB[:], c_sel16[:, s_i * 128:(s_i + 1) * 128],
                        Csb[:], start=True, stop=True),
                        waits=[st["C"]] + wB, inc="pe")
                    w_da = [("ve", dt_v), ("ve", vA2)]
                    if dA_edge[spar]:
                        w_da.append(dA_edge[spar])
                    dA_v = E("scalar", lambda s, o=dAs, i=dts, mi=mi,
                             s_i=s_i: s.activation(
                                 o, i, AF.Exp,
                                 scale=c_Alog[:, mi * NS + s_i:
                                              mi * NS + s_i + 1]),
                             waits=w_da, inc="se")
                    last_dA = dA_v
                    E("vector", lambda v: v.tensor_tensor(
                        ubuf[:], wbuf[:], pbA[:], ALU.mult),
                        waits=[("pe", bA), ("ve", w_v)], inc="ve")
                    idx = mi * NS + s_i
                    sc_v = E("vector", lambda v, a=dAs, idx=idx:
                             v.tensor_tensor_scan(
                                 hbuf[:], a, ubuf[:], carry[:, idx:idx + 1],
                                 ALU.mult, ALU.add),
                             waits=[("se", dA_v)], inc="ve")
                    dA_edge[spar] = ("ve", sc_v)
                    E("vector", lambda v, idx=idx: v.tensor_copy(
                        carry[:, idx:idx + 1], hbuf[:, TC - 1:TC]),
                        waits=[("ve", sc_v)], inc="ve")
                    if s_i == 0:
                        y_v = E("vector", lambda v: v.tensor_tensor(
                            ybuf[:], hbuf[:], pbB[:], ALU.mult),
                            waits=[("pe", bB), ("ve", sc_v)], inc="ve")
                    else:
                        E("vector", lambda v: v.tensor_tensor(
                            hbuf[:], hbuf[:], pbB[:], ALU.mult),
                            waits=[("pe", bB), ("ve", sc_v)], inc="ve")
                        y_v = E("vector", lambda v: v.tensor_tensor(
                            ybuf[:], ybuf[:], hbuf[:], ALU.add), inc="ve")
                    pbA_edge[0] = ("ve", y_v)
                    pbB_edge[0] = ("ve", y_v)
                    pull(2)
                dt_edge[0] = ("se", last_dA)
                if mi == 0:
                    tap("y0", c, ybuf[:], [128, TC], F32, ("ve", y_v))
                _, p_t, w_t = next_pool()
                E("vector", lambda v, a=p_t, mi=mi: v.scalar_tensor_tensor(
                    a[:], xcb[cpar][:, mi * TC:(mi + 1) * TC],
                    c_Dp[:, mi:mi + 1], ybuf[:], ALU.mult, ALU.add),
                    waits=w_t, inc="ve")
                yzv = E("vector", lambda v, a=p_t, mi=mi: v.tensor_tensor(
                    xcb[cpar][:, mi * TC:(mi + 1) * TC], a[:],
                    zb[:, mi * TC:(mi + 1) * TC], ALU.mult),
                    waits=[("se", st["z"][mi])], inc="ve")
                yz_done[mi] = yzv
                pool_edge[pool.index(p_t)] = ("ve", yzv)
            tap("carry", c, carry[:], [128, MI * NS], F32,
                ("ve", pb.now("ve")))
            st["yz"] = yz_done
            edges["zb"] = [("ve", max(yz_done))]
            edges["dtrb"] = [("pe", pb.now("pe"))]
            edges["Bsb"] = [("ve", pb.now("ve"))]
            tap("yz", c, xcb[cpar][:], [128, MI * TC], BF16,
                ("ve", max(yz_done)))

        def router_gates(c, rhs_buf, rhs_done, wtile, out_et, eps, tapname):
            pi, pmt, pmw = next_pm()
            last = None
            for kt in range(KD):
                lhs = wtile[:, kt * NE:(kt + 1) * NE]
                rhs = rhs_buf[:, kt * TC:(kt + 1) * TC]
                last = E("tensor", lambda t, o=pmt, l=lhs, r=rhs,
                         st_=(kt == 0), sp=(kt == KD - 1): t.matmul(
                             o[0:NE, :], l, r, start=st_, stop=sp),
                         waits=[rhs_done, ("glc", consts_gl)] + pmw,
                         inc="pe" if kt == KD - 1 else None)
            w_rl = [("pe", last)]
            if router_edge[0]:
                w_rl.append(router_edge[0])
            rl_v = E("scalar", lambda s, p=pmt: s.activation(
                RLb[:], p[0:NE, :], AF.Copy), waits=w_rl, inc="se")
            pm_edge[pi] = ("se", rl_v)
            yield
            wA = [pbA_edge[0]] if pbA_edge[0] else []
            tp = None
            for ti in range(4):
                tp = E("tensor", lambda t, ti=ti: t.transpose(
                    pbA[:, ti * 8:(ti + 1) * 8],
                    RLb[:, ti * 128:(ti + 1) * 128], c_ident[0:8, 0:8]),
                    waits=[("se", rl_v)] + wA, inc="pe")
            gt_v = E("vector", lambda v: v.tensor_copy(
                Gtb[:], pbA[:, 0:32]), waits=[("pe", tp)], inc="ve")
            pbA_edge[0] = ("ve", gt_v)
            yield
            gv = gt_v
            for ti in range(4):
                g8 = Gtb[:, ti * 8:(ti + 1) * 8]
                m1 = rsc[:, 0:1]
                m2 = rsc[:, 1:2]
                sm = rsc[:, 8:16]
                p2 = rsc[:, 16:24]
                def VS(build, extra=()):
                    nonlocal gv
                    gv = E("vector", build,
                           waits=[("ve", gv)] + list(extra), inc="ve")
                    return gv
                VS(lambda v, g8=g8, m1=m1: v.tensor_reduce(
                    m1, g8, AX.X, ALU.max))
                VS(lambda v, g8=g8, m1=m1, sm=sm: v.tensor_scalar(
                    sm, g8, m1, None, ALU.subtract))
                se_v = E("scalar", lambda s, sm=sm: s.activation(
                    sm, sm, AF.Exp), waits=[("ve", gv)], inc="se")
                VS(lambda v, sm=sm, m1=m1: v.tensor_reduce(
                    m1, sm, AX.X, ALU.add), extra=[("se", se_v)])
                VS(lambda v, m1=m1: v.reciprocal(m1, m1))
                VS(lambda v, g8=g8, sm=sm, m1=m1: v.tensor_scalar(
                    g8, sm, m1, None, ALU.mult))
                VS(lambda v, g8=g8, m1=m1: v.tensor_reduce(
                    m1, g8, AX.X, ALU.max))
                VS(lambda v, g8=g8, m1=m1, sm=sm: v.tensor_scalar(
                    sm, g8, m1, None, ALU.is_ge))
                VS(lambda v, g8=g8, sm=sm, p2=p2: v.tensor_tensor(
                    p2, g8, sm, ALU.mult))
                VS(lambda v, g8=g8, p2=p2: v.tensor_tensor(
                    p2, g8, p2, ALU.subtract))
                VS(lambda v, p2=p2, m2=m2: v.tensor_reduce(
                    m2, p2, AX.X, ALU.max))
                VS(lambda v, p2=p2, m2=m2: v.tensor_scalar(
                    p2, p2, m2, None, ALU.is_ge))
                VS(lambda v, sm=sm, p2=p2: v.tensor_tensor(
                    sm, sm, p2, ALU.add))
                VS(lambda v, m1=m1, m2=m2: v.tensor_tensor(
                    m1, m1, m2, ALU.add))
                if eps:
                    VS(lambda v, m1=m1: v.tensor_scalar(
                        m1, m1, eps, None, ALU.add))
                VS(lambda v, g8=g8, sm=sm: v.tensor_tensor(
                    g8, g8, sm, ALU.mult))
                VS(lambda v, m1=m1: v.reciprocal(m1, m1))
                VS(lambda v, g8=g8, m1=m1: v.tensor_scalar(
                    g8, g8, m1, None, ALU.mult))
            yield
            wB = [pbB_edge[0]] if pbB_edge[0] else []
            tp2 = None
            for ti in range(4):
                tp2 = E("tensor", lambda t, ti=ti: t.transpose(
                    pbB[0:8, ti * 128:(ti + 1) * 128],
                    Gtb[:, ti * 8:(ti + 1) * 8], c_ident[:]),
                    waits=[("ve", gv)] + wB, inc="pe")
            et_v = E("vector", lambda v, o=out_et: v.tensor_copy(
                o[:], pbB[0:8, :]),
                waits=[("pe", tp2)] + edges.get(out_et.name, []), inc="ve")
            pbB_edge[0] = ("ve", et_v)
            router_edge[0] = ("pe", tp2)
            tap(tapname, c, out_et[:], [NE, TC], BF16, ("ve", et_v))
            front_state[c][tapname] = ("ve", et_v)

        def tail_units(c):
            st = front_state[c]
            cpar = c % 2
            xa_done = [None] * MD
            for hs in range(8):
                src = opw_bf.rearrange("(kt p) m -> p kt m", p=128)[
                    :, :, hs * 128:(hs + 1) * 128]
                slot, wl_v = ring_load("opw", src)
                m = hs
                pi, pmt, pmw = next_pm()
                last_pe = None
                for kt in range(KI):
                    lhs = wring[slot][:, kt * 128:kt * 128 + 128]
                    rhs = xcb[cpar][:, kt * TC:(kt + 1) * TC]
                    last_pe = E("tensor", lambda t, o=pmt, l=lhs, r=rhs,
                                st_=(kt == 0), sp=(kt == KI - 1): t.matmul(
                                    o[:], l, r, start=st_, stop=sp),
                                waits=[wl_v, ("ve", st["yz"][kt])] + pmw,
                                inc="pe" if kt == KI - 1 else None)
                xv = E("vector", lambda v, p=pmt, m=m: v.tensor_tensor(
                    t1b[cpar][:, m * TC:(m + 1) * TC],
                    t1b[cpar][:, m * TC:(m + 1) * TC], p[:], ALU.add),
                    waits=[("pe", last_pe)], inc="ve")
                xa_done[m] = xv
                pm_edge[pi] = ("ve", xv)
                ring_release(slot, ("pe", last_pe))
                yield
            tap("xa", c, t1b[cpar][:], [128, MD * TC], F32,
                ("ve", max(xa_done)))
            acc_done = [None] * MD
            for m in range(MD):
                acc_done[m] = E("gpsimd", lambda g,
                                a=x0b[cpar][:, m * TC:(m + 1) * TC],
                                b=t1b[cpar][:, m * TC:(m + 1) * TC]:
                                g.tensor_tensor(a, a, b, ALU.add),
                                waits=[("ve", xa_done[m]), x0_ready[c],
                                       ("se", st["ln1_sq"])], inc="ge")
            yield
            stats(t1b[cpar], MD, False, [[("ve", xa_done[kt])]
                                         for kt in range(MD)])
            rows = ln_rows("rms")
            ht_v = None
            for kt in range(KD):
                ht_v = E("vector", lambda v, kt=kt: v.scalar_tensor_tensor(
                    htb[:, kt * TC:(kt + 1) * TC],
                    t1b[cpar][:, kt * TC:(kt + 1) * TC],
                    c_norm2[:, kt:kt + 1], bcB[:], ALU.mult, ALU.mult),
                    waits=[rows["rho"], ("ve", xa_done[kt])]
                    + edges.get("htb", []), inc="ve")
            st["ht"] = ("ve", ht_v)
            tap("ht", c, htb[:], [128, MD * TC], BF16, ("ve", ht_v))
            yield
            for _ in router_gates(c, htb, ("ve", ht_v), r_rtw, Get, 0.0,
                                  "gates"):
                yield
            get_done = st["gates"]
            for e in range(NE):
                wA = [pbA_edge[0]] if pbA_edge[0] else []
                gb_v = E("tensor", lambda t, e=e: t.matmul(
                    pbA[:], c_sel16[0:8, e * 128:(e + 1) * 128], Get[:],
                    start=True, stop=True), waits=[get_done] + wA, inc="pe")
                gs_v = E("vector", lambda v: v.tensor_copy(Gsb[:], pbA[:]),
                         waits=[("pe", gb_v)] + edges.get("Gsb", []),
                         inc="ve")
                pbA_edge[0] = ("ve", gs_v)
                edges["Gsb"] = [("ve", gs_v)]
                hid_done = [None] * KI
                for hs in range(8):
                    src = w1_bf.rearrange("e (kt p) m -> e p kt m", p=128)[
                        e][:, :, hs * 256:(hs + 1) * 256]
                    slot, wl_v = ring_load(("w1", e), src)
                    last_pe = None
                    for mt_in in range(2):
                        hi = hs * 2 + mt_in
                        pi, pmt, pmw = next_pm()
                        for kt in range(KD):
                            lhs = wring[slot][:, kt * 256 + mt_in * 128:
                                              kt * 256 + mt_in * 128 + 128]
                            rhs = htb[:, kt * TC:(kt + 1) * TC]
                            last_pe = E("tensor", lambda t, o=pmt, l=lhs,
                                        r=rhs, st_=(kt == 0),
                                        sp=(kt == KD - 1): t.matmul(
                                            o[:], l, r, start=st_, stop=sp),
                                        waits=[wl_v, ("ve", ht_v)] + pmw,
                                        inc="pe" if kt == KD - 1 else None)
                        hv = E("scalar", lambda s, p=pmt, hi=hi, e=e:
                               s.activation(
                                   hidb[:, hi * TC:(hi + 1) * TC], p[:],
                                   AF.Identity,
                                   bias=c_b1[:, e * MI + hi:
                                             e * MI + hi + 1]),
                               waits=[("pe", last_pe)]
                               + edges.get("hidb", []), inc="se")
                        hid_done[hi] = hv
                        pm_edge[pi] = ("se", hv)
                    ring_release(slot, ("pe", last_pe))
                    yield
                gelu_v = E("scalar", lambda s: s.activation(
                    hidb[:], hidb[:], AF.Gelu_apprx_tanh),
                    waits=[("se", max(hid_done))], inc="se")
                yield
                for hs in range(8):
                    src = w2_bf.rearrange("e (kt p) m -> e p kt m", p=128)[
                        e][:, :, hs * 128:(hs + 1) * 128]
                    slot, wl_v = ring_load(("w2", e), src)
                    m = hs
                    pi, pmt, pmw = next_pm()
                    last_pe = None
                    for kt in range(KI):
                        lhs = wring[slot][:, kt * 128:kt * 128 + 128]
                        rhs = hidb[:, kt * TC:(kt + 1) * TC]
                        last_pe = E("tensor", lambda t, o=pmt, l=lhs, r=rhs,
                                    st_=(kt == 0), sp=(kt == KI - 1):
                                    t.matmul(o[:], l, r, start=st_, stop=sp),
                                    waits=[wl_v, ("se", gelu_v)] + pmw,
                                    inc="pe" if kt == KI - 1 else None)
                    _, p_t, w_t = next_pool()
                    sv = E("vector", lambda v, a=p_t, p=pmt, m=m, e=e:
                           v.scalar_tensor_tensor(
                               a[:], p[:],
                               c_b2[:, e * MD + m:e * MD + m + 1],
                               Gsb[:], ALU.add, ALU.mult),
                           waits=[("pe", last_pe), ("ve", gs_v)] + w_t,
                           inc="ve")
                    pm_edge[pi] = ("ve", sv)
                    gv2 = E("gpsimd", lambda g, a=p_t,
                            o=x0b[cpar][:, m * TC:(m + 1) * TC]:
                            g.tensor_tensor(o, o, a[:], ALU.add),
                            waits=[("ve", sv), ("ge", acc_done[m])],
                            inc="ge")
                    acc_done[m] = gv2
                    pool_edge[pool.index(p_t)] = ("ge", gv2)
                    ring_release(slot, ("pe", last_pe))
                    yield
            edges["htb"] = [("pe", pb.now("pe"))]
            x1_done = [("ge", v) for v in acc_done]
            tap("x1", c, x0b[cpar][:], [128, MD * TC], F32, x1_done[-1])
            ln2_sq = stats(x0b[cpar], MD, True, [[x1_done[kt]]
                                                 for kt in range(MD)])
            rows2 = ln_rows("ln2")
            b2_v = None
            for kt in range(MD):
                _, p_t, w_t = next_pool()
                E("vector", lambda v, a=p_t,
                  b=x0b[cpar][:, kt * TC:(kt + 1) * TC]: v.tensor_tensor(
                      a[:], b, bcA[:], ALU.subtract),
                  waits=[rows2["mu"], x1_done[kt]] + w_t, inc="ve")
                b2_v = E("vector", lambda v, a=p_t, kt=kt: v.tensor_tensor(
                    htb[:, kt * TC:(kt + 1) * TC], a[:], bcB[:], ALU.mult),
                    waits=[rows2["r"]] + edges.get("htb", []), inc="ve")
                pool_edge[pool.index(p_t)] = ("ve", b2_v)
            tap("b2nd", c, htb[:], [128, MD * TC], BF16, ("ve", b2_v))
            yield
            for _ in router_gates(c, htb, ("ve", b2_v), r_frw, Pet, 1e-9,
                                  "pgates"):
                yield
            pet_done = st["pgates"]
            facc_done = list(x1_done)
            for e in range(NE):
                wA = [pbA_edge[0]] if pbA_edge[0] else []
                pb_v = E("tensor", lambda t, e=e: t.matmul(
                    pbA[:], c_sel16[0:8, e * 128:(e + 1) * 128], Pet[:],
                    start=True, stop=True), waits=[pet_done] + wA, inc="pe")
                gs_v = E("vector", lambda v: v.tensor_copy(Gsb[:], pbA[:]),
                         waits=[("pe", pb_v)] + edges.get("Gsb", []),
                         inc="ve")
                pbA_edge[0] = ("ve", gs_v)
                edges["Gsb"] = [("ve", gs_v)]
                sg_done = [None] * KI
                for hs in range(8):
                    src = gw_bf.rearrange("e (kt p) m -> e p kt m", p=128)[
                        e][:, :, hs * 256:(hs + 1) * 256]
                    slot, wl_v = ring_load(("gw", e), src)
                    last_pe = None
                    for mt_in in range(2):
                        fi = hs * 2 + mt_in
                        pi, pmt, pmw = next_pm()
                        for kt in range(KD):
                            lhs = wring[slot][:, kt * 256 + mt_in * 128:
                                              kt * 256 + mt_in * 128 + 128]
                            rhs = htb[:, kt * TC:(kt + 1) * TC]
                            last_pe = E("tensor", lambda t, o=pmt, l=lhs,
                                        r=rhs, st_=(kt == 0),
                                        sp=(kt == KD - 1): t.matmul(
                                            o[:], l, r, start=st_, stop=sp),
                                        waits=[wl_v, ("ve", b2_v)] + pmw,
                                        inc="pe" if kt == KD - 1 else None)
                        sgv = E("scalar", lambda s, p=pmt, fi=fi, e=e:
                                s.activation(
                                    hidb[:, fi * TC:(fi + 1) * TC], p[:],
                                    AF.Identity,
                                    bias=c_gb[:, e * MI + fi:
                                              e * MI + fi + 1]),
                                waits=[("pe", last_pe)]
                                + edges.get("hidb", []), inc="se")
                        sg_done[fi] = sgv
                        pm_edge[pi] = ("se", sgv)
                    ring_release(slot, ("pe", last_pe))
                    yield
                silu_v = E("scalar", lambda s: s.activation(
                    hidb[:], hidb[:], AF.Silu),
                    waits=[("se", max(sg_done))], inc="se")
                yield
                h2_done = [None] * KI
                for hs in range(8):
                    src = uw_bf.rearrange("e (kt p) m -> e p kt m", p=128)[
                        e][:, :, hs * 256:(hs + 1) * 256]
                    slot, wl_v = ring_load(("uw", e), src)
                    last_pe = None
                    for mt_in in range(2):
                        fi = hs * 2 + mt_in
                        pi, pmt, pmw = next_pm()
                        for kt in range(KD):
                            lhs = wring[slot][:, kt * 256 + mt_in * 128:
                                              kt * 256 + mt_in * 128 + 128]
                            rhs = htb[:, kt * TC:(kt + 1) * TC]
                            last_pe = E("tensor", lambda t, o=pmt, l=lhs,
                                        r=rhs, st_=(kt == 0),
                                        sp=(kt == KD - 1): t.matmul(
                                            o[:], l, r, start=st_, stop=sp),
                                        waits=[wl_v, ("se", silu_v)] + pmw,
                                        inc="pe" if kt == KD - 1 else None)
                        hv = E("vector", lambda v, p=pmt, fi=fi, e=e:
                               v.scalar_tensor_tensor(
                                   hidb[:, fi * TC:(fi + 1) * TC], p[:],
                                   c_ub[:, e * MI + fi:e * MI + fi + 1],
                                   hidb[:, fi * TC:(fi + 1) * TC],
                                   ALU.add, ALU.mult),
                               waits=[("pe", last_pe), ("se", silu_v)],
                               inc="ve")
                        h2_done[fi] = hv
                        pm_edge[pi] = ("ve", hv)
                    ring_release(slot, ("pe", last_pe))
                    yield
                for hs in range(8):
                    src = dw_bf.rearrange("e (kt p) m -> e p kt m", p=128)[
                        e][:, :, hs * 128:(hs + 1) * 128]
                    slot, wl_v = ring_load(("dw", e), src)
                    m = hs
                    pi, pmt, pmw = next_pm()
                    last_pe = None
                    for kt in range(KI):
                        lhs = wring[slot][:, kt * 128:kt * 128 + 128]
                        rhs = hidb[:, kt * TC:(kt + 1) * TC]
                        last_pe = E("tensor", lambda t, o=pmt, l=lhs, r=rhs,
                                    st_=(kt == 0), sp=(kt == KI - 1):
                                    t.matmul(o[:], l, r, start=st_, stop=sp),
                                    waits=[wl_v, ("ve", h2_done[kt])] + pmw,
                                    inc="pe" if kt == KI - 1 else None)
                    _, p_t, w_t = next_pool()
                    sv = E("vector", lambda v, a=p_t, p=pmt, m=m, e=e:
                           v.scalar_tensor_tensor(
                               a[:], p[:],
                               c_db[:, e * MD + m:e * MD + m + 1],
                               Gsb[:], ALU.add, ALU.mult),
                           waits=[("pe", last_pe), ("ve", gs_v)] + w_t,
                           inc="ve")
                    pm_edge[pi] = ("ve", sv)
                    gv2 = E("gpsimd", lambda g, a=p_t,
                            o=x0b[cpar][:, m * TC:(m + 1) * TC]:
                            g.tensor_tensor(o, o, a[:], ALU.add),
                            waits=[("ve", sv), facc_done[m], ("ve", b2_v),
                                   ("se", ln2_sq)], inc="ge")
                    facc_done[m] = ("ge", gv2)
                    pool_edge[pool.index(p_t)] = ("ge", gv2)
                    ring_release(slot, ("pe", last_pe))
                    yield
            edges["hidb"] = [("pe", pb.now("pe"))]
            edges["htb"] = [("pe", pb.now("pe"))]
            out_done = pb.now("ge")
            dst = out.rearrange("(kt p) t -> p kt t", p=128)[
                :, :, c * TC:(c + 1) * TC]
            srcv = x0b[cpar][:].rearrange("p (kt t) -> p kt t", t=TC)
            st_sem = pb.sem(f"sl{cpar}")
            st_v = E("gpsimd", lambda g, d=dst, s=srcv: g.dma_start(d, s),
                     waits=[("ge", out_done), x0_ready[c]], inc=st_sem,
                     inck=16)
            if c + 2 < NCH:
                x0_ready[c + 2] = x0_load(c + 2, waits=[(st_sem, st_v)])
            edges[f"x0_{cpar}"] = [(st_sem, st_v)]
            edges[f"t1_{cpar}"] = [("se", ln2_sq), ("ve", b2_v),
                                   x1_done[-1]]
            edges[f"xc_{cpar}"] = [("pe", pb.now("pe"))]
            edges[Get.name] = [("pe", pb.now("pe"))]
            edges[Pet.name] = [("pe", pb.now("pe"))]
            del front_state[c]
            yield

        gens = {}
        emit_front1(0)
        for it in range(NCH + 1):
            if it < NCH:
                emit_front2(it)
                g = gens.pop(it - 1, None)
                def pull(k, g=g):
                    if g is None or not interleave:
                        return
                    for _ in range(k):
                        if next(g, StopIteration) is StopIteration:
                            break
                emit_scan(it, pull)
                if g is not None:
                    for _ in g:
                        pass
                gens[it] = tail_units(it)
                if it + 1 < NCH:
                    emit_front1(it + 1)
            else:
                for _ in gens.pop(it - 1):
                    pass

        fin = [(s, pb.now(s)) for s in ("sl0", "sl1", "tl") if s in pb.semc]
        E("gpsimd", lambda g: g.nop(), waits=fin)

        with nc.Block() as block:
            @block.sync
            def _(e):
                for f in pb.order["sync"]:
                    f(e)

            @block.tensor
            def _(e):
                for f in pb.order["tensor"]:
                    f(e)

            @block.vector
            def _(e):
                for f in pb.order["vector"]:
                    f(e)

            @block.scalar
            def _(e):
                for f in pb.order["scalar"]:
                    f(e)

            @block.gpsimd
            def _(e):
                for f in pb.order["gpsimd"]:
                    f(e)

    return nc, taps


def _host_inputs(inputs, b):
    f = np.ascontiguousarray
    ip = inputs
    m = {
        "xT": f(np.asarray(ip["L0_attn_x"][b]).T),
        "ipw": f(ip["L0_attn_in_proj_w"]),
        "opw": f(ip["L0_attn_out_proj_w"]),
        "xpw": f(ip["L0_attn_x_proj_w"]),
        "dtw": f(ip["L0_attn_dt_proj_w"]),
        "w1": f(ip["L0_attn_w1"]),
        "w2": f(ip["L0_attn_w2"]),
        "gw": f(np.asarray(ip["L0_ffn_gate_w"])),
        "uw": f(np.asarray(ip["L0_ffn_up_w"])),
        "dw": f(np.asarray(ip["L0_ffn_down_w"])),
        "rtw": f(ip["L0_attn_router_w"]),
        "frw": f(np.asarray(ip["L0_ffn_router_w"]).T),
        "norm1c": f(np.asarray(ip["L0_attn_norm1_w"]).reshape(8, 128).T),
        "norm2c": f(np.asarray(ip["L0_attn_norm2_w"]).reshape(8, 128).T),
        "convw_c": f(np.asarray(ip["L0_attn_conv_w"]).reshape(16, 128, 4)
                     .transpose(1, 0, 2).reshape(128, 64)),
        "convb_c": f(np.asarray(ip["L0_attn_conv_b"]).reshape(16, 128).T),
        "dtb_c": f(np.asarray(ip["L0_attn_dt_proj_b"]).reshape(16, 128).T),
        "Dp_c": f(np.asarray(ip["L0_attn_D_param"]).reshape(16, 128).T),
        "Alog_c": f(np.asarray(ip["L0_attn_A_log"]).reshape(16, 128, 16)
                    .transpose(1, 0, 2).reshape(128, 16 * 16)),
        "b1_c": f(np.asarray(ip["L0_attn_b1"]).reshape(8, 16, 128)
                  .transpose(2, 0, 1).reshape(128, 8 * 16)),
        "b2_c": f(np.asarray(ip["L0_attn_b2"]).reshape(8, 8, 128)
                  .transpose(2, 0, 1).reshape(128, 8 * 8)),
        "gb_c": f(np.asarray(ip["L0_ffn_gate_b"]).reshape(8, 16, 128)
                  .transpose(2, 0, 1).reshape(128, 8 * 16)),
        "ub_c": f(np.asarray(ip["L0_ffn_up_b"]).reshape(8, 16, 128)
                  .transpose(2, 0, 1).reshape(128, 8 * 16)),
        "db_c": f(np.asarray(ip["L0_ffn_down_b"]).reshape(8, 8, 128)
                  .transpose(2, 0, 1).reshape(128, 8 * 8)),
        "ident_c": np.eye(128, dtype=np.float32),
        "misc_c": np.tile(np.array([[1e-5, 1e-6, 1.0, 0.0]], np.float32),
                          (128, 1)),
    }
    import ml_dtypes
    sel = np.zeros((16, 16, 128), np.float32)
    for s in range(16):
        sel[s, s, :] = 1.0
    m["sel16_c"] = f(sel.transpose(1, 0, 2).reshape(16, 16 * 128)
                     .astype(ml_dtypes.bfloat16))
    m["onescol_c"] = np.ones((128, 1), np.float32).astype(ml_dtypes.bfloat16)
    m["onesrow_c"] = np.ones((1, 128), np.float32).astype(ml_dtypes.bfloat16)
    for k in m:
        if m[k].dtype == np.float64:
            m[k] = m[k].astype(np.float32)
    return m


def kernel(**inputs):
    nc, _ = build_program(DEBUG_CHUNKS)
    in_maps = [_host_inputs(inputs, b) for b in range(N_CORES)]
    res = run_bass_kernel_spmd(nc, in_maps, core_ids=list(range(N_CORES)))
    outs = [np.ascontiguousarray(np.asarray(r["out"]).T)
            for r in res.results]
    return np.stack(outs, 0)


# version 05
# speedup vs baseline: 1.2874x; 1.0698x over previous
import contextlib
import numpy as np
import concourse.bass as bass
import concourse.mybir as mybir
from concourse.bass_utils import run_bass_kernel_spmd

F32 = mybir.dt.float32
BF16 = mybir.dt.bfloat16
AF = mybir.ActivationFunctionType
ALU = mybir.AluOpType
AX = mybir.AxisListType

D = 1024
DIN = 2048
T = 4096
TC = 512
NCH = T // TC
NS = 16
DTR = 64
NE = 8
KD = D // 128
KI = DIN // 128
MD = D // 128
MI = DIN // 128

N_CORES = 8
DEBUG_CHUNKS = ()


class PB:

    ENGS = ("tensor", "vector", "scalar", "gpsimd", "sync")

    def __init__(self, nc, stack, self_chain=(), dedup=True):
        self.nc = nc
        self.stack = stack
        self.dedup = dedup
        self.order = {e: [] for e in self.ENGS}
        self.semh = {}
        self.semc = {}
        for name in ("pe", "ve", "se", "ge"):
            self.sem(name)
        self.waited = {e: {} for e in self.ENGS}
        self.self_chain = {"tensor": "pe", "vector": "ve", "scalar": "se",
                           "gpsimd": "ge"}
        self.self_chain = {k: v for k, v in self.self_chain.items()
                           if k in self_chain}

    def sem(self, name):
        if name not in self.semh:
            self.semh[name] = self.stack.enter_context(self.nc.semaphore(name))
            self.semc[name] = 0
        return name

    def emit(self, eng, build, waits=(), inc=None, inck=1):
        chain = self.self_chain.get(eng)
        if chain is not None:
            waits = list(waits) + [(chain, self.semc[chain])]
            if inc is None:
                inc = chain
        w = []
        for (s, v) in waits:
            if v is None or v <= 0:
                continue
            if self.dedup and self.waited[eng].get(s, 0) >= v:
                continue
            if self.waited[eng].get(s, 0) < v:
                self.waited[eng][s] = v
            if not any(h is self.semh[s] and vv >= v for (h, vv) in w):
                w.append((self.semh[s], v))
        val = None
        if inc is not None:
            self.semc[inc] += inck
            val = self.semc[inc]
            hinc = self.semh[inc]
        w = tuple(w)
        if inc is None:
            def thunk(e, _w=w, _b=build):
                for (h, v) in _w:
                    e.wait_ge(h, v)
                _b(e)
        else:
            def thunk(e, _w=w, _b=build, _h=hinc, _k=inck):
                for (h, v) in _w:
                    e.wait_ge(h, v)
                _b(e).then_inc(_h, _k)
        self.order[eng].append(thunk)
        return val

    def now(self, sem):
        return self.semc.get(sem, 0)


def build_program(debug_chunks=(), sim_edges=(), interleave=True):
    nc = bass.Bass("TRN2", target_bir_lowering=False)

    def din(name, shape, dtype=F32):
        return nc.dram_tensor(name, list(shape), dtype,
                              kind="ExternalInput").ap()

    xT = din("xT", [D, T])
    ipw = din("ipw", [D, 2 * DIN])
    opw = din("opw", [DIN, D])
    xpw = din("xpw", [DIN, 96])
    dtw = din("dtw", [DTR, DIN])
    w1 = din("w1", [NE, D, DIN])
    w2 = din("w2", [NE, DIN, D])
    gw = din("gw", [NE, D, DIN])
    uw = din("uw", [NE, D, DIN])
    dw = din("dw", [NE, DIN, D])
    rtw = din("rtw", [D, NE])
    frw = din("frw", [D, NE])
    norm1c = din("norm1c", [128, KD])
    norm2c = din("norm2c", [128, KD])
    convw_c = din("convw_c", [128, MI * 4])
    convb_c = din("convb_c", [128, MI])
    dtb_c = din("dtb_c", [128, MI])
    Dp_c = din("Dp_c", [128, MI])
    Alog_c = din("Alog_c", [128, MI * NS])
    b1_c = din("b1_c", [128, NE * MI])
    b2_c = din("b2_c", [128, NE * MD])
    gb_c = din("gb_c", [128, NE * MI])
    ub_c = din("ub_c", [128, NE * MI])
    db_c = din("db_c", [128, NE * MD])
    sel16_c = din("sel16_c", [16, 16 * 128], BF16)
    ident_c = din("ident_c", [128, 128])
    onescol_c = din("onescol_c", [128, 1], BF16)
    onesrow_c = din("onesrow_c", [1, 128], BF16)
    misc_c = din("misc_c", [128, 4])

    out = nc.dram_tensor("out", [D, T], F32, kind="ExternalOutput").ap()

    ipw_bf = nc.dram_tensor("ipw_bf", [D, 2 * DIN], BF16, kind="Internal").ap()
    opw_bf = nc.dram_tensor("opw_bf", [DIN, D], BF16, kind="Internal").ap()
    w1_bf = nc.dram_tensor("w1_bf", [NE, D, DIN], BF16, kind="Internal").ap()
    w2_bf = nc.dram_tensor("w2_bf", [NE, DIN, D], BF16, kind="Internal").ap()
    gw_bf = nc.dram_tensor("gw_bf", [NE, D, DIN], BF16, kind="Internal").ap()
    uw_bf = nc.dram_tensor("uw_bf", [NE, D, DIN], BF16, kind="Internal").ap()
    dw_bf = nc.dram_tensor("dw_bf", [NE, DIN, D], BF16, kind="Internal").ap()

    taps = {}

    with contextlib.ExitStack() as stack:
        def sb(name, shape, dtype=F32):
            return stack.enter_context(nc.sbuf_tensor(name, list(shape),
                                                      dtype))
        def psum(name, shape, dtype=F32):
            return stack.enter_context(nc.psum_tensor(name, list(shape),
                                                      dtype))

        c_norm1 = sb("c_norm1", [128, KD])
        c_norm2 = sb("c_norm2", [128, KD])
        c_convw = sb("c_convw", [128, MI * 4])
        c_convb = sb("c_convb", [128, MI])
        c_dtb = sb("c_dtb", [128, MI])
        c_Dp = sb("c_Dp", [128, MI])
        c_Alog = sb("c_Alog", [128, MI * NS])
        c_b1 = sb("c_b1", [128, NE * MI])
        c_b2 = sb("c_b2", [128, NE * MD])
        c_gb = sb("c_gb", [128, NE * MI])
        c_ub = sb("c_ub", [128, NE * MI])
        c_db = sb("c_db", [128, NE * MD])
        c_sel16 = sb("c_sel16", [16, 16 * 128], BF16)
        c_ident = sb("c_ident", [128, 128])
        c_onescol = sb("c_onescol", [128, 1], BF16)
        c_onesrow = sb("c_onesrow", [1, 128], BF16)
        c_misc = sb("c_misc", [128, 4])

        r_xpw = sb("r_xpw", [128, KI * 96], BF16)
        r_dtw = sb("r_dtw", [DTR, DIN], BF16)
        r_rtw = sb("r_rtw", [128, KD * NE], BF16)
        r_frw = sb("r_frw", [128, KD * NE], BF16)

        NRING = 3
        wring = [sb(f"wring{i}", [128, 2048], BF16) for i in range(NRING)]

        x0b = [sb(f"x0_{i}", [128, MD * TC]) for i in range(2)]
        t1b = [sb(f"t1_{i}", [128, MD * TC]) for i in range(2)]
        h1b = sb("h1b", [128, MD * TC], BF16)
        htb = sb("htb", [128, MD * TC], BF16)
        zb = sb("zb", [128, MI * TC], BF16)
        xcb = [sb(f"xc_{i}", [128, MI * TC], BF16) for i in range(2)]
        tailb = sb("tailb", [128, MI * 3], BF16)
        dtrb = sb("dtrb", [DTR, TC], BF16)
        Bsb = sb("Bsb", [NS, TC], BF16)
        Csb = sb("Csb", [NS, TC], BF16)
        dtb2 = sb("dtb2", [128, TC])
        wbuf = sb("wbuf", [128, TC])
        dAb = sb("dAb", [128, TC])
        ubuf = sb("ubuf", [128, TC], BF16)
        hbuf = sb("hbuf", [128, TC], BF16)
        ybuf = sb("ybuf", [128, TC])
        carry = sb("carry", [128, MI * NS])
        hidb = sb("hidb", [128, KI * TC], BF16)
        Gsb = sb("Gsb", [128, TC], BF16)
        Get = sb("Get", [NE, TC], BF16)
        Pet = sb("Pet", [NE, TC], BF16)
        Gtb = sb("Gtb", [128, 32])
        RLb = sb("RLb", [NE, TC])
        rsc = sb("rsc", [128, 32])
        strow = sb("strow", [1, TC])
        stbf = sb("stbf", [1, TC], BF16)
        stsrc = sb("stsrc", [128, 2 * TC], BF16)
        bcA = sb("bcA", [128, TC], BF16)
        bcB = sb("bcB", [128, TC], BF16)
        pool = [sb(f"pool{i}", [128, TC]) for i in range(4)]

        pm = [psum(f"pm{i}", [128, TC]) for i in range(6)]
        pbA = psum("pbA", [128, TC])
        pbB = psum("pbB", [128, TC])

        pb = PB(nc, stack, self_chain=sim_edges, dedup=not sim_edges)
        E = pb.emit

        pm_edge = [None] * 6
        pm_i = [0]
        pbA_edge = [None]
        pbB_edge = [None]
        pool_edge = [None] * len(pool)
        pool_i = [0]
        stats_edge = [None]
        router_edge = [None]
        dA_edge = [None, None]
        dt_edge = [None]
        edges = {}

        def next_pm():
            i = pm_i[0] % 6
            pm_i[0] += 1
            return i, pm[i], ([pm_edge[i]] if pm_edge[i] else [])

        def next_pool():
            i = pool_i[0] % len(pool)
            pool_i[0] += 1
            return i, pool[i], ([pool_edge[i]] if pool_edge[i] else [])

        def gload(dst_ap, src_ap, sem="glc"):
            return E("gpsimd", lambda g, d=dst_ap, s=src_ap: g.dma_start(d, s),
                     inc=pb.sem(sem), inck=16)

        for dst, src in [
            (c_norm1[:], norm1c), (c_norm2[:], norm2c), (c_convw[:], convw_c),
            (c_convb[:], convb_c), (c_dtb[:], dtb_c), (c_Dp[:], Dp_c),
            (c_Alog[:], Alog_c), (c_b1[:], b1_c), (c_b2[:], b2_c),
            (c_gb[:], gb_c), (c_ub[:], ub_c), (c_db[:], db_c),
            (c_sel16[:], sel16_c), (c_ident[:], ident_c),
            (c_onescol[:], onescol_c), (c_onesrow[:], onesrow_c),
            (c_misc[:], misc_c),
        ]:
            gload(dst, src)
        gload(r_xpw[:].rearrange("p (kt m) -> p kt m", m=96),
              xpw.rearrange("(kt p) m -> p kt m", p=128))
        gload(r_dtw[:], dtw)
        gload(r_rtw[:].rearrange("p (kt m) -> p kt m", m=NE),
              rtw.rearrange("(kt p) m -> p kt m", p=128))
        gload(r_frw[:].rearrange("p (kt m) -> p kt m", m=NE),
              frw.rearrange("(kt p) m -> p kt m", p=128))

        def x0_load(c, waits=()):
            dst = x0b[c % 2][:].rearrange("p (kt t) -> p kt t", t=TC)
            src = xT.rearrange("(kt p) t -> p kt t", p=128)[
                :, :, c * TC:(c + 1) * TC]
            sem = pb.sem(f"glx{c % 2}")
            v = E("gpsimd", lambda g, d=dst, s=src: g.dma_start(d, s),
                  waits=waits, inc=sem, inck=16)
            return (sem, v)
        x0_ready = [None] * NCH
        x0_ready[0] = x0_load(0)
        x0_ready[1] = x0_load(1)
        consts_gl = pb.now("glc")

        prep = {}
        def prep_dma(key, dst_ap, src_ap):
            sem = pb.sem(f"gp{len(prep)}")
            v = E("gpsimd", lambda g, d=dst_ap, s=src_ap: g.dma_start(d, s),
                  inc=sem, inck=16)
            prep[key] = (sem, v)
        prep_dma("ipw", ipw_bf, ipw)
        prep_dma("opw", opw_bf, opw)
        for e in range(NE):
            prep_dma(("w1", e), w1_bf[e], w1[e])
            prep_dma(("w2", e), w2_bf[e], w2[e])
        for e in range(NE):
            prep_dma(("gw", e), gw_bf[e], gw[e])
            prep_dma(("uw", e), uw_bf[e], uw[e])
        for e in range(NE):
            prep_dma(("dw", e), dw_bf[e], dw[e])

        vA = E("scalar", lambda s: s.activation(c_Alog[:], c_Alog[:], AF.Exp),
               waits=[("glc", consts_gl)], inc="se")
        vA2 = E("vector", lambda v: v.tensor_scalar(
            c_Alog[:], c_Alog[:], -1.0, None, ALU.mult),
            waits=[("se", vA)], inc="ve")
        E("vector", lambda v: v.memset(carry[:], 0.0), inc="ve")
        vz2 = E("vector", lambda v: v.memset(tailb[:], 0.0), inc="ve")
        edges["tail0"] = ("ve", vz2)

        ring_state = {"n": 0, "slot_edge": [None] * NRING}
        for _i in range(NRING):
            pb.sem(f"wl{_i}")

        def ring_load(key, src_view):
            slot = ring_state["n"] % NRING
            sem = f"wl{slot}"
            waits = []
            if key in prep:
                waits.append(prep[key])
            if ring_state["slot_edge"][slot] is not None:
                waits.append(ring_state["slot_edge"][slot])
                waits.append((sem, pb.now(sem)))
            dst = wring[slot][:].rearrange(
                "p (kt m) -> p kt m", m=src_view.shape[2])
            wl = E("sync", lambda sy, d=dst, s=src_view: sy.dma_start(d, s),
                   waits=waits, inc=sem, inck=16)
            ring_state["n"] += 1
            return slot, (sem, wl)

        def ring_release(slot, edge):
            ring_state["slot_edge"][slot] = edge

        def tap(name, c, src_ap, shape, dtype, wait):
            if c not in debug_chunks:
                return
            key = f"tap_{name}_{c}"
            t = nc.dram_tensor(key, list(shape), dtype,
                               kind="ExternalOutput").ap()
            taps[key] = (name, c)
            v = E("gpsimd", lambda g, d=t, s=src_ap: g.dma_start(d, s),
                  waits=[wait], inc=pb.sem("tl"), inck=16)
            for eng in ("tensor", "vector", "scalar", "gpsimd"):
                E(eng, lambda x: x.nop(), waits=[("tl", v)])

        stats_banks = {}

        def stats(xbuf, nkt, need_mean, src_waits):
            sv = None
            i6, p6, w6 = next_pm()
            i7, p7, w7 = next_pm()
            stats_banks["s1"], stats_banks["s2"] = p6, p7
            stats_banks["i1"], stats_banks["i2"] = i6, i7
            for kt in range(nkt):
                xs = xbuf[:, kt * TC:(kt + 1) * TC]
                w_ = list(src_waits[kt]) if src_waits else []
                if stats_edge[0]:
                    w_.append(stats_edge[0])
                cv = None
                if need_mean:
                    cv = E("vector", lambda v, a=xs: v.tensor_copy(
                        stsrc[:, 0:TC], a), waits=w_, inc="ve")
                sv = E("scalar", lambda s, a=xs: s.activation(
                    stsrc[:, TC:2 * TC], a, AF.Square), waits=w_, inc="se")
                st_ = (kt == 0)
                sp = (kt == nkt - 1)
                if need_mean:
                    E("tensor", lambda t, st_=st_, sp=sp, p6=p6: t.matmul(
                        p6[0:1, :], c_onescol[:, 0:1], stsrc[:, 0:TC],
                        start=st_, stop=sp),
                        waits=[("ve", cv), ("glc", consts_gl)] + w6,
                        inc="pe")
                sq_v = E("tensor", lambda t, st_=st_, sp=sp, p7=p7: t.matmul(
                    p7[0:1, :], c_onescol[:, 0:1], stsrc[:, TC:2 * TC],
                    start=st_, stop=sp),
                    waits=[("se", sv), ("glc", consts_gl)] + w7, inc="pe")
                stats_edge[0] = ("pe", sq_v)
                w6 = w7 = []
            return sv

        stbf_edge = [None]

        def bcast_row_ap(src_row, target, extra_waits):
            w_ = list(extra_waits)
            if stbf_edge[0]:
                w_.append(stbf_edge[0])
            cb = E("vector", lambda v, s=src_row: v.tensor_copy(
                stbf[0:1, :], s), waits=w_, inc="ve")
            bv = E("tensor", lambda t: t.matmul(
                pbA[:], c_onesrow[0:1, :], stbf[0:1, :], start=True,
                stop=True), waits=[("ve", cb)]
                + ([pbA_edge[0]] if pbA_edge[0] else []), inc="pe")
            stbf_edge[0] = ("pe", bv)
            ev = E("vector", lambda v, o=target: v.tensor_copy(o[:], pbA[:]),
                   waits=[("pe", bv)], inc="ve")
            pbA_edge[0] = ("ve", ev)
            return ("ve", ev)

        def bcast_row(src_is_strow_off, target, extra_waits):
            w_ = list(extra_waits)
            if stbf_edge[0]:
                w_.append(stbf_edge[0])
            cb = E("vector", lambda v: v.tensor_copy(
                stbf[0:1, :], strow[0:1, src_is_strow_off:
                                    src_is_strow_off + TC]),
                waits=w_, inc="ve")
            bv = E("tensor", lambda t: t.matmul(
                pbA[:], c_onesrow[0:1, :], stbf[0:1, :], start=True,
                stop=True), waits=[("ve", cb)]
                + ([pbA_edge[0]] if pbA_edge[0] else []), inc="pe")
            stbf_edge[0] = ("pe", bv)
            ev = E("vector", lambda v, o=target: v.tensor_copy(o[:], pbA[:]),
                   waits=[("pe", bv)], inc="ve")
            pbA_edge[0] = ("ve", ev)
            return ("ve", ev)

        def ln_rows(kind):
            out = {}
            prow = prow2 = None
            if kind in ("ln1", "ln2"):
                _, prow, wrow = next_pool()
                _, prow2, wrow2 = next_pool()
                tmp = prow[0:1, 0:TC]
                var = prow2[0:1, 0:TC]
                p6, p7 = stats_banks["s1"], stats_banks["s2"]
                v1 = E("vector", lambda v, p6=p6: v.tensor_scalar(
                    strow[0:1, 0:TC], p6[0:1, :], 1.0 / D, None, ALU.mult),
                    waits=[stats_edge[0]], inc="ve")
                v2b = E("vector", lambda v, var=var, p7=p7: v.tensor_scalar(
                    var, p7[0:1, :], 1.0 / D, None,
                    ALU.mult), waits=wrow2, inc="ve")
                pm_edge[stats_banks["i1"]] = ("ve", v1)
                pm_edge[stats_banks["i2"]] = ("ve", v2b)
                v3 = E("vector", lambda v, tmp=tmp: v.tensor_tensor(
                    tmp, strow[0:1, 0:TC],
                    strow[0:1, 0:TC], ALU.mult),
                    waits=[("ve", v1)] + wrow, inc="ve")
                v4 = E("vector", lambda v, tmp=tmp, var=var: v.tensor_tensor(
                    var, var,
                    tmp, ALU.subtract),
                    waits=[("ve", v3)], inc="ve")
                s1 = E("scalar", lambda s, tmp=tmp, var=var: s.activation(
                    tmp, var, AF.Ln,
                    bias=c_misc[0:1, 0:1]), waits=[("ve", v4)], inc="se")
                s2 = E("scalar", lambda s, tmp=tmp: s.activation(
                    tmp, tmp,
                    AF.Exp, scale=-0.5), waits=[("se", s1)], inc="se")
                out["mu"] = bcast_row(0, bcA, [("ve", v4)])
                out["r"] = bcast_row_ap(tmp, bcB, [("se", s2)])
            if kind == "ln1":
                var = prow2[0:1, 0:TC]
                w1v = E("vector", lambda v, var=var: v.tensor_scalar(
                    strow[0:1, 0:TC], var, 1e-5, None,
                    ALU.add), waits=[out["mu"]], inc="ve")
                w2r = E("vector", lambda v: v.reciprocal(
                    strow[0:1, 0:TC], strow[0:1, 0:TC]),
                    waits=[("ve", w1v)], inc="ve")
                w2v = E("vector", lambda v, var=var: v.tensor_tensor(
                    strow[0:1, 0:TC], var,
                    strow[0:1, 0:TC], ALU.mult), waits=[("ve", w2r)],
                    inc="ve")
                s3 = E("scalar", lambda s: s.activation(
                    strow[0:1, 0:TC], strow[0:1, 0:TC], AF.Ln,
                    bias=c_misc[0:1, 1:2]), waits=[("ve", w2v)], inc="se")
                s4 = E("scalar", lambda s: s.activation(
                    strow[0:1, 0:TC], strow[0:1, 0:TC], AF.Exp, scale=-0.5),
                    waits=[("se", s3)], inc="se")
                out["rq_row"] = ("se", s4)
                pool_edge[pool.index(prow)] = out["r"]
                pool_edge[pool.index(prow2)] = ("se", s3)
            elif prow is not None:
                pool_edge[pool.index(prow)] = out["r"]
                pool_edge[pool.index(prow2)] = ("se", s1)
            elif kind == "rms":
                p7 = stats_banks["s2"]
                v2 = E("vector", lambda v, p7=p7: v.tensor_scalar(
                    strow[0:1, 0:TC], p7[0:1, :], 1.0 / D, None, ALU.mult),
                    waits=[stats_edge[0]], inc="ve")
                pm_edge[stats_banks["i1"]] = stats_edge[0]
                pm_edge[stats_banks["i2"]] = ("ve", v2)
                s1 = E("scalar", lambda s: s.activation(
                    strow[0:1, 0:TC], strow[0:1, 0:TC], AF.Ln,
                    bias=c_misc[0:1, 1:2]), waits=[("ve", v2)], inc="se")
                s2_ = E("scalar", lambda s: s.activation(
                    strow[0:1, 0:TC], strow[0:1, 0:TC], AF.Exp, scale=-0.5),
                    waits=[("se", s1)], inc="se")
                out["rho"] = bcast_row(0, bcB, [("se", s2_)])
            return out

        front_state = {}

        def emit_front1(c):
            st = {}
            cpar = c % 2
            ln1_sq = stats(x0b[cpar], MD, True,
                           [[x0_ready[c]] + edges.get(f"x0_{cpar}", [])
                            for _ in range(MD)])
            st["ln1_sq"] = ln1_sq
            rows = ln_rows("ln1")
            a_v = None
            for kt in range(KD):
                xs = x0b[cpar][:, kt * TC:(kt + 1) * TC]
                t1s = t1b[cpar][:, kt * TC:(kt + 1) * TC]
                E("vector", lambda v, a=t1s, b=xs: v.tensor_tensor(
                    a, b, bcA[:], ALU.subtract),
                    waits=[rows["mu"], x0_ready[c]]
                    + edges.get(f"t1_{cpar}", []), inc="ve")
                a_v = E("vector", lambda v, a=t1s: v.tensor_tensor(
                    a, a, bcB[:], ALU.mult), waits=[rows["r"]], inc="ve")
            rq_done = bcast_row(0, bcB, [rows["rq_row"], ("ve", a_v)])
            h1_v = None
            for kt in range(KD):
                t1s = t1b[cpar][:, kt * TC:(kt + 1) * TC]
                h1s = h1b[:, kt * TC:(kt + 1) * TC]
                h1_v = E("vector", lambda v, o=h1s, a=t1s, kt=kt:
                         v.scalar_tensor_tensor(
                             o, a, c_norm1[:, kt:kt + 1], bcB[:], ALU.mult,
                             ALU.mult),
                         waits=[rq_done] + edges.get("h1b", []),
                         inc="ve")
            st["h1"] = ("ve", h1_v)
            tap("a", c, t1b[cpar][:], [128, MD * TC], F32, ("ve", a_v))
            tap("h1", c, h1b[:], [128, MD * TC], BF16, ("ve", h1_v))
            conv_done = [None] * MI
            last_pe = None
            for hs in range(8):
                src = ipw_bf.rearrange("(kt p) m -> p kt m", p=128)[
                    :, :, hs * 256:(hs + 1) * 256]
                slot, wl_v = ring_load("ipw", src)
                for mt_in in range(2):
                    mi = hs * 2 + mt_in
                    pi, pmt, pmw = next_pm()
                    for kt in range(KD):
                        lhs = wring[slot][:, kt * 256 + mt_in * 128:
                                          kt * 256 + mt_in * 128 + 128]
                        rhs = h1b[:, kt * TC:(kt + 1) * TC]
                        last_pe = E("tensor", lambda t, o=pmt, l=lhs, r=rhs,
                                    st_=(kt == 0), sp=(kt == KD - 1):
                                    t.matmul(o[:], l, r, start=st_, stop=sp),
                                    waits=[wl_v, ("ve", h1_v)] + pmw,
                                    inc="pe" if kt == KD - 1 else None)
                    _, acc, accw = next_pool()
                    w_ = [("pe", last_pe), ("glc", consts_gl),
                          edges["tail0"]] + accw
                    E("vector", lambda v, a=acc, p=pmt, mi=mi:
                      v.tensor_scalar(
                          a[:], p[:], c_convw[:, mi * 4 + 3:mi * 4 + 4],
                          None, ALU.mult), waits=w_, inc="ve")
                    for k in (2, 1, 0):
                        n = 3 - k
                        E("vector", lambda v, a=acc, p=pmt, mi=mi, k=k, n=n:
                          v.scalar_tensor_tensor(
                              a[:, n:TC], p[:, 0:TC - n],
                              c_convw[:, mi * 4 + k:mi * 4 + k + 1],
                              a[:, n:TC], ALU.mult, ALU.add), inc="ve")
                        E("vector", lambda v, a=acc, mi=mi, k=k, n=n:
                          v.scalar_tensor_tensor(
                              a[:, 0:n], tailb[:, mi * 3 + k:mi * 3 + 3],
                              c_convw[:, mi * 4 + k:mi * 4 + k + 1],
                              a[:, 0:n], ALU.mult, ALU.add), inc="ve")
                    tl_v = E("vector", lambda v, p=pmt, mi=mi: v.tensor_copy(
                        tailb[:, mi * 3:mi * 3 + 3], p[:, TC - 3:TC]),
                        inc="ve")
                    cd = E("scalar", lambda s, a=acc, mi=mi: s.activation(
                        xcb[cpar][:, mi * TC:(mi + 1) * TC], a[:], AF.Silu,
                        bias=c_convb[:, mi:mi + 1]),
                        waits=[("ve", tl_v)] + edges.get(f"xc_{cpar}", []),
                        inc="se")
                    conv_done[mi] = cd
                    pm_edge[pi] = ("se", cd)
                    pool_edge[pool.index(acc)] = ("se", cd)
                ring_release(slot, ("pe", last_pe))
            edges["tail0"] = ("ve", pb.now("ve"))
            st["conv"] = conv_done
            front_state[c] = st
            tap("xc", c, xcb[cpar][:], [128, MI * TC], BF16,
                ("se", max(conv_done)))

        def emit_front2(c):
            st = front_state[c]
            cpar = c % 2
            z_done = [None] * MI
            last_pe = None
            for hs in range(8, 16):
                src = ipw_bf.rearrange("(kt p) m -> p kt m", p=128)[
                    :, :, hs * 256:(hs + 1) * 256]
                slot, wl_v = ring_load("ipw", src)
                for mt_in in range(2):
                    mi = (hs - 8) * 2 + mt_in
                    pi, pmt, pmw = next_pm()
                    for kt in range(KD):
                        lhs = wring[slot][:, kt * 256 + mt_in * 128:
                                          kt * 256 + mt_in * 128 + 128]
                        rhs = h1b[:, kt * TC:(kt + 1) * TC]
                        last_pe = E("tensor", lambda t, o=pmt, l=lhs, r=rhs,
                                    st_=(kt == 0), sp=(kt == KD - 1):
                                    t.matmul(o[:], l, r, start=st_, stop=sp),
                                    waits=[wl_v, st["h1"]] + pmw,
                                    inc="pe" if kt == KD - 1 else None)
                    zd = E("scalar", lambda s, p=pmt, mi=mi: s.activation(
                        zb[:, mi * TC:(mi + 1) * TC], p[:], AF.Silu),
                        waits=[("pe", last_pe)] + edges.get("zb", []),
                        inc="se")
                    z_done[mi] = zd
                    pm_edge[pi] = ("se", zd)
                ring_release(slot, ("pe", last_pe))
            st["z"] = z_done
            edges["h1b"] = [("pe", last_pe)]
            tap("z", c, zb[:], [128, MI * TC], BF16, ("se", max(z_done)))

            def xp_part(col0, mc, evict):
                pi, pmt, pmw = next_pm()
                last = None
                for kt in range(KI):
                    lhs = r_xpw[:, kt * 96 + col0:kt * 96 + col0 + mc]
                    rhs = xcb[cpar][:, kt * TC:(kt + 1) * TC]
                    last = E("tensor", lambda t, o=pmt, l=lhs, r=rhs,
                             st_=(kt == 0), sp=(kt == KI - 1), mc=mc:
                             t.matmul(o[0:mc, :], l, r, start=st_, stop=sp),
                             waits=[("se", st["conv"][kt]),
                                    ("glc", consts_gl)] + pmw,
                             inc="pe" if kt == KI - 1 else None)
                ev = evict(pmt, last)
                pm_edge[pi] = ev
                return ev
            st["dtr"] = xp_part(0, DTR, lambda p, l: (
                "se", E("scalar", lambda s, p=p: s.activation(
                    dtrb[:], p[0:DTR, :], AF.Copy),
                    waits=[("pe", l)] + edges.get("dtrb", []), inc="se")))
            st["B"] = xp_part(DTR, NS, lambda p, l: (
                "se", E("scalar", lambda s, p=p: s.activation(
                    Bsb[:], p[0:NS, :], AF.Copy),
                    waits=[("pe", l)] + edges.get("Bsb", []), inc="se")))
            st["C"] = xp_part(DTR + NS, NS, lambda p, l: (
                "se", E("scalar", lambda s, p=p: s.activation(
                    Csb[:], p[0:NS, :], AF.Copy),
                    waits=[("pe", l)] + edges.get("Bsb", []), inc="se")))
            tap("B", c, Bsb[:], [NS, TC], BF16, st["B"])
            tap("C", c, Csb[:], [NS, TC], BF16, st["C"])

        def emit_scan(c, pull):
            st = front_state[c]
            cpar = c % 2
            yz_done = [None] * MI
            for mi in range(MI):
                dts = dtb2[:, 0:TC]
                pi, pmt, pmw = next_pm()
                last = E("tensor", lambda t, o=pmt, mi=mi: t.matmul(
                    o[:], r_dtw[:, mi * 128:(mi + 1) * 128], dtrb[:],
                    start=True, stop=True),
                    waits=[st["dtr"], ("glc", consts_gl)] + pmw, inc="pe")
                _, p_av, w_av = next_pool()
                _, p_m, w_m = next_pool()
                E("vector", lambda v, a=p_av, p=pmt, mi=mi: v.tensor_scalar(
                    a[:], p[:], c_dtb[:, mi:mi + 1], 0.0, ALU.add, ALU.min),
                    waits=[("pe", last)] + w_av + w_m, inc="ve")
                E("vector", lambda v, a=p_m, p=pmt, mi=mi: v.tensor_scalar(
                    a[:], p[:], c_dtb[:, mi:mi + 1], 0.0, ALU.add, ALU.max),
                    inc="ve")
                E("vector", lambda v, a=p_av, b=p_m: v.tensor_tensor(
                    a[:], a[:], b[:], ALU.subtract), inc="ve")
                pm_edge[pi] = ("ve", pb.now("ve"))
                s_e = E("scalar", lambda s, a=p_av: s.activation(
                    a[:], a[:], AF.Exp), waits=[("ve", pb.now("ve"))],
                    inc="se")
                s_l = E("scalar", lambda s, a=p_av: s.activation(
                    a[:], a[:], AF.Ln, bias=c_misc[:, 2:3]),
                    waits=[("se", s_e)], inc="se")
                w_dt = [("se", s_l)]
                if dt_edge[0]:
                    w_dt.append(dt_edge[0])
                dt_v = E("vector", lambda v, o=dts, a=p_m, b=p_av:
                         v.tensor_tensor(o, a[:], b[:], ALU.add),
                         waits=w_dt, inc="ve")
                pool_edge[pool.index(p_av)] = ("ve", dt_v)
                pool_edge[pool.index(p_m)] = ("ve", dt_v)
                if mi == 0:
                    tap("dt0", c, dts, [128, TC], F32, ("ve", dt_v))
                w_v = E("vector", lambda v, o=wbuf, a=dts, mi=mi:
                        v.tensor_tensor(
                            o[:], a, xcb[cpar][:, mi * TC:(mi + 1) * TC],
                            ALU.mult),
                        waits=[("se", st["conv"][mi])], inc="ve")
                y_v = None
                last_dA = None
                for s_i in range(NS):
                    spar = 0
                    dAs = dAb[:, 0:TC]
                    wA = [pbA_edge[0]] if pbA_edge[0] else []
                    wB = [pbB_edge[0]] if pbB_edge[0] else []
                    bA = E("tensor", lambda t, s_i=s_i: t.matmul(
                        pbA[:], c_sel16[:, s_i * 128:(s_i + 1) * 128],
                        Bsb[:], start=True, stop=True),
                        waits=[st["B"]] + wA, inc="pe")
                    bB = E("tensor", lambda t, s_i=s_i: t.matmul(
                        pbB[:], c_sel16[:, s_i * 128:(s_i + 1) * 128],
                        Csb[:], start=True, stop=True),
                        waits=[st["C"]] + wB, inc="pe")
                    w_da = [("ve", dt_v), ("ve", vA2)]
                    if dA_edge[spar]:
                        w_da.append(dA_edge[spar])
                    dA_v = E("scalar", lambda s, o=dAs, i=dts, mi=mi,
                             s_i=s_i: s.activation(
                                 o, i, AF.Exp,
                                 scale=c_Alog[:, mi * NS + s_i:
                                              mi * NS + s_i + 1]),
                             waits=w_da, inc="se")
                    last_dA = dA_v
                    E("vector", lambda v: v.tensor_tensor(
                        ubuf[:], wbuf[:], pbA[:], ALU.mult),
                        waits=[("pe", bA), ("ve", w_v)], inc="ve")
                    idx = mi * NS + s_i
                    sc_v = E("vector", lambda v, a=dAs, idx=idx:
                             v.tensor_tensor_scan(
                                 hbuf[:], a, ubuf[:], carry[:, idx:idx + 1],
                                 ALU.mult, ALU.add),
                             waits=[("se", dA_v)], inc="ve")
                    dA_edge[spar] = ("ve", sc_v)
                    E("vector", lambda v, idx=idx: v.tensor_copy(
                        carry[:, idx:idx + 1], hbuf[:, TC - 1:TC]),
                        waits=[("ve", sc_v)], inc="ve")
                    if s_i == 0:
                        y_v = E("vector", lambda v: v.tensor_tensor(
                            ybuf[:], hbuf[:], pbB[:], ALU.mult),
                            waits=[("pe", bB), ("ve", sc_v)], inc="ve")
                    else:
                        E("vector", lambda v: v.tensor_tensor(
                            hbuf[:], hbuf[:], pbB[:], ALU.mult),
                            waits=[("pe", bB), ("ve", sc_v)], inc="ve")
                        y_v = E("vector", lambda v: v.tensor_tensor(
                            ybuf[:], ybuf[:], hbuf[:], ALU.add), inc="ve")
                    pbA_edge[0] = ("ve", y_v)
                    pbB_edge[0] = ("ve", y_v)
                    pull(2)
                dt_edge[0] = ("se", last_dA)
                if mi == 0:
                    tap("y0", c, ybuf[:], [128, TC], F32, ("ve", y_v))
                _, p_t, w_t = next_pool()
                E("vector", lambda v, a=p_t, mi=mi: v.scalar_tensor_tensor(
                    a[:], xcb[cpar][:, mi * TC:(mi + 1) * TC],
                    c_Dp[:, mi:mi + 1], ybuf[:], ALU.mult, ALU.add),
                    waits=w_t, inc="ve")
                yzv = E("vector", lambda v, a=p_t, mi=mi: v.tensor_tensor(
                    xcb[cpar][:, mi * TC:(mi + 1) * TC], a[:],
                    zb[:, mi * TC:(mi + 1) * TC], ALU.mult),
                    waits=[("se", st["z"][mi])], inc="ve")
                yz_done[mi] = yzv
                pool_edge[pool.index(p_t)] = ("ve", yzv)
            tap("carry", c, carry[:], [128, MI * NS], F32,
                ("ve", pb.now("ve")))
            st["yz"] = yz_done
            edges["zb"] = [("ve", max(yz_done))]
            edges["dtrb"] = [("pe", pb.now("pe"))]
            edges["Bsb"] = [("ve", pb.now("ve"))]
            tap("yz", c, xcb[cpar][:], [128, MI * TC], BF16,
                ("ve", max(yz_done)))

        def router_gates(c, rhs_buf, rhs_done, wtile, out_et, eps, tapname):
            pi, pmt, pmw = next_pm()
            last = None
            for kt in range(KD):
                lhs = wtile[:, kt * NE:(kt + 1) * NE]
                rhs = rhs_buf[:, kt * TC:(kt + 1) * TC]
                last = E("tensor", lambda t, o=pmt, l=lhs, r=rhs,
                         st_=(kt == 0), sp=(kt == KD - 1): t.matmul(
                             o[0:NE, :], l, r, start=st_, stop=sp),
                         waits=[rhs_done, ("glc", consts_gl)] + pmw,
                         inc="pe" if kt == KD - 1 else None)
            w_rl = [("pe", last)]
            if router_edge[0]:
                w_rl.append(router_edge[0])
            rl_v = E("scalar", lambda s, p=pmt: s.activation(
                RLb[:], p[0:NE, :], AF.Copy), waits=w_rl, inc="se")
            pm_edge[pi] = ("se", rl_v)
            yield
            wA = [pbA_edge[0]] if pbA_edge[0] else []
            tp = None
            for ti in range(4):
                tp = E("tensor", lambda t, ti=ti: t.transpose(
                    pbA[:, ti * 8:(ti + 1) * 8],
                    RLb[:, ti * 128:(ti + 1) * 128], c_ident[0:8, 0:8]),
                    waits=[("se", rl_v)] + wA, inc="pe")
            gt_v = E("vector", lambda v: v.tensor_copy(
                Gtb[:], pbA[:, 0:32]), waits=[("pe", tp)], inc="ve")
            pbA_edge[0] = ("ve", gt_v)
            yield
            gv = gt_v
            for ti in range(4):
                g8 = Gtb[:, ti * 8:(ti + 1) * 8]
                m1 = rsc[:, 0:1]
                m2 = rsc[:, 1:2]
                sm = rsc[:, 8:16]
                p2 = rsc[:, 16:24]
                def VS(build, extra=()):
                    nonlocal gv
                    gv = E("vector", build,
                           waits=[("ve", gv)] + list(extra), inc="ve")
                    return gv
                VS(lambda v, g8=g8, m1=m1: v.tensor_reduce(
                    m1, g8, AX.X, ALU.max))
                VS(lambda v, g8=g8, m1=m1, sm=sm: v.tensor_scalar(
                    sm, g8, m1, None, ALU.subtract))
                se_v = E("scalar", lambda s, sm=sm: s.activation(
                    sm, sm, AF.Exp), waits=[("ve", gv)], inc="se")
                VS(lambda v, sm=sm, m1=m1: v.tensor_reduce(
                    m1, sm, AX.X, ALU.add), extra=[("se", se_v)])
                VS(lambda v, m1=m1: v.reciprocal(m1, m1))
                VS(lambda v, g8=g8, sm=sm, m1=m1: v.tensor_scalar(
                    g8, sm, m1, None, ALU.mult))
                VS(lambda v, g8=g8, m1=m1: v.tensor_reduce(
                    m1, g8, AX.X, ALU.max))
                VS(lambda v, g8=g8, m1=m1, sm=sm: v.tensor_scalar(
                    sm, g8, m1, None, ALU.is_ge))
                VS(lambda v, g8=g8, sm=sm, p2=p2: v.tensor_tensor(
                    p2, g8, sm, ALU.mult))
                VS(lambda v, g8=g8, p2=p2: v.tensor_tensor(
                    p2, g8, p2, ALU.subtract))
                VS(lambda v, p2=p2, m2=m2: v.tensor_reduce(
                    m2, p2, AX.X, ALU.max))
                VS(lambda v, p2=p2, m2=m2: v.tensor_scalar(
                    p2, p2, m2, None, ALU.is_ge))
                VS(lambda v, sm=sm, p2=p2: v.tensor_tensor(
                    sm, sm, p2, ALU.add))
                VS(lambda v, m1=m1, m2=m2: v.tensor_tensor(
                    m1, m1, m2, ALU.add))
                if eps:
                    VS(lambda v, m1=m1: v.tensor_scalar(
                        m1, m1, eps, None, ALU.add))
                VS(lambda v, g8=g8, sm=sm: v.tensor_tensor(
                    g8, g8, sm, ALU.mult))
                VS(lambda v, m1=m1: v.reciprocal(m1, m1))
                VS(lambda v, g8=g8, m1=m1: v.tensor_scalar(
                    g8, g8, m1, None, ALU.mult))
            yield
            wB = [pbB_edge[0]] if pbB_edge[0] else []
            tp2 = None
            for ti in range(4):
                tp2 = E("tensor", lambda t, ti=ti: t.transpose(
                    pbB[0:8, ti * 128:(ti + 1) * 128],
                    Gtb[:, ti * 8:(ti + 1) * 8], c_ident[:]),
                    waits=[("ve", gv)] + wB, inc="pe")
            et_v = E("vector", lambda v, o=out_et: v.tensor_copy(
                o[:], pbB[0:8, :]),
                waits=[("pe", tp2)] + edges.get(out_et.name, []), inc="ve")
            pbB_edge[0] = ("ve", et_v)
            router_edge[0] = ("pe", tp2)
            tap(tapname, c, out_et[:], [NE, TC], BF16, ("ve", et_v))
            front_state[c][tapname] = ("ve", et_v)

        def tail_units(c):
            st = front_state[c]
            cpar = c % 2
            xa_done = [None] * MD
            for hs in range(8):
                src = opw_bf.rearrange("(kt p) m -> p kt m", p=128)[
                    :, :, hs * 128:(hs + 1) * 128]
                slot, wl_v = ring_load("opw", src)
                m = hs
                pi, pmt, pmw = next_pm()
                last_pe = None
                for kt in range(KI):
                    lhs = wring[slot][:, kt * 128:kt * 128 + 128]
                    rhs = xcb[cpar][:, kt * TC:(kt + 1) * TC]
                    last_pe = E("tensor", lambda t, o=pmt, l=lhs, r=rhs,
                                st_=(kt == 0), sp=(kt == KI - 1): t.matmul(
                                    o[:], l, r, start=st_, stop=sp),
                                waits=[wl_v, ("ve", st["yz"][kt])] + pmw,
                                inc="pe" if kt == KI - 1 else None)
                xv = E("vector", lambda v, p=pmt, m=m: v.tensor_tensor(
                    t1b[cpar][:, m * TC:(m + 1) * TC],
                    t1b[cpar][:, m * TC:(m + 1) * TC], p[:], ALU.add),
                    waits=[("pe", last_pe)], inc="ve")
                xa_done[m] = xv
                pm_edge[pi] = ("ve", xv)
                ring_release(slot, ("pe", last_pe))
                yield
            tap("xa", c, t1b[cpar][:], [128, MD * TC], F32,
                ("ve", max(xa_done)))
            acc_done = [None] * MD
            for m in range(MD):
                acc_done[m] = E("gpsimd", lambda g,
                                a=x0b[cpar][:, m * TC:(m + 1) * TC],
                                b=t1b[cpar][:, m * TC:(m + 1) * TC]:
                                g.tensor_tensor(a, a, b, ALU.add),
                                waits=[("ve", xa_done[m]), x0_ready[c],
                                       ("se", st["ln1_sq"])], inc="ge")
            yield
            stats(t1b[cpar], MD, False, [[("ve", xa_done[kt])]
                                         for kt in range(MD)])
            rows = ln_rows("rms")
            ht_v = None
            for kt in range(KD):
                ht_v = E("vector", lambda v, kt=kt: v.scalar_tensor_tensor(
                    htb[:, kt * TC:(kt + 1) * TC],
                    t1b[cpar][:, kt * TC:(kt + 1) * TC],
                    c_norm2[:, kt:kt + 1], bcB[:], ALU.mult, ALU.mult),
                    waits=[rows["rho"], ("ve", xa_done[kt])]
                    + edges.get("htb", []), inc="ve")
            st["ht"] = ("ve", ht_v)
            tap("ht", c, htb[:], [128, MD * TC], BF16, ("ve", ht_v))
            yield
            for _ in router_gates(c, htb, ("ve", ht_v), r_rtw, Get, 0.0,
                                  "gates"):
                yield
            get_done = st["gates"]
            for e in range(NE):
                wA = [pbA_edge[0]] if pbA_edge[0] else []
                gb_v = E("tensor", lambda t, e=e: t.matmul(
                    pbA[:], c_sel16[0:8, e * 128:(e + 1) * 128], Get[:],
                    start=True, stop=True), waits=[get_done] + wA, inc="pe")
                gs_v = E("vector", lambda v: v.tensor_copy(Gsb[:], pbA[:]),
                         waits=[("pe", gb_v)] + edges.get("Gsb", []),
                         inc="ve")
                pbA_edge[0] = ("ve", gs_v)
                edges["Gsb"] = [("ve", gs_v)]
                hid_done = [None] * KI
                for hs in range(8):
                    src = w1_bf.rearrange("e (kt p) m -> e p kt m", p=128)[
                        e][:, :, hs * 256:(hs + 1) * 256]
                    slot, wl_v = ring_load(("w1", e), src)
                    last_pe = None
                    for mt_in in range(2):
                        hi = hs * 2 + mt_in
                        pi, pmt, pmw = next_pm()
                        for kt in range(KD):
                            lhs = wring[slot][:, kt * 256 + mt_in * 128:
                                              kt * 256 + mt_in * 128 + 128]
                            rhs = htb[:, kt * TC:(kt + 1) * TC]
                            last_pe = E("tensor", lambda t, o=pmt, l=lhs,
                                        r=rhs, st_=(kt == 0),
                                        sp=(kt == KD - 1): t.matmul(
                                            o[:], l, r, start=st_, stop=sp),
                                        waits=[wl_v, ("ve", ht_v)] + pmw,
                                        inc="pe" if kt == KD - 1 else None)
                        hv = E("scalar", lambda s, p=pmt, hi=hi, e=e:
                               s.activation(
                                   hidb[:, hi * TC:(hi + 1) * TC], p[:],
                                   AF.Identity,
                                   bias=c_b1[:, e * MI + hi:
                                             e * MI + hi + 1]),
                               waits=[("pe", last_pe)]
                               + edges.get("hidb", []), inc="se")
                        hid_done[hi] = hv
                        pm_edge[pi] = ("se", hv)
                    ring_release(slot, ("pe", last_pe))
                    yield
                gelu_h1 = E("scalar", lambda s: s.activation(
                    hidb[:, 0:8 * TC], hidb[:, 0:8 * TC],
                    AF.Gelu_apprx_tanh),
                    waits=[("se", max(hid_done[0:8]))], inc="se")
                gelu_v = E("scalar", lambda s: s.activation(
                    hidb[:, 8 * TC:16 * TC], hidb[:, 8 * TC:16 * TC],
                    AF.Gelu_apprx_tanh),
                    waits=[("se", max(hid_done))], inc="se")
                yield
                for hs in range(8):
                    src = w2_bf.rearrange("e (kt p) m -> e p kt m", p=128)[
                        e][:, :, hs * 128:(hs + 1) * 128]
                    slot, wl_v = ring_load(("w2", e), src)
                    m = hs
                    pi, pmt, pmw = next_pm()
                    last_pe = None
                    for kt in range(KI):
                        lhs = wring[slot][:, kt * 128:kt * 128 + 128]
                        rhs = hidb[:, kt * TC:(kt + 1) * TC]
                        gw_ = ("se", gelu_h1 if kt < 8 else gelu_v)
                        last_pe = E("tensor", lambda t, o=pmt, l=lhs, r=rhs,
                                    st_=(kt == 0), sp=(kt == KI - 1):
                                    t.matmul(o[:], l, r, start=st_, stop=sp),
                                    waits=[wl_v, gw_] + pmw,
                                    inc="pe" if kt == KI - 1 else None)
                    _, p_t, w_t = next_pool()
                    sv = E("vector", lambda v, a=p_t, p=pmt, m=m, e=e:
                           v.scalar_tensor_tensor(
                               a[:], p[:],
                               c_b2[:, e * MD + m:e * MD + m + 1],
                               Gsb[:], ALU.add, ALU.mult),
                           waits=[("pe", last_pe), ("ve", gs_v)] + w_t,
                           inc="ve")
                    pm_edge[pi] = ("ve", sv)
                    gv2 = E("gpsimd", lambda g, a=p_t,
                            o=x0b[cpar][:, m * TC:(m + 1) * TC]:
                            g.tensor_tensor(o, o, a[:], ALU.add),
                            waits=[("ve", sv), ("ge", acc_done[m])],
                            inc="ge")
                    acc_done[m] = gv2
                    pool_edge[pool.index(p_t)] = ("ge", gv2)
                    ring_release(slot, ("pe", last_pe))
                    yield
            edges["htb"] = [("pe", pb.now("pe"))]
            x1_done = [("ge", v) for v in acc_done]
            tap("x1", c, x0b[cpar][:], [128, MD * TC], F32, x1_done[-1])
            ln2_sq = stats(x0b[cpar], MD, True, [[x1_done[kt]]
                                                 for kt in range(MD)])
            rows2 = ln_rows("ln2")
            b2_v = None
            for kt in range(MD):
                _, p_t, w_t = next_pool()
                E("vector", lambda v, a=p_t,
                  b=x0b[cpar][:, kt * TC:(kt + 1) * TC]: v.tensor_tensor(
                      a[:], b, bcA[:], ALU.subtract),
                  waits=[rows2["mu"], x1_done[kt]] + w_t, inc="ve")
                b2_v = E("vector", lambda v, a=p_t, kt=kt: v.tensor_tensor(
                    htb[:, kt * TC:(kt + 1) * TC], a[:], bcB[:], ALU.mult),
                    waits=[rows2["r"]] + edges.get("htb", []), inc="ve")
                pool_edge[pool.index(p_t)] = ("ve", b2_v)
            tap("b2nd", c, htb[:], [128, MD * TC], BF16, ("ve", b2_v))
            yield
            for _ in router_gates(c, htb, ("ve", b2_v), r_frw, Pet, 1e-9,
                                  "pgates"):
                yield
            pet_done = st["pgates"]
            facc_done = list(x1_done)
            for e in range(NE):
                wA = [pbA_edge[0]] if pbA_edge[0] else []
                pb_v = E("tensor", lambda t, e=e: t.matmul(
                    pbA[:], c_sel16[0:8, e * 128:(e + 1) * 128], Pet[:],
                    start=True, stop=True), waits=[pet_done] + wA, inc="pe")
                gs_v = E("vector", lambda v: v.tensor_copy(Gsb[:], pbA[:]),
                         waits=[("pe", pb_v)] + edges.get("Gsb", []),
                         inc="ve")
                pbA_edge[0] = ("ve", gs_v)
                edges["Gsb"] = [("ve", gs_v)]
                sg_done = [None] * KI
                for hs in range(8):
                    src = gw_bf.rearrange("e (kt p) m -> e p kt m", p=128)[
                        e][:, :, hs * 256:(hs + 1) * 256]
                    slot, wl_v = ring_load(("gw", e), src)
                    last_pe = None
                    for mt_in in range(2):
                        fi = hs * 2 + mt_in
                        pi, pmt, pmw = next_pm()
                        for kt in range(KD):
                            lhs = wring[slot][:, kt * 256 + mt_in * 128:
                                              kt * 256 + mt_in * 128 + 128]
                            rhs = htb[:, kt * TC:(kt + 1) * TC]
                            last_pe = E("tensor", lambda t, o=pmt, l=lhs,
                                        r=rhs, st_=(kt == 0),
                                        sp=(kt == KD - 1): t.matmul(
                                            o[:], l, r, start=st_, stop=sp),
                                        waits=[wl_v, ("ve", b2_v)] + pmw,
                                        inc="pe" if kt == KD - 1 else None)
                        sgv = E("scalar", lambda s, p=pmt, fi=fi, e=e:
                                s.activation(
                                    hidb[:, fi * TC:(fi + 1) * TC], p[:],
                                    AF.Identity,
                                    bias=c_gb[:, e * MI + fi:
                                              e * MI + fi + 1]),
                                waits=[("pe", last_pe)]
                                + edges.get("hidb", []), inc="se")
                        sg_done[fi] = sgv
                        pm_edge[pi] = ("se", sgv)
                    ring_release(slot, ("pe", last_pe))
                    yield
                silu_h1 = E("scalar", lambda s: s.activation(
                    hidb[:, 0:8 * TC], hidb[:, 0:8 * TC], AF.Silu),
                    waits=[("se", max(sg_done[0:8]))], inc="se")
                silu_v = E("scalar", lambda s: s.activation(
                    hidb[:, 8 * TC:16 * TC], hidb[:, 8 * TC:16 * TC],
                    AF.Silu),
                    waits=[("se", max(sg_done))], inc="se")
                yield
                h2_done = [None] * KI
                for hs in range(8):
                    src = uw_bf.rearrange("e (kt p) m -> e p kt m", p=128)[
                        e][:, :, hs * 256:(hs + 1) * 256]
                    slot, wl_v = ring_load(("uw", e), src)
                    last_pe = None
                    for mt_in in range(2):
                        fi = hs * 2 + mt_in
                        pi, pmt, pmw = next_pm()
                        for kt in range(KD):
                            lhs = wring[slot][:, kt * 256 + mt_in * 128:
                                              kt * 256 + mt_in * 128 + 128]
                            rhs = htb[:, kt * TC:(kt + 1) * TC]
                            last_pe = E("tensor", lambda t, o=pmt, l=lhs,
                                        r=rhs, st_=(kt == 0),
                                        sp=(kt == KD - 1): t.matmul(
                                            o[:], l, r, start=st_, stop=sp),
                                        waits=[wl_v, ("ve", b2_v)] + pmw,
                                        inc="pe" if kt == KD - 1 else None)
                        sw_ = ("se", silu_h1 if fi < 8 else silu_v)
                        hv = E("vector", lambda v, p=pmt, fi=fi, e=e:
                               v.scalar_tensor_tensor(
                                   hidb[:, fi * TC:(fi + 1) * TC], p[:],
                                   c_ub[:, e * MI + fi:e * MI + fi + 1],
                                   hidb[:, fi * TC:(fi + 1) * TC],
                                   ALU.add, ALU.mult),
                               waits=[("pe", last_pe), sw_],
                               inc="ve")
                        h2_done[fi] = hv
                        pm_edge[pi] = ("ve", hv)
                    ring_release(slot, ("pe", last_pe))
                    yield
                for hs in range(8):
                    src = dw_bf.rearrange("e (kt p) m -> e p kt m", p=128)[
                        e][:, :, hs * 128:(hs + 1) * 128]
                    slot, wl_v = ring_load(("dw", e), src)
                    m = hs
                    pi, pmt, pmw = next_pm()
                    last_pe = None
                    for kt in range(KI):
                        lhs = wring[slot][:, kt * 128:kt * 128 + 128]
                        rhs = hidb[:, kt * TC:(kt + 1) * TC]
                        last_pe = E("tensor", lambda t, o=pmt, l=lhs, r=rhs,
                                    st_=(kt == 0), sp=(kt == KI - 1):
                                    t.matmul(o[:], l, r, start=st_, stop=sp),
                                    waits=[wl_v, ("ve", h2_done[kt])] + pmw,
                                    inc="pe" if kt == KI - 1 else None)
                    _, p_t, w_t = next_pool()
                    sv = E("vector", lambda v, a=p_t, p=pmt, m=m, e=e:
                           v.scalar_tensor_tensor(
                               a[:], p[:],
                               c_db[:, e * MD + m:e * MD + m + 1],
                               Gsb[:], ALU.add, ALU.mult),
                           waits=[("pe", last_pe), ("ve", gs_v)] + w_t,
                           inc="ve")
                    pm_edge[pi] = ("ve", sv)
                    gv2 = E("gpsimd", lambda g, a=p_t,
                            o=x0b[cpar][:, m * TC:(m + 1) * TC]:
                            g.tensor_tensor(o, o, a[:], ALU.add),
                            waits=[("ve", sv), facc_done[m], ("ve", b2_v),
                                   ("se", ln2_sq)], inc="ge")
                    facc_done[m] = ("ge", gv2)
                    pool_edge[pool.index(p_t)] = ("ge", gv2)
                    ring_release(slot, ("pe", last_pe))
                    yield
            edges["hidb"] = [("pe", pb.now("pe"))]
            edges["htb"] = [("pe", pb.now("pe"))]
            out_done = pb.now("ge")
            dst = out.rearrange("(kt p) t -> p kt t", p=128)[
                :, :, c * TC:(c + 1) * TC]
            srcv = x0b[cpar][:].rearrange("p (kt t) -> p kt t", t=TC)
            st_sem = pb.sem(f"sl{cpar}")
            st_v = E("gpsimd", lambda g, d=dst, s=srcv: g.dma_start(d, s),
                     waits=[("ge", out_done), x0_ready[c]], inc=st_sem,
                     inck=16)
            if c + 2 < NCH:
                x0_ready[c + 2] = x0_load(c + 2, waits=[(st_sem, st_v)])
            edges[f"x0_{cpar}"] = [(st_sem, st_v)]
            edges[f"t1_{cpar}"] = [("se", ln2_sq), ("ve", b2_v),
                                   x1_done[-1]]
            edges[f"xc_{cpar}"] = [("pe", pb.now("pe"))]
            edges[Get.name] = [("pe", pb.now("pe"))]
            edges[Pet.name] = [("pe", pb.now("pe"))]
            del front_state[c]
            yield

        gens = {}
        emit_front1(0)
        for it in range(NCH + 1):
            if it < NCH:
                emit_front2(it)
                g = gens.pop(it - 1, None)
                def pull(k, g=g):
                    if g is None or not interleave:
                        return
                    for _ in range(k):
                        if next(g, StopIteration) is StopIteration:
                            break
                emit_scan(it, pull)
                if g is not None:
                    for _ in g:
                        pass
                gens[it] = tail_units(it)
                if it + 1 < NCH:
                    emit_front1(it + 1)
            else:
                for _ in gens.pop(it - 1):
                    pass

        fin = [(s, pb.now(s)) for s in ("sl0", "sl1", "tl") if s in pb.semc]
        E("gpsimd", lambda g: g.nop(), waits=fin)

        with nc.Block() as block:
            @block.sync
            def _(e):
                for f in pb.order["sync"]:
                    f(e)

            @block.tensor
            def _(e):
                for f in pb.order["tensor"]:
                    f(e)

            @block.vector
            def _(e):
                for f in pb.order["vector"]:
                    f(e)

            @block.scalar
            def _(e):
                for f in pb.order["scalar"]:
                    f(e)

            @block.gpsimd
            def _(e):
                for f in pb.order["gpsimd"]:
                    f(e)

    return nc, taps


def _host_inputs(inputs, b):
    f = np.ascontiguousarray
    ip = inputs
    m = {
        "xT": f(np.asarray(ip["L0_attn_x"][b]).T),
        "ipw": f(ip["L0_attn_in_proj_w"]),
        "opw": f(ip["L0_attn_out_proj_w"]),
        "xpw": f(ip["L0_attn_x_proj_w"]),
        "dtw": f(ip["L0_attn_dt_proj_w"]),
        "w1": f(ip["L0_attn_w1"]),
        "w2": f(ip["L0_attn_w2"]),
        "gw": f(np.asarray(ip["L0_ffn_gate_w"])),
        "uw": f(np.asarray(ip["L0_ffn_up_w"])),
        "dw": f(np.asarray(ip["L0_ffn_down_w"])),
        "rtw": f(ip["L0_attn_router_w"]),
        "frw": f(np.asarray(ip["L0_ffn_router_w"]).T),
        "norm1c": f(np.asarray(ip["L0_attn_norm1_w"]).reshape(8, 128).T),
        "norm2c": f(np.asarray(ip["L0_attn_norm2_w"]).reshape(8, 128).T),
        "convw_c": f(np.asarray(ip["L0_attn_conv_w"]).reshape(16, 128, 4)
                     .transpose(1, 0, 2).reshape(128, 64)),
        "convb_c": f(np.asarray(ip["L0_attn_conv_b"]).reshape(16, 128).T),
        "dtb_c": f(np.asarray(ip["L0_attn_dt_proj_b"]).reshape(16, 128).T),
        "Dp_c": f(np.asarray(ip["L0_attn_D_param"]).reshape(16, 128).T),
        "Alog_c": f(np.asarray(ip["L0_attn_A_log"]).reshape(16, 128, 16)
                    .transpose(1, 0, 2).reshape(128, 16 * 16)),
        "b1_c": f(np.asarray(ip["L0_attn_b1"]).reshape(8, 16, 128)
                  .transpose(2, 0, 1).reshape(128, 8 * 16)),
        "b2_c": f(np.asarray(ip["L0_attn_b2"]).reshape(8, 8, 128)
                  .transpose(2, 0, 1).reshape(128, 8 * 8)),
        "gb_c": f(np.asarray(ip["L0_ffn_gate_b"]).reshape(8, 16, 128)
                  .transpose(2, 0, 1).reshape(128, 8 * 16)),
        "ub_c": f(np.asarray(ip["L0_ffn_up_b"]).reshape(8, 16, 128)
                  .transpose(2, 0, 1).reshape(128, 8 * 16)),
        "db_c": f(np.asarray(ip["L0_ffn_down_b"]).reshape(8, 8, 128)
                  .transpose(2, 0, 1).reshape(128, 8 * 8)),
        "ident_c": np.eye(128, dtype=np.float32),
        "misc_c": np.tile(np.array([[1e-5, 1e-6, 1.0, 0.0]], np.float32),
                          (128, 1)),
    }
    import ml_dtypes
    sel = np.zeros((16, 16, 128), np.float32)
    for s in range(16):
        sel[s, s, :] = 1.0
    m["sel16_c"] = f(sel.transpose(1, 0, 2).reshape(16, 16 * 128)
                     .astype(ml_dtypes.bfloat16))
    m["onescol_c"] = np.ones((128, 1), np.float32).astype(ml_dtypes.bfloat16)
    m["onesrow_c"] = np.ones((1, 128), np.float32).astype(ml_dtypes.bfloat16)
    for k in m:
        if m[k].dtype == np.float64:
            m[k] = m[k].astype(np.float32)
    return m


def kernel(**inputs):
    nc, _ = build_program(DEBUG_CHUNKS)
    in_maps = [_host_inputs(inputs, b) for b in range(N_CORES)]
    res = run_bass_kernel_spmd(nc, in_maps, core_ids=list(range(N_CORES)))
    outs = [np.ascontiguousarray(np.asarray(r["out"]).T)
            for r in res.results]
    return np.stack(outs, 0)


# version 06
# speedup vs baseline: 1.2945x; 1.0054x over previous
import contextlib
import numpy as np
import concourse.bass as bass
import concourse.mybir as mybir
from concourse.bass_utils import run_bass_kernel_spmd

F32 = mybir.dt.float32
BF16 = mybir.dt.bfloat16
AF = mybir.ActivationFunctionType
ALU = mybir.AluOpType
AX = mybir.AxisListType

D = 1024
DIN = 2048
T = 4096
TC = 512
NCH = T // TC
NS = 16
DTR = 64
NE = 8
KD = D // 128
KI = DIN // 128
MD = D // 128
MI = DIN // 128

N_CORES = 8
DEBUG_CHUNKS = ()


class PB:

    ENGS = ("tensor", "vector", "scalar", "gpsimd", "sync")

    def __init__(self, nc, stack, self_chain=(), dedup=True):
        self.nc = nc
        self.stack = stack
        self.dedup = dedup
        self.order = {e: [] for e in self.ENGS}
        self.semh = {}
        self.semc = {}
        for name in ("pe", "ve", "se", "ge"):
            self.sem(name)
        self.waited = {e: {} for e in self.ENGS}
        self.self_chain = {"tensor": "pe", "vector": "ve", "scalar": "se",
                           "gpsimd": "ge"}
        self.self_chain = {k: v for k, v in self.self_chain.items()
                           if k in self_chain}

    def sem(self, name):
        if name not in self.semh:
            self.semh[name] = self.stack.enter_context(self.nc.semaphore(name))
            self.semc[name] = 0
        return name

    def emit(self, eng, build, waits=(), inc=None, inck=1):
        chain = self.self_chain.get(eng)
        if chain is not None:
            waits = list(waits) + [(chain, self.semc[chain])]
            if inc is None:
                inc = chain
        w = []
        for (s, v) in waits:
            if v is None or v <= 0:
                continue
            if self.dedup and self.waited[eng].get(s, 0) >= v:
                continue
            if self.waited[eng].get(s, 0) < v:
                self.waited[eng][s] = v
            if not any(h is self.semh[s] and vv >= v for (h, vv) in w):
                w.append((self.semh[s], v))
        val = None
        if inc is not None:
            self.semc[inc] += inck
            val = self.semc[inc]
            hinc = self.semh[inc]
        w = tuple(w)
        if inc is None:
            def thunk(e, _w=w, _b=build):
                for (h, v) in _w:
                    e.wait_ge(h, v)
                _b(e)
        else:
            def thunk(e, _w=w, _b=build, _h=hinc, _k=inck):
                for (h, v) in _w:
                    e.wait_ge(h, v)
                _b(e).then_inc(_h, _k)
        self.order[eng].append(thunk)
        return val

    def now(self, sem):
        return self.semc.get(sem, 0)


def build_program(debug_chunks=(), sim_edges=(), interleave=True):
    nc = bass.Bass("TRN2", target_bir_lowering=False)

    def din(name, shape, dtype=F32):
        return nc.dram_tensor(name, list(shape), dtype,
                              kind="ExternalInput").ap()

    xT = din("xT", [D, T])
    ipw = din("ipw", [D, 2 * DIN])
    opw = din("opw", [DIN, D])
    xpw = din("xpw", [DIN, 96])
    dtw = din("dtw", [DTR, DIN])
    w1 = din("w1", [NE, D, DIN])
    w2 = din("w2", [NE, DIN, D])
    gw = din("gw", [NE, D, DIN])
    uw = din("uw", [NE, D, DIN])
    dw = din("dw", [NE, DIN, D])
    rtw = din("rtw", [D, NE])
    frw = din("frw", [D, NE])
    norm1c = din("norm1c", [128, KD])
    norm2c = din("norm2c", [128, KD])
    convw_c = din("convw_c", [128, MI * 4])
    convb_c = din("convb_c", [128, MI])
    dtb_c = din("dtb_c", [128, MI])
    Dp_c = din("Dp_c", [128, MI])
    Alog_c = din("Alog_c", [128, MI * NS])
    b1_c = din("b1_c", [128, NE * MI])
    b2_c = din("b2_c", [128, NE * MD])
    gb_c = din("gb_c", [128, NE * MI])
    ub_c = din("ub_c", [128, NE * MI])
    db_c = din("db_c", [128, NE * MD])
    sel16_c = din("sel16_c", [16, 16 * 128], BF16)
    ident_c = din("ident_c", [128, 128])
    onescol_c = din("onescol_c", [128, 1], BF16)
    onesrow_c = din("onesrow_c", [1, 128], BF16)
    misc_c = din("misc_c", [128, 4])

    out = nc.dram_tensor("out", [D, T], F32, kind="ExternalOutput").ap()

    ipw_bf = nc.dram_tensor("ipw_bf", [D, 2 * DIN], BF16, kind="Internal").ap()
    opw_bf = nc.dram_tensor("opw_bf", [DIN, D], BF16, kind="Internal").ap()
    w1_bf = nc.dram_tensor("w1_bf", [NE, D, DIN], BF16, kind="Internal").ap()
    w2_bf = nc.dram_tensor("w2_bf", [NE, DIN, D], BF16, kind="Internal").ap()
    gw_bf = nc.dram_tensor("gw_bf", [NE, D, DIN], BF16, kind="Internal").ap()
    uw_bf = nc.dram_tensor("uw_bf", [NE, D, DIN], BF16, kind="Internal").ap()
    dw_bf = nc.dram_tensor("dw_bf", [NE, DIN, D], BF16, kind="Internal").ap()

    taps = {}

    with contextlib.ExitStack() as stack:
        def sb(name, shape, dtype=F32):
            return stack.enter_context(nc.sbuf_tensor(name, list(shape),
                                                      dtype))
        def psum(name, shape, dtype=F32):
            return stack.enter_context(nc.psum_tensor(name, list(shape),
                                                      dtype))

        c_norm1 = sb("c_norm1", [128, KD])
        c_norm2 = sb("c_norm2", [128, KD])
        c_convw = sb("c_convw", [128, MI * 4])
        c_convb = sb("c_convb", [128, MI])
        c_dtb = sb("c_dtb", [128, MI])
        c_Dp = sb("c_Dp", [128, MI])
        c_Alog = sb("c_Alog", [128, MI * NS])
        c_b1 = sb("c_b1", [128, NE * MI])
        c_b2 = sb("c_b2", [128, NE * MD])
        c_gb = sb("c_gb", [128, NE * MI])
        c_ub = sb("c_ub", [128, NE * MI])
        c_db = sb("c_db", [128, NE * MD])
        c_sel16 = sb("c_sel16", [16, 16 * 128], BF16)
        c_ident = sb("c_ident", [128, 128])
        c_onescol = sb("c_onescol", [128, 1], BF16)
        c_onesrow = sb("c_onesrow", [1, 128], BF16)
        c_misc = sb("c_misc", [128, 4])

        r_xpw = sb("r_xpw", [128, KI * 96], BF16)
        r_dtw = sb("r_dtw", [DTR, DIN], BF16)
        r_rtw = sb("r_rtw", [128, KD * NE], BF16)
        r_frw = sb("r_frw", [128, KD * NE], BF16)

        NRING = 3
        wring = [sb(f"wring{i}", [128, 2048], BF16) for i in range(NRING)]

        x0b = [sb(f"x0_{i}", [128, MD * TC]) for i in range(2)]
        t1b = [sb(f"t1_{i}", [128, MD * TC]) for i in range(2)]
        h1b = sb("h1b", [128, MD * TC], BF16)
        htb = sb("htb", [128, MD * TC], BF16)
        zb = sb("zb", [128, MI * TC], BF16)
        xcb = [sb(f"xc_{i}", [128, MI * TC], BF16) for i in range(2)]
        tailb = sb("tailb", [128, MI * 3], BF16)
        dtrb = sb("dtrb", [DTR, TC], BF16)
        Bsb = sb("Bsb", [NS, TC], BF16)
        Csb = sb("Csb", [NS, TC], BF16)
        dtb2 = sb("dtb2", [128, TC])
        wbuf = sb("wbuf", [128, TC])
        dAb = sb("dAb", [128, TC])
        ubuf = sb("ubuf", [128, TC], BF16)
        hbuf = sb("hbuf", [128, TC], BF16)
        ybuf = sb("ybuf", [128, TC])
        carry = sb("carry", [128, MI * NS])
        hidb = sb("hidb", [128, KI * TC], BF16)
        Gsb = sb("Gsb", [128, TC], BF16)
        Get = sb("Get", [NE, TC], BF16)
        Pet = sb("Pet", [NE, TC], BF16)
        Gtb = sb("Gtb", [128, 32])
        RLb = sb("RLb", [NE, TC])
        rsc = sb("rsc", [128, 32])
        strow = sb("strow", [1, TC])
        stbf = sb("stbf", [1, TC], BF16)
        stsrc = sb("stsrc", [128, 2 * TC], BF16)
        bcA = sb("bcA", [128, TC], BF16)
        bcB = sb("bcB", [128, TC], BF16)
        pool = [sb(f"pool{i}", [128, TC]) for i in range(4)]

        pm = [psum(f"pm{i}", [128, TC]) for i in range(6)]
        pbA = psum("pbA", [128, TC])
        pbB = psum("pbB", [128, TC])

        pb = PB(nc, stack, self_chain=sim_edges, dedup=not sim_edges)
        E = pb.emit

        pm_edge = [None] * 6
        pm_i = [0]
        pbA_edge = [None]
        pbB_edge = [None]
        pool_edge = [None] * len(pool)
        pool_i = [0]
        stats_edge = [None]
        router_edge = [None]
        dA_edge = [None, None]
        dt_edge = [None]
        edges = {}

        def next_pm():
            i = pm_i[0] % 6
            pm_i[0] += 1
            return i, pm[i], ([pm_edge[i]] if pm_edge[i] else [])

        def next_pool():
            i = pool_i[0] % len(pool)
            pool_i[0] += 1
            return i, pool[i], ([pool_edge[i]] if pool_edge[i] else [])

        def gload(dst_ap, src_ap, sem="glc"):
            return E("gpsimd", lambda g, d=dst_ap, s=src_ap: g.dma_start(d, s),
                     inc=pb.sem(sem), inck=16)

        for dst, src in [
            (c_norm1[:], norm1c), (c_norm2[:], norm2c), (c_convw[:], convw_c),
            (c_convb[:], convb_c), (c_dtb[:], dtb_c), (c_Dp[:], Dp_c),
            (c_Alog[:], Alog_c), (c_b1[:], b1_c), (c_b2[:], b2_c),
            (c_gb[:], gb_c), (c_ub[:], ub_c), (c_db[:], db_c),
            (c_sel16[:], sel16_c), (c_ident[:], ident_c),
            (c_onescol[:], onescol_c), (c_onesrow[:], onesrow_c),
            (c_misc[:], misc_c),
        ]:
            gload(dst, src)
        gload(r_xpw[:].rearrange("p (kt m) -> p kt m", m=96),
              xpw.rearrange("(kt p) m -> p kt m", p=128))
        gload(r_dtw[:], dtw)
        gload(r_rtw[:].rearrange("p (kt m) -> p kt m", m=NE),
              rtw.rearrange("(kt p) m -> p kt m", p=128))
        gload(r_frw[:].rearrange("p (kt m) -> p kt m", m=NE),
              frw.rearrange("(kt p) m -> p kt m", p=128))

        def x0_load(c, waits=()):
            dst = x0b[c % 2][:].rearrange("p (kt t) -> p kt t", t=TC)
            src = xT.rearrange("(kt p) t -> p kt t", p=128)[
                :, :, c * TC:(c + 1) * TC]
            sem = pb.sem(f"glx{c % 2}")
            v = E("gpsimd", lambda g, d=dst, s=src: g.dma_start(d, s),
                  waits=waits, inc=sem, inck=16)
            return (sem, v)
        x0_ready = [None] * NCH
        x0_ready[0] = x0_load(0)
        x0_ready[1] = x0_load(1)
        consts_gl = pb.now("glc")

        prep = {}
        def prep_dma(key, dst_ap, src_ap):
            sem = pb.sem(f"gp{len(prep)}")
            v = E("gpsimd", lambda g, d=dst_ap, s=src_ap: g.dma_start(d, s),
                  inc=sem, inck=16)
            prep[key] = (sem, v)
        prep_dma("ipw", ipw_bf, ipw)
        prep_dma("opw", opw_bf, opw)
        for e in range(NE):
            prep_dma(("w1", e), w1_bf[e], w1[e])
            prep_dma(("w2", e), w2_bf[e], w2[e])
        for e in range(NE):
            prep_dma(("gw", e), gw_bf[e], gw[e])
            prep_dma(("uw", e), uw_bf[e], uw[e])
        for e in range(NE):
            prep_dma(("dw", e), dw_bf[e], dw[e])

        vA = E("scalar", lambda s: s.activation(c_Alog[:], c_Alog[:], AF.Exp),
               waits=[("glc", consts_gl)], inc="se")
        vA2 = E("vector", lambda v: v.tensor_scalar(
            c_Alog[:], c_Alog[:], -1.0, None, ALU.mult),
            waits=[("se", vA)], inc="ve")
        E("vector", lambda v: v.memset(carry[:], 0.0), inc="ve")
        vz2 = E("vector", lambda v: v.memset(tailb[:], 0.0), inc="ve")
        edges["tail0"] = ("ve", vz2)

        ring_state = {"n": 0, "slot_edge": [None] * NRING}
        for _i in range(NRING):
            pb.sem(f"wl{_i}")

        def ring_load(key, src_view):
            slot = ring_state["n"] % NRING
            sem = f"wl{slot}"
            waits = []
            if key in prep:
                waits.append(prep[key])
            if ring_state["slot_edge"][slot] is not None:
                waits.append(ring_state["slot_edge"][slot])
                waits.append((sem, pb.now(sem)))
            dst = wring[slot][:].rearrange(
                "p (kt m) -> p kt m", m=src_view.shape[2])
            wl = E("sync", lambda sy, d=dst, s=src_view: sy.dma_start(d, s),
                   waits=waits, inc=sem, inck=16)
            ring_state["n"] += 1
            return slot, (sem, wl)

        def ring_release(slot, edge):
            ring_state["slot_edge"][slot] = edge

        def tap(name, c, src_ap, shape, dtype, wait):
            if c not in debug_chunks:
                return
            key = f"tap_{name}_{c}"
            t = nc.dram_tensor(key, list(shape), dtype,
                               kind="ExternalOutput").ap()
            taps[key] = (name, c)
            v = E("gpsimd", lambda g, d=t, s=src_ap: g.dma_start(d, s),
                  waits=[wait], inc=pb.sem("tl"), inck=16)
            for eng in ("tensor", "vector", "scalar", "gpsimd"):
                E(eng, lambda x: x.nop(), waits=[("tl", v)])

        stats_banks = {}

        def stats(xbuf, nkt, need_mean, src_waits):
            sv = None
            i6, p6, w6 = next_pm()
            i7, p7, w7 = next_pm()
            stats_banks["s1"], stats_banks["s2"] = p6, p7
            stats_banks["i1"], stats_banks["i2"] = i6, i7
            for kt in range(nkt):
                xs = xbuf[:, kt * TC:(kt + 1) * TC]
                w_ = list(src_waits[kt]) if src_waits else []
                if stats_edge[0]:
                    w_.append(stats_edge[0])
                cv = None
                if need_mean:
                    cv = E("vector", lambda v, a=xs: v.tensor_copy(
                        stsrc[:, 0:TC], a), waits=w_, inc="ve")
                sv = E("scalar", lambda s, a=xs: s.activation(
                    stsrc[:, TC:2 * TC], a, AF.Square), waits=w_, inc="se")
                st_ = (kt == 0)
                sp = (kt == nkt - 1)
                if need_mean:
                    E("tensor", lambda t, st_=st_, sp=sp, p6=p6: t.matmul(
                        p6[0:1, :], c_onescol[:, 0:1], stsrc[:, 0:TC],
                        start=st_, stop=sp),
                        waits=[("ve", cv), ("glc", consts_gl)] + w6,
                        inc="pe")
                sq_v = E("tensor", lambda t, st_=st_, sp=sp, p7=p7: t.matmul(
                    p7[0:1, :], c_onescol[:, 0:1], stsrc[:, TC:2 * TC],
                    start=st_, stop=sp),
                    waits=[("se", sv), ("glc", consts_gl)] + w7, inc="pe")
                stats_edge[0] = ("pe", sq_v)
                w6 = w7 = []
            return sv

        stbf_edge = [None]

        def bcast_row_ap(src_row, target, extra_waits):
            w_ = list(extra_waits)
            if stbf_edge[0]:
                w_.append(stbf_edge[0])
            cb = E("vector", lambda v, s=src_row: v.tensor_copy(
                stbf[0:1, :], s), waits=w_, inc="ve")
            bv = E("tensor", lambda t: t.matmul(
                pbA[:], c_onesrow[0:1, :], stbf[0:1, :], start=True,
                stop=True), waits=[("ve", cb)]
                + ([pbA_edge[0]] if pbA_edge[0] else []), inc="pe")
            stbf_edge[0] = ("pe", bv)
            ev = E("vector", lambda v, o=target: v.tensor_copy(o[:], pbA[:]),
                   waits=[("pe", bv)], inc="ve")
            pbA_edge[0] = ("ve", ev)
            return ("ve", ev)

        def bcast_row(src_is_strow_off, target, extra_waits):
            w_ = list(extra_waits)
            if stbf_edge[0]:
                w_.append(stbf_edge[0])
            cb = E("vector", lambda v: v.tensor_copy(
                stbf[0:1, :], strow[0:1, src_is_strow_off:
                                    src_is_strow_off + TC]),
                waits=w_, inc="ve")
            bv = E("tensor", lambda t: t.matmul(
                pbA[:], c_onesrow[0:1, :], stbf[0:1, :], start=True,
                stop=True), waits=[("ve", cb)]
                + ([pbA_edge[0]] if pbA_edge[0] else []), inc="pe")
            stbf_edge[0] = ("pe", bv)
            ev = E("vector", lambda v, o=target: v.tensor_copy(o[:], pbA[:]),
                   waits=[("pe", bv)], inc="ve")
            pbA_edge[0] = ("ve", ev)
            return ("ve", ev)

        def ln_rows(kind):
            out = {}
            prow = prow2 = None
            if kind in ("ln1", "ln2"):
                _, prow, wrow = next_pool()
                _, prow2, wrow2 = next_pool()
                tmp = prow[0:1, 0:TC]
                var = prow2[0:1, 0:TC]
                p6, p7 = stats_banks["s1"], stats_banks["s2"]
                v1 = E("vector", lambda v, p6=p6: v.tensor_scalar(
                    strow[0:1, 0:TC], p6[0:1, :], 1.0 / D, None, ALU.mult),
                    waits=[stats_edge[0]], inc="ve")
                v2b = E("vector", lambda v, var=var, p7=p7: v.tensor_scalar(
                    var, p7[0:1, :], 1.0 / D, None,
                    ALU.mult), waits=wrow2, inc="ve")
                pm_edge[stats_banks["i1"]] = ("ve", v1)
                pm_edge[stats_banks["i2"]] = ("ve", v2b)
                v3 = E("vector", lambda v, tmp=tmp: v.tensor_tensor(
                    tmp, strow[0:1, 0:TC],
                    strow[0:1, 0:TC], ALU.mult),
                    waits=[("ve", v1)] + wrow, inc="ve")
                v4 = E("vector", lambda v, tmp=tmp, var=var: v.tensor_tensor(
                    var, var,
                    tmp, ALU.subtract),
                    waits=[("ve", v3)], inc="ve")
                s1 = E("scalar", lambda s, tmp=tmp, var=var: s.activation(
                    tmp, var, AF.Ln,
                    bias=c_misc[0:1, 0:1]), waits=[("ve", v4)], inc="se")
                s2 = E("scalar", lambda s, tmp=tmp: s.activation(
                    tmp, tmp,
                    AF.Exp, scale=-0.5), waits=[("se", s1)], inc="se")
                out["mu"] = bcast_row(0, bcA, [("ve", v4)])
                out["r"] = bcast_row_ap(tmp, bcB, [("se", s2)])
            if kind == "ln1":
                var = prow2[0:1, 0:TC]
                w1v = E("vector", lambda v, var=var: v.tensor_scalar(
                    strow[0:1, 0:TC], var, 1e-5, None,
                    ALU.add), waits=[out["mu"]], inc="ve")
                w2r = E("vector", lambda v: v.reciprocal(
                    strow[0:1, 0:TC], strow[0:1, 0:TC]),
                    waits=[("ve", w1v)], inc="ve")
                w2v = E("vector", lambda v, var=var: v.tensor_tensor(
                    strow[0:1, 0:TC], var,
                    strow[0:1, 0:TC], ALU.mult), waits=[("ve", w2r)],
                    inc="ve")
                s3 = E("scalar", lambda s: s.activation(
                    strow[0:1, 0:TC], strow[0:1, 0:TC], AF.Ln,
                    bias=c_misc[0:1, 1:2]), waits=[("ve", w2v)], inc="se")
                s4 = E("scalar", lambda s: s.activation(
                    strow[0:1, 0:TC], strow[0:1, 0:TC], AF.Exp, scale=-0.5),
                    waits=[("se", s3)], inc="se")
                out["rq_row"] = ("se", s4)
                pool_edge[pool.index(prow)] = out["r"]
                pool_edge[pool.index(prow2)] = ("se", s3)
            elif prow is not None:
                pool_edge[pool.index(prow)] = out["r"]
                pool_edge[pool.index(prow2)] = ("se", s1)
            elif kind == "rms":
                p7 = stats_banks["s2"]
                v2 = E("vector", lambda v, p7=p7: v.tensor_scalar(
                    strow[0:1, 0:TC], p7[0:1, :], 1.0 / D, None, ALU.mult),
                    waits=[stats_edge[0]], inc="ve")
                pm_edge[stats_banks["i1"]] = stats_edge[0]
                pm_edge[stats_banks["i2"]] = ("ve", v2)
                s1 = E("scalar", lambda s: s.activation(
                    strow[0:1, 0:TC], strow[0:1, 0:TC], AF.Ln,
                    bias=c_misc[0:1, 1:2]), waits=[("ve", v2)], inc="se")
                s2_ = E("scalar", lambda s: s.activation(
                    strow[0:1, 0:TC], strow[0:1, 0:TC], AF.Exp, scale=-0.5),
                    waits=[("se", s1)], inc="se")
                out["rho"] = bcast_row(0, bcB, [("se", s2_)])
            return out

        front_state = {}

        def emit_front1(c):
            st = {}
            front_state[c] = st
            cpar = c % 2
            ln1_sq = stats(x0b[cpar], MD, True,
                           [[x0_ready[c]] + edges.get(f"x0_{cpar}", [])
                            for _ in range(MD)])
            st["ln1_sq"] = ln1_sq
            yield
            rows = ln_rows("ln1")
            yield
            a_v = None
            for kt in range(KD):
                xs = x0b[cpar][:, kt * TC:(kt + 1) * TC]
                t1s = t1b[cpar][:, kt * TC:(kt + 1) * TC]
                E("vector", lambda v, a=t1s, b=xs: v.tensor_tensor(
                    a, b, bcA[:], ALU.subtract),
                    waits=[rows["mu"], x0_ready[c]]
                    + edges.get(f"t1_{cpar}", []), inc="ve")
                a_v = E("vector", lambda v, a=t1s: v.tensor_tensor(
                    a, a, bcB[:], ALU.mult), waits=[rows["r"]], inc="ve")
                yield
            rq_done = bcast_row(0, bcB, [rows["rq_row"], ("ve", a_v)])
            h1_v = None
            for kt in range(KD):
                t1s = t1b[cpar][:, kt * TC:(kt + 1) * TC]
                h1s = h1b[:, kt * TC:(kt + 1) * TC]
                h1_v = E("vector", lambda v, o=h1s, a=t1s, kt=kt:
                         v.scalar_tensor_tensor(
                             o, a, c_norm1[:, kt:kt + 1], bcB[:], ALU.mult,
                             ALU.mult),
                         waits=[rq_done] + edges.get("h1b", []),
                         inc="ve")
            st["h1"] = ("ve", h1_v)
            yield
            tap("a", c, t1b[cpar][:], [128, MD * TC], F32, ("ve", a_v))
            tap("h1", c, h1b[:], [128, MD * TC], BF16, ("ve", h1_v))
            conv_done = [None] * MI
            last_pe = None
            for hs in range(8):
                src = ipw_bf.rearrange("(kt p) m -> p kt m", p=128)[
                    :, :, hs * 256:(hs + 1) * 256]
                slot, wl_v = ring_load("ipw", src)
                for mt_in in range(2):
                    mi = hs * 2 + mt_in
                    pi, pmt, pmw = next_pm()
                    for kt in range(KD):
                        lhs = wring[slot][:, kt * 256 + mt_in * 128:
                                          kt * 256 + mt_in * 128 + 128]
                        rhs = h1b[:, kt * TC:(kt + 1) * TC]
                        last_pe = E("tensor", lambda t, o=pmt, l=lhs, r=rhs,
                                    st_=(kt == 0), sp=(kt == KD - 1):
                                    t.matmul(o[:], l, r, start=st_, stop=sp),
                                    waits=[wl_v, ("ve", h1_v)] + pmw,
                                    inc="pe" if kt == KD - 1 else None)
                    _, acc, accw = next_pool()
                    w_ = [("pe", last_pe), ("glc", consts_gl),
                          edges["tail0"]] + accw
                    E("vector", lambda v, a=acc, p=pmt, mi=mi:
                      v.tensor_scalar(
                          a[:], p[:], c_convw[:, mi * 4 + 3:mi * 4 + 4],
                          None, ALU.mult), waits=w_, inc="ve")
                    for k in (2, 1, 0):
                        n = 3 - k
                        E("vector", lambda v, a=acc, p=pmt, mi=mi, k=k, n=n:
                          v.scalar_tensor_tensor(
                              a[:, n:TC], p[:, 0:TC - n],
                              c_convw[:, mi * 4 + k:mi * 4 + k + 1],
                              a[:, n:TC], ALU.mult, ALU.add), inc="ve")
                        E("vector", lambda v, a=acc, mi=mi, k=k, n=n:
                          v.scalar_tensor_tensor(
                              a[:, 0:n], tailb[:, mi * 3 + k:mi * 3 + 3],
                              c_convw[:, mi * 4 + k:mi * 4 + k + 1],
                              a[:, 0:n], ALU.mult, ALU.add), inc="ve")
                    tl_v = E("vector", lambda v, p=pmt, mi=mi: v.tensor_copy(
                        tailb[:, mi * 3:mi * 3 + 3], p[:, TC - 3:TC]),
                        inc="ve")
                    cd = E("scalar", lambda s, a=acc, mi=mi: s.activation(
                        xcb[cpar][:, mi * TC:(mi + 1) * TC], a[:], AF.Silu,
                        bias=c_convb[:, mi:mi + 1]),
                        waits=[("ve", tl_v)] + edges.get(f"xc_{cpar}", []),
                        inc="se")
                    conv_done[mi] = cd
                    pm_edge[pi] = ("se", cd)
                    pool_edge[pool.index(acc)] = ("se", cd)
                ring_release(slot, ("pe", last_pe))
                yield
            edges["tail0"] = ("ve", pb.now("ve"))
            st["conv"] = conv_done
            tap("xc", c, xcb[cpar][:], [128, MI * TC], BF16,
                ("se", max(conv_done)))

        def emit_front2(c):
            st = front_state[c]
            cpar = c % 2
            z_done = [None] * MI
            last_pe = None
            for hs in range(8, 16):
                src = ipw_bf.rearrange("(kt p) m -> p kt m", p=128)[
                    :, :, hs * 256:(hs + 1) * 256]
                slot, wl_v = ring_load("ipw", src)
                for mt_in in range(2):
                    mi = (hs - 8) * 2 + mt_in
                    pi, pmt, pmw = next_pm()
                    for kt in range(KD):
                        lhs = wring[slot][:, kt * 256 + mt_in * 128:
                                          kt * 256 + mt_in * 128 + 128]
                        rhs = h1b[:, kt * TC:(kt + 1) * TC]
                        last_pe = E("tensor", lambda t, o=pmt, l=lhs, r=rhs,
                                    st_=(kt == 0), sp=(kt == KD - 1):
                                    t.matmul(o[:], l, r, start=st_, stop=sp),
                                    waits=[wl_v, st["h1"]] + pmw,
                                    inc="pe" if kt == KD - 1 else None)
                    zd = E("scalar", lambda s, p=pmt, mi=mi: s.activation(
                        zb[:, mi * TC:(mi + 1) * TC], p[:], AF.Silu),
                        waits=[("pe", last_pe)] + edges.get("zb", []),
                        inc="se")
                    z_done[mi] = zd
                    pm_edge[pi] = ("se", zd)
                ring_release(slot, ("pe", last_pe))
            st["z"] = z_done
            edges["h1b"] = [("pe", last_pe)]
            tap("z", c, zb[:], [128, MI * TC], BF16, ("se", max(z_done)))

            def xp_part(col0, mc, evict):
                pi, pmt, pmw = next_pm()
                last = None
                for kt in range(KI):
                    lhs = r_xpw[:, kt * 96 + col0:kt * 96 + col0 + mc]
                    rhs = xcb[cpar][:, kt * TC:(kt + 1) * TC]
                    last = E("tensor", lambda t, o=pmt, l=lhs, r=rhs,
                             st_=(kt == 0), sp=(kt == KI - 1), mc=mc:
                             t.matmul(o[0:mc, :], l, r, start=st_, stop=sp),
                             waits=[("se", st["conv"][kt]),
                                    ("glc", consts_gl)] + pmw,
                             inc="pe" if kt == KI - 1 else None)
                ev = evict(pmt, last)
                pm_edge[pi] = ev
                return ev
            st["dtr"] = xp_part(0, DTR, lambda p, l: (
                "se", E("scalar", lambda s, p=p: s.activation(
                    dtrb[:], p[0:DTR, :], AF.Copy),
                    waits=[("pe", l)] + edges.get("dtrb", []), inc="se")))
            st["B"] = xp_part(DTR, NS, lambda p, l: (
                "se", E("scalar", lambda s, p=p: s.activation(
                    Bsb[:], p[0:NS, :], AF.Copy),
                    waits=[("pe", l)] + edges.get("Bsb", []), inc="se")))
            st["C"] = xp_part(DTR + NS, NS, lambda p, l: (
                "se", E("scalar", lambda s, p=p: s.activation(
                    Csb[:], p[0:NS, :], AF.Copy),
                    waits=[("pe", l)] + edges.get("Bsb", []), inc="se")))
            tap("B", c, Bsb[:], [NS, TC], BF16, st["B"])
            tap("C", c, Csb[:], [NS, TC], BF16, st["C"])

        def emit_scan(c, pull):
            st = front_state[c]
            cpar = c % 2
            yz_done = [None] * MI
            for mi in range(MI):
                dts = dtb2[:, 0:TC]
                pi, pmt, pmw = next_pm()
                last = E("tensor", lambda t, o=pmt, mi=mi: t.matmul(
                    o[:], r_dtw[:, mi * 128:(mi + 1) * 128], dtrb[:],
                    start=True, stop=True),
                    waits=[st["dtr"], ("glc", consts_gl)] + pmw, inc="pe")
                _, p_av, w_av = next_pool()
                _, p_m, w_m = next_pool()
                E("vector", lambda v, a=p_av, p=pmt, mi=mi: v.tensor_scalar(
                    a[:], p[:], c_dtb[:, mi:mi + 1], 0.0, ALU.add, ALU.min),
                    waits=[("pe", last)] + w_av + w_m, inc="ve")
                E("vector", lambda v, a=p_m, p=pmt, mi=mi: v.tensor_scalar(
                    a[:], p[:], c_dtb[:, mi:mi + 1], 0.0, ALU.add, ALU.max),
                    inc="ve")
                E("vector", lambda v, a=p_av, b=p_m: v.tensor_tensor(
                    a[:], a[:], b[:], ALU.subtract), inc="ve")
                pm_edge[pi] = ("ve", pb.now("ve"))
                s_e = E("scalar", lambda s, a=p_av: s.activation(
                    a[:], a[:], AF.Exp), waits=[("ve", pb.now("ve"))],
                    inc="se")
                s_l = E("scalar", lambda s, a=p_av: s.activation(
                    a[:], a[:], AF.Ln, bias=c_misc[:, 2:3]),
                    waits=[("se", s_e)], inc="se")
                w_dt = [("se", s_l)]
                if dt_edge[0]:
                    w_dt.append(dt_edge[0])
                dt_v = E("vector", lambda v, o=dts, a=p_m, b=p_av:
                         v.tensor_tensor(o, a[:], b[:], ALU.add),
                         waits=w_dt, inc="ve")
                pool_edge[pool.index(p_av)] = ("ve", dt_v)
                pool_edge[pool.index(p_m)] = ("ve", dt_v)
                if mi == 0:
                    tap("dt0", c, dts, [128, TC], F32, ("ve", dt_v))
                w_v = E("vector", lambda v, o=wbuf, a=dts, mi=mi:
                        v.tensor_tensor(
                            o[:], a, xcb[cpar][:, mi * TC:(mi + 1) * TC],
                            ALU.mult),
                        waits=[("se", st["conv"][mi])], inc="ve")
                y_v = None
                last_dA = None
                for s_i in range(NS):
                    spar = 0
                    dAs = dAb[:, 0:TC]
                    wA = [pbA_edge[0]] if pbA_edge[0] else []
                    wB = [pbB_edge[0]] if pbB_edge[0] else []
                    bA = E("tensor", lambda t, s_i=s_i: t.matmul(
                        pbA[:], c_sel16[:, s_i * 128:(s_i + 1) * 128],
                        Bsb[:], start=True, stop=True),
                        waits=[st["B"]] + wA, inc="pe")
                    bB = E("tensor", lambda t, s_i=s_i: t.matmul(
                        pbB[:], c_sel16[:, s_i * 128:(s_i + 1) * 128],
                        Csb[:], start=True, stop=True),
                        waits=[st["C"]] + wB, inc="pe")
                    w_da = [("ve", dt_v), ("ve", vA2)]
                    if dA_edge[spar]:
                        w_da.append(dA_edge[spar])
                    dA_v = E("scalar", lambda s, o=dAs, i=dts, mi=mi,
                             s_i=s_i: s.activation(
                                 o, i, AF.Exp,
                                 scale=c_Alog[:, mi * NS + s_i:
                                              mi * NS + s_i + 1]),
                             waits=w_da, inc="se")
                    last_dA = dA_v
                    u_v = E("vector", lambda v: v.tensor_tensor(
                        ubuf[:], wbuf[:], pbA[:], ALU.mult),
                        waits=[("pe", bA), ("ve", w_v)], inc="ve")
                    pbA_edge[0] = ("ve", u_v)
                    idx = mi * NS + s_i
                    sc_v = E("vector", lambda v, a=dAs, idx=idx:
                             v.tensor_tensor_scan(
                                 hbuf[:], a, ubuf[:], carry[:, idx:idx + 1],
                                 ALU.mult, ALU.add),
                             waits=[("se", dA_v)], inc="ve")
                    dA_edge[spar] = ("ve", sc_v)
                    E("vector", lambda v, idx=idx: v.tensor_copy(
                        carry[:, idx:idx + 1], hbuf[:, TC - 1:TC]),
                        waits=[("ve", sc_v)], inc="ve")
                    if s_i == 0:
                        y_v = E("vector", lambda v: v.tensor_tensor(
                            ybuf[:], hbuf[:], pbB[:], ALU.mult),
                            waits=[("pe", bB), ("ve", sc_v)], inc="ve")
                        pbB_edge[0] = ("ve", y_v)
                    else:
                        p_v = E("vector", lambda v: v.tensor_tensor(
                            hbuf[:], hbuf[:], pbB[:], ALU.mult),
                            waits=[("pe", bB), ("ve", sc_v)], inc="ve")
                        pbB_edge[0] = ("ve", p_v)
                        y_v = E("vector", lambda v: v.tensor_tensor(
                            ybuf[:], ybuf[:], hbuf[:], ALU.add), inc="ve")
                    pull(2)
                dt_edge[0] = ("se", last_dA)
                if mi == 0:
                    tap("y0", c, ybuf[:], [128, TC], F32, ("ve", y_v))
                _, p_t, w_t = next_pool()
                E("vector", lambda v, a=p_t, mi=mi: v.scalar_tensor_tensor(
                    a[:], xcb[cpar][:, mi * TC:(mi + 1) * TC],
                    c_Dp[:, mi:mi + 1], ybuf[:], ALU.mult, ALU.add),
                    waits=w_t, inc="ve")
                yzv = E("vector", lambda v, a=p_t, mi=mi: v.tensor_tensor(
                    xcb[cpar][:, mi * TC:(mi + 1) * TC], a[:],
                    zb[:, mi * TC:(mi + 1) * TC], ALU.mult),
                    waits=[("se", st["z"][mi])], inc="ve")
                yz_done[mi] = yzv
                pool_edge[pool.index(p_t)] = ("ve", yzv)
            tap("carry", c, carry[:], [128, MI * NS], F32,
                ("ve", pb.now("ve")))
            st["yz"] = yz_done
            edges["zb"] = [("ve", max(yz_done))]
            edges["dtrb"] = [("pe", pb.now("pe"))]
            edges["Bsb"] = [("ve", pb.now("ve"))]
            tap("yz", c, xcb[cpar][:], [128, MI * TC], BF16,
                ("ve", max(yz_done)))

        def router_gates(c, rhs_buf, rhs_done, wtile, out_et, eps, tapname):
            pi, pmt, pmw = next_pm()
            last = None
            for kt in range(KD):
                lhs = wtile[:, kt * NE:(kt + 1) * NE]
                rhs = rhs_buf[:, kt * TC:(kt + 1) * TC]
                last = E("tensor", lambda t, o=pmt, l=lhs, r=rhs,
                         st_=(kt == 0), sp=(kt == KD - 1): t.matmul(
                             o[0:NE, :], l, r, start=st_, stop=sp),
                         waits=[rhs_done, ("glc", consts_gl)] + pmw,
                         inc="pe" if kt == KD - 1 else None)
            w_rl = [("pe", last)]
            if router_edge[0]:
                w_rl.append(router_edge[0])
            rl_v = E("scalar", lambda s, p=pmt: s.activation(
                RLb[:], p[0:NE, :], AF.Copy), waits=w_rl, inc="se")
            pm_edge[pi] = ("se", rl_v)
            yield
            wA = [pbA_edge[0]] if pbA_edge[0] else []
            tp = None
            for ti in range(4):
                tp = E("tensor", lambda t, ti=ti: t.transpose(
                    pbA[:, ti * 8:(ti + 1) * 8],
                    RLb[:, ti * 128:(ti + 1) * 128], c_ident[0:8, 0:8]),
                    waits=[("se", rl_v)] + wA, inc="pe")
            gt_v = E("vector", lambda v: v.tensor_copy(
                Gtb[:], pbA[:, 0:32]), waits=[("pe", tp)], inc="ve")
            pbA_edge[0] = ("ve", gt_v)
            yield
            gv = gt_v
            for ti in range(4):
                g8 = Gtb[:, ti * 8:(ti + 1) * 8]
                m1 = rsc[:, 0:1]
                m2 = rsc[:, 1:2]
                sm = rsc[:, 8:16]
                p2 = rsc[:, 16:24]
                def VS(build, extra=()):
                    nonlocal gv
                    gv = E("vector", build,
                           waits=[("ve", gv)] + list(extra), inc="ve")
                    return gv
                VS(lambda v, g8=g8, m1=m1: v.tensor_reduce(
                    m1, g8, AX.X, ALU.max))
                VS(lambda v, g8=g8, m1=m1, sm=sm: v.tensor_scalar(
                    sm, g8, m1, None, ALU.subtract))
                se_v = E("scalar", lambda s, sm=sm: s.activation(
                    sm, sm, AF.Exp), waits=[("ve", gv)], inc="se")
                VS(lambda v, sm=sm, m1=m1: v.tensor_reduce(
                    m1, sm, AX.X, ALU.add), extra=[("se", se_v)])
                VS(lambda v, m1=m1: v.reciprocal(m1, m1))
                VS(lambda v, g8=g8, sm=sm, m1=m1: v.tensor_scalar(
                    g8, sm, m1, None, ALU.mult))
                VS(lambda v, g8=g8, m1=m1: v.tensor_reduce(
                    m1, g8, AX.X, ALU.max))
                VS(lambda v, g8=g8, m1=m1, sm=sm: v.tensor_scalar(
                    sm, g8, m1, None, ALU.is_ge))
                VS(lambda v, g8=g8, sm=sm, p2=p2: v.tensor_tensor(
                    p2, g8, sm, ALU.mult))
                VS(lambda v, g8=g8, p2=p2: v.tensor_tensor(
                    p2, g8, p2, ALU.subtract))
                VS(lambda v, p2=p2, m2=m2: v.tensor_reduce(
                    m2, p2, AX.X, ALU.max))
                VS(lambda v, p2=p2, m2=m2: v.tensor_scalar(
                    p2, p2, m2, None, ALU.is_ge))
                VS(lambda v, sm=sm, p2=p2: v.tensor_tensor(
                    sm, sm, p2, ALU.add))
                VS(lambda v, m1=m1, m2=m2: v.tensor_tensor(
                    m1, m1, m2, ALU.add))
                if eps:
                    VS(lambda v, m1=m1: v.tensor_scalar(
                        m1, m1, eps, None, ALU.add))
                VS(lambda v, g8=g8, sm=sm: v.tensor_tensor(
                    g8, g8, sm, ALU.mult))
                VS(lambda v, m1=m1: v.reciprocal(m1, m1))
                VS(lambda v, g8=g8, m1=m1: v.tensor_scalar(
                    g8, g8, m1, None, ALU.mult))
            yield
            wB = [pbB_edge[0]] if pbB_edge[0] else []
            tp2 = None
            for ti in range(4):
                tp2 = E("tensor", lambda t, ti=ti: t.transpose(
                    pbB[0:8, ti * 128:(ti + 1) * 128],
                    Gtb[:, ti * 8:(ti + 1) * 8], c_ident[:]),
                    waits=[("ve", gv)] + wB, inc="pe")
            et_v = E("vector", lambda v, o=out_et: v.tensor_copy(
                o[:], pbB[0:8, :]),
                waits=[("pe", tp2)] + edges.get(out_et.name, []), inc="ve")
            pbB_edge[0] = ("ve", et_v)
            router_edge[0] = ("pe", tp2)
            tap(tapname, c, out_et[:], [NE, TC], BF16, ("ve", et_v))
            front_state[c][tapname] = ("ve", et_v)

        def tail_units(c):
            st = front_state[c]
            cpar = c % 2
            xa_done = [None] * MD
            for hs in range(8):
                src = opw_bf.rearrange("(kt p) m -> p kt m", p=128)[
                    :, :, hs * 128:(hs + 1) * 128]
                slot, wl_v = ring_load("opw", src)
                m = hs
                pi, pmt, pmw = next_pm()
                last_pe = None
                for kt in range(KI):
                    lhs = wring[slot][:, kt * 128:kt * 128 + 128]
                    rhs = xcb[cpar][:, kt * TC:(kt + 1) * TC]
                    last_pe = E("tensor", lambda t, o=pmt, l=lhs, r=rhs,
                                st_=(kt == 0), sp=(kt == KI - 1): t.matmul(
                                    o[:], l, r, start=st_, stop=sp),
                                waits=[wl_v, ("ve", st["yz"][kt])] + pmw,
                                inc="pe" if kt == KI - 1 else None)
                xv = E("vector", lambda v, p=pmt, m=m: v.tensor_tensor(
                    t1b[cpar][:, m * TC:(m + 1) * TC],
                    t1b[cpar][:, m * TC:(m + 1) * TC], p[:], ALU.add),
                    waits=[("pe", last_pe)], inc="ve")
                xa_done[m] = xv
                pm_edge[pi] = ("ve", xv)
                ring_release(slot, ("pe", last_pe))
                yield
            tap("xa", c, t1b[cpar][:], [128, MD * TC], F32,
                ("ve", max(xa_done)))
            acc_done = [None] * MD
            for m in range(MD):
                acc_done[m] = E("gpsimd", lambda g,
                                a=x0b[cpar][:, m * TC:(m + 1) * TC],
                                b=t1b[cpar][:, m * TC:(m + 1) * TC]:
                                g.tensor_tensor(a, a, b, ALU.add),
                                waits=[("ve", xa_done[m]), x0_ready[c],
                                       ("se", st["ln1_sq"])], inc="ge")
            yield
            stats(t1b[cpar], MD, False, [[("ve", xa_done[kt])]
                                         for kt in range(MD)])
            rows = ln_rows("rms")
            ht_v = None
            for kt in range(KD):
                ht_v = E("vector", lambda v, kt=kt: v.scalar_tensor_tensor(
                    htb[:, kt * TC:(kt + 1) * TC],
                    t1b[cpar][:, kt * TC:(kt + 1) * TC],
                    c_norm2[:, kt:kt + 1], bcB[:], ALU.mult, ALU.mult),
                    waits=[rows["rho"], ("ve", xa_done[kt])]
                    + edges.get("htb", []), inc="ve")
            st["ht"] = ("ve", ht_v)
            tap("ht", c, htb[:], [128, MD * TC], BF16, ("ve", ht_v))
            yield
            def w1_stage(e, key, wsrc, rhs_done, bias_c):
                hid_done = [None] * KI
                for hs in range(8):
                    src = wsrc.rearrange("e (kt p) m -> e p kt m", p=128)[
                        e][:, :, hs * 256:(hs + 1) * 256]
                    slot, wl_v = ring_load((key, e), src)
                    last_pe = None
                    for mt_in in range(2):
                        hi = hs * 2 + mt_in
                        pi, pmt, pmw = next_pm()
                        for kt in range(KD):
                            lhs = wring[slot][:, kt * 256 + mt_in * 128:
                                              kt * 256 + mt_in * 128 + 128]
                            rhs = htb[:, kt * TC:(kt + 1) * TC]
                            last_pe = E("tensor", lambda t, o=pmt, l=lhs,
                                        r=rhs, st_=(kt == 0),
                                        sp=(kt == KD - 1): t.matmul(
                                            o[:], l, r, start=st_, stop=sp),
                                        waits=[wl_v, rhs_done] + pmw,
                                        inc="pe" if kt == KD - 1 else None)
                        hv = E("scalar", lambda s, p=pmt, hi=hi, e=e:
                               s.activation(
                                   hidb[:, hi * TC:(hi + 1) * TC], p[:],
                                   AF.Identity,
                                   bias=bias_c[:, e * MI + hi:
                                               e * MI + hi + 1]),
                               waits=[("pe", last_pe)]
                               + edges.get("hidb", []), inc="se")
                        hid_done[hi] = hv
                        pm_edge[pi] = ("se", hv)
                    ring_release(slot, ("pe", last_pe))
                    yield
                hid_ret[0] = hid_done

            hid_ret = [None]
            w1_e0 = w1_stage(0, "w1", w1_bf, ("ve", ht_v), c_b1)
            for _ in w1_e0:
                yield
            hid_e0 = hid_ret[0]
            for _ in router_gates(c, htb, ("ve", ht_v), r_rtw, Get, 0.0,
                                  "gates"):
                yield
            get_done = st["gates"]
            for e in range(NE):
                wA = [pbA_edge[0]] if pbA_edge[0] else []
                gb_v = E("tensor", lambda t, e=e: t.matmul(
                    pbA[:], c_sel16[0:8, e * 128:(e + 1) * 128], Get[:],
                    start=True, stop=True), waits=[get_done] + wA, inc="pe")
                gs_v = E("vector", lambda v: v.tensor_copy(Gsb[:], pbA[:]),
                         waits=[("pe", gb_v)] + edges.get("Gsb", []),
                         inc="ve")
                pbA_edge[0] = ("ve", gs_v)
                edges["Gsb"] = [("ve", gs_v)]
                if e == 0:
                    hid_done = hid_e0
                else:
                    for _ in w1_stage(e, "w1", w1_bf, ("ve", ht_v), c_b1):
                        yield
                    hid_done = hid_ret[0]
                gelu_h1 = E("scalar", lambda s: s.activation(
                    hidb[:, 0:8 * TC], hidb[:, 0:8 * TC],
                    AF.Gelu_apprx_tanh),
                    waits=[("se", max(hid_done[0:8]))], inc="se")
                gelu_v = E("scalar", lambda s: s.activation(
                    hidb[:, 8 * TC:16 * TC], hidb[:, 8 * TC:16 * TC],
                    AF.Gelu_apprx_tanh),
                    waits=[("se", max(hid_done))], inc="se")
                yield
                for hs in range(8):
                    src = w2_bf.rearrange("e (kt p) m -> e p kt m", p=128)[
                        e][:, :, hs * 128:(hs + 1) * 128]
                    slot, wl_v = ring_load(("w2", e), src)
                    m = hs
                    pi, pmt, pmw = next_pm()
                    last_pe = None
                    for kt in range(KI):
                        lhs = wring[slot][:, kt * 128:kt * 128 + 128]
                        rhs = hidb[:, kt * TC:(kt + 1) * TC]
                        gw_ = ("se", gelu_h1 if kt < 8 else gelu_v)
                        last_pe = E("tensor", lambda t, o=pmt, l=lhs, r=rhs,
                                    st_=(kt == 0), sp=(kt == KI - 1):
                                    t.matmul(o[:], l, r, start=st_, stop=sp),
                                    waits=[wl_v, gw_] + pmw,
                                    inc="pe" if kt == KI - 1 else None)
                    _, p_t, w_t = next_pool()
                    sv = E("vector", lambda v, a=p_t, p=pmt, m=m, e=e:
                           v.scalar_tensor_tensor(
                               a[:], p[:],
                               c_b2[:, e * MD + m:e * MD + m + 1],
                               Gsb[:], ALU.add, ALU.mult),
                           waits=[("pe", last_pe), ("ve", gs_v)] + w_t,
                           inc="ve")
                    pm_edge[pi] = ("ve", sv)
                    gv2 = E("gpsimd", lambda g, a=p_t,
                            o=x0b[cpar][:, m * TC:(m + 1) * TC]:
                            g.tensor_tensor(o, o, a[:], ALU.add),
                            waits=[("ve", sv), ("ge", acc_done[m])],
                            inc="ge")
                    acc_done[m] = gv2
                    pool_edge[pool.index(p_t)] = ("ge", gv2)
                    ring_release(slot, ("pe", last_pe))
                    yield
            edges["htb"] = [("pe", pb.now("pe"))]
            edges["hidb"] = [("pe", pb.now("pe"))]
            x1_done = [("ge", v) for v in acc_done]
            tap("x1", c, x0b[cpar][:], [128, MD * TC], F32, x1_done[-1])
            ln2_sq = stats(x0b[cpar], MD, True, [[x1_done[kt]]
                                                 for kt in range(MD)])
            rows2 = ln_rows("ln2")
            b2_v = None
            for kt in range(MD):
                _, p_t, w_t = next_pool()
                E("vector", lambda v, a=p_t,
                  b=x0b[cpar][:, kt * TC:(kt + 1) * TC]: v.tensor_tensor(
                      a[:], b, bcA[:], ALU.subtract),
                  waits=[rows2["mu"], x1_done[kt]] + w_t, inc="ve")
                b2_v = E("vector", lambda v, a=p_t, kt=kt: v.tensor_tensor(
                    htb[:, kt * TC:(kt + 1) * TC], a[:], bcB[:], ALU.mult),
                    waits=[rows2["r"]] + edges.get("htb", []), inc="ve")
                pool_edge[pool.index(p_t)] = ("ve", b2_v)
            tap("b2nd", c, htb[:], [128, MD * TC], BF16, ("ve", b2_v))
            yield
            sg_e0_gen = w1_stage(0, "gw", gw_bf, ("ve", b2_v), c_gb)
            for _ in sg_e0_gen:
                yield
            sg_e0 = hid_ret[0]
            for _ in router_gates(c, htb, ("ve", b2_v), r_frw, Pet, 1e-9,
                                  "pgates"):
                yield
            pet_done = st["pgates"]
            facc_done = list(x1_done)
            for e in range(NE):
                wA = [pbA_edge[0]] if pbA_edge[0] else []
                pb_v = E("tensor", lambda t, e=e: t.matmul(
                    pbA[:], c_sel16[0:8, e * 128:(e + 1) * 128], Pet[:],
                    start=True, stop=True), waits=[pet_done] + wA, inc="pe")
                gs_v = E("vector", lambda v: v.tensor_copy(Gsb[:], pbA[:]),
                         waits=[("pe", pb_v)] + edges.get("Gsb", []),
                         inc="ve")
                pbA_edge[0] = ("ve", gs_v)
                edges["Gsb"] = [("ve", gs_v)]
                if e == 0:
                    sg_done = sg_e0
                else:
                    for _ in w1_stage(e, "gw", gw_bf, ("ve", b2_v), c_gb):
                        yield
                    sg_done = hid_ret[0]
                silu_h1 = E("scalar", lambda s: s.activation(
                    hidb[:, 0:8 * TC], hidb[:, 0:8 * TC], AF.Silu),
                    waits=[("se", max(sg_done[0:8]))], inc="se")
                silu_v = E("scalar", lambda s: s.activation(
                    hidb[:, 8 * TC:16 * TC], hidb[:, 8 * TC:16 * TC],
                    AF.Silu),
                    waits=[("se", max(sg_done))], inc="se")
                yield
                h2_done = [None] * KI
                for hs in range(8):
                    src = uw_bf.rearrange("e (kt p) m -> e p kt m", p=128)[
                        e][:, :, hs * 256:(hs + 1) * 256]
                    slot, wl_v = ring_load(("uw", e), src)
                    last_pe = None
                    for mt_in in range(2):
                        fi = hs * 2 + mt_in
                        pi, pmt, pmw = next_pm()
                        for kt in range(KD):
                            lhs = wring[slot][:, kt * 256 + mt_in * 128:
                                              kt * 256 + mt_in * 128 + 128]
                            rhs = htb[:, kt * TC:(kt + 1) * TC]
                            last_pe = E("tensor", lambda t, o=pmt, l=lhs,
                                        r=rhs, st_=(kt == 0),
                                        sp=(kt == KD - 1): t.matmul(
                                            o[:], l, r, start=st_, stop=sp),
                                        waits=[wl_v, ("ve", b2_v)] + pmw,
                                        inc="pe" if kt == KD - 1 else None)
                        sw_ = ("se", silu_h1 if fi < 8 else silu_v)
                        hv = E("vector", lambda v, p=pmt, fi=fi, e=e:
                               v.scalar_tensor_tensor(
                                   hidb[:, fi * TC:(fi + 1) * TC], p[:],
                                   c_ub[:, e * MI + fi:e * MI + fi + 1],
                                   hidb[:, fi * TC:(fi + 1) * TC],
                                   ALU.add, ALU.mult),
                               waits=[("pe", last_pe), sw_],
                               inc="ve")
                        h2_done[fi] = hv
                        pm_edge[pi] = ("ve", hv)
                    ring_release(slot, ("pe", last_pe))
                    yield
                for hs in range(8):
                    src = dw_bf.rearrange("e (kt p) m -> e p kt m", p=128)[
                        e][:, :, hs * 128:(hs + 1) * 128]
                    slot, wl_v = ring_load(("dw", e), src)
                    m = hs
                    pi, pmt, pmw = next_pm()
                    last_pe = None
                    for kt in range(KI):
                        lhs = wring[slot][:, kt * 128:kt * 128 + 128]
                        rhs = hidb[:, kt * TC:(kt + 1) * TC]
                        last_pe = E("tensor", lambda t, o=pmt, l=lhs, r=rhs,
                                    st_=(kt == 0), sp=(kt == KI - 1):
                                    t.matmul(o[:], l, r, start=st_, stop=sp),
                                    waits=[wl_v, ("ve", h2_done[kt])] + pmw,
                                    inc="pe" if kt == KI - 1 else None)
                    _, p_t, w_t = next_pool()
                    sv = E("vector", lambda v, a=p_t, p=pmt, m=m, e=e:
                           v.scalar_tensor_tensor(
                               a[:], p[:],
                               c_db[:, e * MD + m:e * MD + m + 1],
                               Gsb[:], ALU.add, ALU.mult),
                           waits=[("pe", last_pe), ("ve", gs_v)] + w_t,
                           inc="ve")
                    pm_edge[pi] = ("ve", sv)
                    gv2 = E("gpsimd", lambda g, a=p_t,
                            o=x0b[cpar][:, m * TC:(m + 1) * TC]:
                            g.tensor_tensor(o, o, a[:], ALU.add),
                            waits=[("ve", sv), facc_done[m], ("ve", b2_v),
                                   ("se", ln2_sq)], inc="ge")
                    facc_done[m] = ("ge", gv2)
                    pool_edge[pool.index(p_t)] = ("ge", gv2)
                    ring_release(slot, ("pe", last_pe))
                    yield
            edges["hidb"] = [("pe", pb.now("pe"))]
            edges["htb"] = [("pe", pb.now("pe"))]
            out_done = pb.now("ge")
            dst = out.rearrange("(kt p) t -> p kt t", p=128)[
                :, :, c * TC:(c + 1) * TC]
            srcv = x0b[cpar][:].rearrange("p (kt t) -> p kt t", t=TC)
            st_sem = pb.sem(f"sl{cpar}")
            st_v = E("gpsimd", lambda g, d=dst, s=srcv: g.dma_start(d, s),
                     waits=[("ge", out_done), x0_ready[c]], inc=st_sem,
                     inck=16)
            if c + 2 < NCH:
                x0_ready[c + 2] = x0_load(c + 2, waits=[(st_sem, st_v)])
            edges[f"x0_{cpar}"] = [(st_sem, st_v)]
            edges[f"t1_{cpar}"] = [("se", ln2_sq), ("ve", b2_v),
                                   x1_done[-1]]
            edges[f"xc_{cpar}"] = [("pe", pb.now("pe"))]
            edges[Get.name] = [("pe", pb.now("pe"))]
            edges[Pet.name] = [("pe", pb.now("pe"))]
            del front_state[c]
            yield

        def chain2(g1, g2):
            if g1 is not None:
                for x in g1:
                    yield x
            if g2 is not None:
                for x in g2:
                    yield x

        gens = {}
        for _ in emit_front1(0):
            pass
        for it in range(NCH + 1):
            if it < NCH:
                emit_front2(it)
                ch = chain2(gens.pop(it - 1, None),
                            emit_front1(it + 1) if it + 1 < NCH else None)
                def pull(k, ch=ch):
                    if not interleave:
                        return
                    for _ in range(k):
                        if next(ch, StopIteration) is StopIteration:
                            break
                emit_scan(it, pull)
                for _ in ch:
                    pass
                gens[it] = tail_units(it)
            else:
                for _ in gens.pop(it - 1):
                    pass

        fin = [(s, pb.now(s)) for s in ("sl0", "sl1", "tl") if s in pb.semc]
        E("gpsimd", lambda g: g.nop(), waits=fin)

        with nc.Block() as block:
            @block.sync
            def _(e):
                for f in pb.order["sync"]:
                    f(e)

            @block.tensor
            def _(e):
                for f in pb.order["tensor"]:
                    f(e)

            @block.vector
            def _(e):
                for f in pb.order["vector"]:
                    f(e)

            @block.scalar
            def _(e):
                for f in pb.order["scalar"]:
                    f(e)

            @block.gpsimd
            def _(e):
                for f in pb.order["gpsimd"]:
                    f(e)

    return nc, taps


def _host_inputs(inputs, b):
    f = np.ascontiguousarray
    ip = inputs
    m = {
        "xT": f(np.asarray(ip["L0_attn_x"][b]).T),
        "ipw": f(ip["L0_attn_in_proj_w"]),
        "opw": f(ip["L0_attn_out_proj_w"]),
        "xpw": f(ip["L0_attn_x_proj_w"]),
        "dtw": f(ip["L0_attn_dt_proj_w"]),
        "w1": f(ip["L0_attn_w1"]),
        "w2": f(ip["L0_attn_w2"]),
        "gw": f(np.asarray(ip["L0_ffn_gate_w"])),
        "uw": f(np.asarray(ip["L0_ffn_up_w"])),
        "dw": f(np.asarray(ip["L0_ffn_down_w"])),
        "rtw": f(ip["L0_attn_router_w"]),
        "frw": f(np.asarray(ip["L0_ffn_router_w"]).T),
        "norm1c": f(np.asarray(ip["L0_attn_norm1_w"]).reshape(8, 128).T),
        "norm2c": f(np.asarray(ip["L0_attn_norm2_w"]).reshape(8, 128).T),
        "convw_c": f(np.asarray(ip["L0_attn_conv_w"]).reshape(16, 128, 4)
                     .transpose(1, 0, 2).reshape(128, 64)),
        "convb_c": f(np.asarray(ip["L0_attn_conv_b"]).reshape(16, 128).T),
        "dtb_c": f(np.asarray(ip["L0_attn_dt_proj_b"]).reshape(16, 128).T),
        "Dp_c": f(np.asarray(ip["L0_attn_D_param"]).reshape(16, 128).T),
        "Alog_c": f(np.asarray(ip["L0_attn_A_log"]).reshape(16, 128, 16)
                    .transpose(1, 0, 2).reshape(128, 16 * 16)),
        "b1_c": f(np.asarray(ip["L0_attn_b1"]).reshape(8, 16, 128)
                  .transpose(2, 0, 1).reshape(128, 8 * 16)),
        "b2_c": f(np.asarray(ip["L0_attn_b2"]).reshape(8, 8, 128)
                  .transpose(2, 0, 1).reshape(128, 8 * 8)),
        "gb_c": f(np.asarray(ip["L0_ffn_gate_b"]).reshape(8, 16, 128)
                  .transpose(2, 0, 1).reshape(128, 8 * 16)),
        "ub_c": f(np.asarray(ip["L0_ffn_up_b"]).reshape(8, 16, 128)
                  .transpose(2, 0, 1).reshape(128, 8 * 16)),
        "db_c": f(np.asarray(ip["L0_ffn_down_b"]).reshape(8, 8, 128)
                  .transpose(2, 0, 1).reshape(128, 8 * 8)),
        "ident_c": np.eye(128, dtype=np.float32),
        "misc_c": np.tile(np.array([[1e-5, 1e-6, 1.0, 0.0]], np.float32),
                          (128, 1)),
    }
    import ml_dtypes
    sel = np.zeros((16, 16, 128), np.float32)
    for s in range(16):
        sel[s, s, :] = 1.0
    m["sel16_c"] = f(sel.transpose(1, 0, 2).reshape(16, 16 * 128)
                     .astype(ml_dtypes.bfloat16))
    m["onescol_c"] = np.ones((128, 1), np.float32).astype(ml_dtypes.bfloat16)
    m["onesrow_c"] = np.ones((1, 128), np.float32).astype(ml_dtypes.bfloat16)
    for k in m:
        if m[k].dtype == np.float64:
            m[k] = m[k].astype(np.float32)
    return m


def kernel(**inputs):
    nc, _ = build_program(DEBUG_CHUNKS)
    in_maps = [_host_inputs(inputs, b) for b in range(N_CORES)]
    res = run_bass_kernel_spmd(nc, in_maps, core_ids=list(range(N_CORES)))
    outs = [np.ascontiguousarray(np.asarray(r["out"]).T)
            for r in res.results]
    return np.stack(outs, 0)


# version 09
# speedup vs baseline: 1.3324x; 1.0293x over previous
import contextlib
import numpy as np
import concourse.bass as bass
import concourse.mybir as mybir
from concourse.bass_utils import run_bass_kernel_spmd

F32 = mybir.dt.float32
BF16 = mybir.dt.bfloat16
AF = mybir.ActivationFunctionType
ALU = mybir.AluOpType
AX = mybir.AxisListType

D = 1024
DIN = 2048
T = 4096
TC = 512
NCH = T // TC
NS = 16
DTR = 64
NE = 8
KD = D // 128
KI = DIN // 128
MD = D // 128
MI = DIN // 128

N_CORES = 8
DEBUG_CHUNKS = ()


class PB:

    ENGS = ("tensor", "vector", "scalar", "gpsimd", "sync")

    def __init__(self, nc, stack, self_chain=(), dedup=True):
        self.nc = nc
        self.stack = stack
        self.dedup = dedup
        self.order = {e: [] for e in self.ENGS}
        self.semh = {}
        self.semc = {}
        for name in ("pe", "ve", "se", "ge"):
            self.sem(name)
        self.waited = {e: {} for e in self.ENGS}
        self.self_chain = {"tensor": "pe", "vector": "ve", "scalar": "se",
                           "gpsimd": "ge"}
        self.self_chain = {k: v for k, v in self.self_chain.items()
                           if k in self_chain}

    def sem(self, name):
        if name not in self.semh:
            self.semh[name] = self.stack.enter_context(self.nc.semaphore(name))
            self.semc[name] = 0
        return name

    def emit(self, eng, build, waits=(), inc=None, inck=1):
        chain = self.self_chain.get(eng)
        if chain is not None:
            waits = list(waits) + [(chain, self.semc[chain])]
            if inc is None:
                inc = chain
        w = []
        for (s, v) in waits:
            if v is None or v <= 0:
                continue
            if self.dedup and self.waited[eng].get(s, 0) >= v:
                continue
            if self.waited[eng].get(s, 0) < v:
                self.waited[eng][s] = v
            if not any(h is self.semh[s] and vv >= v for (h, vv) in w):
                w.append((self.semh[s], v))
        val = None
        if inc is not None:
            self.semc[inc] += inck
            val = self.semc[inc]
            hinc = self.semh[inc]
        w = tuple(w)
        if inc is None:
            def thunk(e, _w=w, _b=build):
                for (h, v) in _w:
                    e.wait_ge(h, v)
                _b(e)
        else:
            def thunk(e, _w=w, _b=build, _h=hinc, _k=inck):
                for (h, v) in _w:
                    e.wait_ge(h, v)
                _b(e).then_inc(_h, _k)
        self.order[eng].append(thunk)
        return val

    def now(self, sem):
        return self.semc.get(sem, 0)


def build_program(debug_chunks=(), sim_edges=(), interleave=True):
    nc = bass.Bass("TRN2", target_bir_lowering=False)

    def din(name, shape, dtype=F32):
        return nc.dram_tensor(name, list(shape), dtype,
                              kind="ExternalInput").ap()

    xT = din("xT", [D, T])
    ipw = din("ipw", [D, 2 * DIN])
    opw = din("opw", [DIN, D])
    xpw = din("xpw", [DIN, 96])
    dtw = din("dtw", [DTR, DIN])
    w1 = din("w1", [NE, D, DIN])
    w2 = din("w2", [NE, DIN, D])
    gw = din("gw", [NE, D, DIN])
    uw = din("uw", [NE, D, DIN])
    dw = din("dw", [NE, DIN, D])
    rtw = din("rtw", [D, NE])
    frw = din("frw", [D, NE])
    norm1c = din("norm1c", [128, KD])
    norm2c = din("norm2c", [128, KD])
    convw_c = din("convw_c", [128, MI * 4])
    convb_c = din("convb_c", [128, MI])
    dtb_c = din("dtb_c", [128, MI])
    Dp_c = din("Dp_c", [128, MI])
    Alog_c = din("Alog_c", [128, MI * NS])
    b1_c = din("b1_c", [128, NE * MI])
    b2_c = din("b2_c", [128, NE * MD])
    gb_c = din("gb_c", [128, NE * MI])
    ub_c = din("ub_c", [128, NE * MI])
    db_c = din("db_c", [128, NE * MD])
    sel16_c = din("sel16_c", [16, 16 * 128], BF16)
    ident_c = din("ident_c", [128, 128])
    onescol_c = din("onescol_c", [128, 1], BF16)
    onesrow_c = din("onesrow_c", [1, 128], BF16)
    misc_c = din("misc_c", [128, 4])

    out = nc.dram_tensor("out", [D, T], F32, kind="ExternalOutput").ap()

    ipw_bf = nc.dram_tensor("ipw_bf", [D, 2 * DIN], BF16, kind="Internal").ap()
    opw_bf = nc.dram_tensor("opw_bf", [DIN, D], BF16, kind="Internal").ap()
    w1_bf = nc.dram_tensor("w1_bf", [NE, D, DIN], BF16, kind="Internal").ap()
    w2_bf = nc.dram_tensor("w2_bf", [NE, DIN, D], BF16, kind="Internal").ap()
    gw_bf = nc.dram_tensor("gw_bf", [NE, D, DIN], BF16, kind="Internal").ap()
    uw_bf = nc.dram_tensor("uw_bf", [NE, D, DIN], BF16, kind="Internal").ap()
    dw_bf = nc.dram_tensor("dw_bf", [NE, DIN, D], BF16, kind="Internal").ap()

    taps = {}

    with contextlib.ExitStack() as stack:
        def sb(name, shape, dtype=F32):
            return stack.enter_context(nc.sbuf_tensor(name, list(shape),
                                                      dtype))
        def psum(name, shape, dtype=F32):
            return stack.enter_context(nc.psum_tensor(name, list(shape),
                                                      dtype))

        c_norm1 = sb("c_norm1", [128, KD])
        c_norm2 = sb("c_norm2", [128, KD])
        c_convw = sb("c_convw", [128, MI * 4])
        c_convb = sb("c_convb", [128, MI])
        c_dtb = sb("c_dtb", [128, MI])
        c_Dp = sb("c_Dp", [128, MI])
        c_Alog = sb("c_Alog", [128, MI * NS])
        c_b1 = sb("c_b1", [128, NE * MI])
        c_b2 = sb("c_b2", [128, NE * MD])
        c_gb = sb("c_gb", [128, NE * MI])
        c_ub = sb("c_ub", [128, NE * MI])
        c_db = sb("c_db", [128, NE * MD])
        c_sel16 = sb("c_sel16", [16, 16 * 128], BF16)
        c_ident = sb("c_ident", [128, 128])
        c_onescol = sb("c_onescol", [128, 1], BF16)
        c_onesrow = sb("c_onesrow", [1, 128], BF16)
        c_misc = sb("c_misc", [128, 4])

        r_xpw = sb("r_xpw", [128, KI * 96], BF16)
        r_dtw = sb("r_dtw", [DTR, DIN], BF16)
        r_rtw = sb("r_rtw", [128, KD * NE], BF16)
        r_frw = sb("r_frw", [128, KD * NE], BF16)

        NRING = 3
        wring = [sb(f"wring{i}", [128, 2048], BF16) for i in range(NRING)]

        x0b = [sb(f"x0_{i}", [128, MD * TC]) for i in range(2)]
        t1b = [sb(f"t1_{i}", [128, MD * TC]) for i in range(2)]
        h1b = sb("h1b", [128, MD * TC], BF16)
        htb = sb("htb", [128, MD * TC], BF16)
        zb = sb("zb", [128, MI * TC], BF16)
        xcb = [sb(f"xc_{i}", [128, MI * TC], BF16) for i in range(2)]
        tailb = sb("tailb", [128, MI * 3], BF16)
        dtrb = sb("dtrb", [DTR, TC], BF16)
        Bsb = sb("Bsb", [NS, TC], BF16)
        Csb = sb("Csb", [NS, TC], BF16)
        dtb2 = sb("dtb2", [128, TC])
        wbuf = sb("wbuf", [128, TC])
        dAb = sb("dAb", [128, TC])
        ubuf = sb("ubuf", [128, TC], BF16)
        hbuf = sb("hbuf", [128, TC], BF16)
        ybuf = sb("ybuf", [128, TC])
        carry = sb("carry", [128, MI * NS])
        hidb = sb("hidb", [128, KI * TC], BF16)
        Gsb = sb("Gsb", [128, TC], BF16)
        Get = sb("Get", [NE, TC], BF16)
        Pet = sb("Pet", [NE, TC], BF16)
        Gtb = sb("Gtb", [128, 32])
        RLb = sb("RLb", [NE, TC])
        rsc = sb("rsc", [128, 32])
        strow = sb("strow", [1, TC])
        stbf = sb("stbf", [1, TC], BF16)
        stsrc = sb("stsrc", [128, 2 * TC], BF16)
        bcA = sb("bcA", [128, TC], BF16)
        bcB = sb("bcB", [128, TC], BF16)
        pool = [sb(f"pool{i}", [128, TC]) for i in range(4)]

        pm = [psum(f"pm{i}", [128, TC]) for i in range(6)]
        pbA = psum("pbA", [128, TC])
        pbB = psum("pbB", [128, TC])

        pb = PB(nc, stack, self_chain=sim_edges, dedup=not sim_edges)
        E = pb.emit

        pm_edge = [None] * 6
        pm_i = [0]
        pbA_edge = [None]
        pbB_edge = [None]
        pool_edge = [None] * len(pool)
        pool_i = [0]
        stats_edge = [None]
        router_edge = [None]
        dA_edge = [None, None]
        dt_edge = [None]
        edges = {}

        def next_pm():
            i = pm_i[0] % 6
            pm_i[0] += 1
            return i, pm[i], ([pm_edge[i]] if pm_edge[i] else [])

        def next_pool():
            i = pool_i[0] % len(pool)
            pool_i[0] += 1
            return i, pool[i], ([pool_edge[i]] if pool_edge[i] else [])

        def gload(dst_ap, src_ap, sem="glc"):
            return E("gpsimd", lambda g, d=dst_ap, s=src_ap: g.dma_start(d, s),
                     inc=pb.sem(sem), inck=16)

        for dst, src in [
            (c_norm1[:], norm1c), (c_norm2[:], norm2c), (c_convw[:], convw_c),
            (c_convb[:], convb_c), (c_dtb[:], dtb_c), (c_Dp[:], Dp_c),
            (c_Alog[:], Alog_c), (c_b1[:], b1_c), (c_b2[:], b2_c),
            (c_gb[:], gb_c), (c_ub[:], ub_c), (c_db[:], db_c),
            (c_sel16[:], sel16_c), (c_ident[:], ident_c),
            (c_onescol[:], onescol_c), (c_onesrow[:], onesrow_c),
            (c_misc[:], misc_c),
        ]:
            gload(dst, src)
        gload(r_xpw[:].rearrange("p (kt m) -> p kt m", m=96),
              xpw.rearrange("(kt p) m -> p kt m", p=128))
        gload(r_dtw[:], dtw)
        gload(r_rtw[:].rearrange("p (kt m) -> p kt m", m=NE),
              rtw.rearrange("(kt p) m -> p kt m", p=128))
        gload(r_frw[:].rearrange("p (kt m) -> p kt m", m=NE),
              frw.rearrange("(kt p) m -> p kt m", p=128))

        def x0_load(c, waits=()):
            dst = x0b[c % 2][:].rearrange("p (kt t) -> p kt t", t=TC)
            src = xT.rearrange("(kt p) t -> p kt t", p=128)[
                :, :, c * TC:(c + 1) * TC]
            sem = pb.sem(f"glx{c % 2}")
            v = E("gpsimd", lambda g, d=dst, s=src: g.dma_start(d, s),
                  waits=waits, inc=sem, inck=16)
            return (sem, v)
        x0_ready = [(pb.sem(f"glx{c % 2}"), 16 * (c // 2 + 1))
                    for c in range(NCH)]
        x0_load(0)
        x0_load(1)
        consts_gl = pb.now("glc")

        prep = {}
        def prep_dma(key, dst_ap, src_ap):
            sem = pb.sem(f"gp{len(prep)}")
            v = E("gpsimd", lambda g, d=dst_ap, s=src_ap: g.dma_start(d, s),
                  inc=sem, inck=16)
            prep[key] = (sem, v)
        prep_dma("ipw", ipw_bf, ipw)
        prep_dma("opw", opw_bf, opw)
        for e in range(NE):
            prep_dma(("w1", e), w1_bf[e], w1[e])
            prep_dma(("w2", e), w2_bf[e], w2[e])
        for e in range(NE):
            prep_dma(("gw", e), gw_bf[e], gw[e])
            prep_dma(("uw", e), uw_bf[e], uw[e])
        for e in range(NE):
            prep_dma(("dw", e), dw_bf[e], dw[e])

        vA = E("scalar", lambda s: s.activation(c_Alog[:], c_Alog[:], AF.Exp),
               waits=[("glc", consts_gl)], inc="se")
        vA2 = E("vector", lambda v: v.tensor_scalar(
            c_Alog[:], c_Alog[:], -1.0, None, ALU.mult),
            waits=[("se", vA)], inc="ve")
        E("vector", lambda v: v.memset(carry[:], 0.0), inc="ve")
        vz2 = E("vector", lambda v: v.memset(tailb[:], 0.0), inc="ve")
        edges["tail0"] = ("ve", vz2)

        ring_state = {"n": 0, "slot_edge": [None] * NRING}
        for _i in range(NRING):
            pb.sem(f"wl{_i}")

        def ring_load(key, src_view):
            slot = ring_state["n"] % NRING
            sem = f"wl{slot}"
            waits = []
            if key in prep:
                waits.append(prep[key])
            if ring_state["slot_edge"][slot] is not None:
                waits.append(ring_state["slot_edge"][slot])
                waits.append((sem, pb.now(sem)))
            dst = wring[slot][:].rearrange(
                "p (kt m) -> p kt m", m=src_view.shape[2])
            wl = E("sync", lambda sy, d=dst, s=src_view: sy.dma_start(d, s),
                   waits=waits, inc=sem, inck=16)
            ring_state["n"] += 1
            return slot, (sem, wl)

        def ring_release(slot, edge):
            ring_state["slot_edge"][slot] = edge

        def tap(name, c, src_ap, shape, dtype, wait):
            if c not in debug_chunks:
                return
            key = f"tap_{name}_{c}"
            t = nc.dram_tensor(key, list(shape), dtype,
                               kind="ExternalOutput").ap()
            taps[key] = (name, c)
            v = E("gpsimd", lambda g, d=t, s=src_ap: g.dma_start(d, s),
                  waits=[wait], inc=pb.sem("tl"), inck=16)
            for eng in ("tensor", "vector", "scalar", "gpsimd"):
                E(eng, lambda x: x.nop(), waits=[("tl", v)])

        stats_banks = {}

        def stats(xbuf, nkt, need_mean, src_waits):
            sv = None
            i6, p6, w6 = next_pm()
            i7, p7, w7 = next_pm()
            stats_banks["s1"], stats_banks["s2"] = p6, p7
            stats_banks["i1"], stats_banks["i2"] = i6, i7
            for kt in range(nkt):
                xs = xbuf[:, kt * TC:(kt + 1) * TC]
                w_ = list(src_waits[kt]) if src_waits else []
                if stats_edge[0]:
                    w_.append(stats_edge[0])
                cv = None
                if need_mean:
                    cv = E("vector", lambda v, a=xs: v.tensor_copy(
                        stsrc[:, 0:TC], a), waits=w_, inc="ve")
                sv = E("scalar", lambda s, a=xs: s.activation(
                    stsrc[:, TC:2 * TC], a, AF.Square), waits=w_, inc="se")
                st_ = (kt == 0)
                sp = (kt == nkt - 1)
                if need_mean:
                    E("tensor", lambda t, st_=st_, sp=sp, p6=p6: t.matmul(
                        p6[0:1, :], c_onescol[:, 0:1], stsrc[:, 0:TC],
                        start=st_, stop=sp),
                        waits=[("ve", cv), ("glc", consts_gl)] + w6,
                        inc="pe")
                sq_v = E("tensor", lambda t, st_=st_, sp=sp, p7=p7: t.matmul(
                    p7[0:1, :], c_onescol[:, 0:1], stsrc[:, TC:2 * TC],
                    start=st_, stop=sp),
                    waits=[("se", sv), ("glc", consts_gl)] + w7, inc="pe")
                stats_edge[0] = ("pe", sq_v)
                w6 = w7 = []
            return sv

        stbf_edge = [None]

        def bcast_row_ap(src_row, target, extra_waits):
            w_ = list(extra_waits)
            if stbf_edge[0]:
                w_.append(stbf_edge[0])
            cb = E("vector", lambda v, s=src_row: v.tensor_copy(
                stbf[0:1, :], s), waits=w_, inc="ve")
            bv = E("tensor", lambda t: t.matmul(
                pbA[:], c_onesrow[0:1, :], stbf[0:1, :], start=True,
                stop=True), waits=[("ve", cb)]
                + ([pbA_edge[0]] if pbA_edge[0] else []), inc="pe")
            stbf_edge[0] = ("pe", bv)
            ev = E("vector", lambda v, o=target: v.tensor_copy(o[:], pbA[:]),
                   waits=[("pe", bv)], inc="ve")
            pbA_edge[0] = ("ve", ev)
            return ("ve", ev)

        def bcast_row(src_is_strow_off, target, extra_waits):
            w_ = list(extra_waits)
            if stbf_edge[0]:
                w_.append(stbf_edge[0])
            cb = E("vector", lambda v: v.tensor_copy(
                stbf[0:1, :], strow[0:1, src_is_strow_off:
                                    src_is_strow_off + TC]),
                waits=w_, inc="ve")
            bv = E("tensor", lambda t: t.matmul(
                pbA[:], c_onesrow[0:1, :], stbf[0:1, :], start=True,
                stop=True), waits=[("ve", cb)]
                + ([pbA_edge[0]] if pbA_edge[0] else []), inc="pe")
            stbf_edge[0] = ("pe", bv)
            ev = E("vector", lambda v, o=target: v.tensor_copy(o[:], pbA[:]),
                   waits=[("pe", bv)], inc="ve")
            pbA_edge[0] = ("ve", ev)
            return ("ve", ev)

        def ln_rows(kind):
            out = {}
            prow = prow2 = None
            if kind in ("ln1", "ln2"):
                _, prow, wrow = next_pool()
                _, prow2, wrow2 = next_pool()
                tmp = prow[0:1, 0:TC]
                var = prow2[0:1, 0:TC]
                p6, p7 = stats_banks["s1"], stats_banks["s2"]
                v1 = E("vector", lambda v, p6=p6: v.tensor_scalar(
                    strow[0:1, 0:TC], p6[0:1, :], 1.0 / D, None, ALU.mult),
                    waits=[stats_edge[0]], inc="ve")
                v2b = E("vector", lambda v, var=var, p7=p7: v.tensor_scalar(
                    var, p7[0:1, :], 1.0 / D, None,
                    ALU.mult), waits=wrow2, inc="ve")
                pm_edge[stats_banks["i1"]] = ("ve", v1)
                pm_edge[stats_banks["i2"]] = ("ve", v2b)
                v3 = E("vector", lambda v, tmp=tmp: v.tensor_tensor(
                    tmp, strow[0:1, 0:TC],
                    strow[0:1, 0:TC], ALU.mult),
                    waits=[("ve", v1)] + wrow, inc="ve")
                v4 = E("vector", lambda v, tmp=tmp, var=var: v.tensor_tensor(
                    var, var,
                    tmp, ALU.subtract),
                    waits=[("ve", v3)], inc="ve")
                s1 = E("scalar", lambda s, tmp=tmp, var=var: s.activation(
                    tmp, var, AF.Ln,
                    bias=c_misc[0:1, 0:1]), waits=[("ve", v4)], inc="se")
                s2 = E("scalar", lambda s, tmp=tmp: s.activation(
                    tmp, tmp,
                    AF.Exp, scale=-0.5), waits=[("se", s1)], inc="se")
                out["mu"] = bcast_row(0, bcA, [("ve", v4)])
                out["r"] = bcast_row_ap(tmp, bcB, [("se", s2)])
            if kind == "ln1":
                var = prow2[0:1, 0:TC]
                w1v = E("vector", lambda v, var=var: v.tensor_scalar(
                    strow[0:1, 0:TC], var, 1e-5, None,
                    ALU.add), waits=[out["mu"]], inc="ve")
                w2r = E("vector", lambda v: v.reciprocal(
                    strow[0:1, 0:TC], strow[0:1, 0:TC]),
                    waits=[("ve", w1v)], inc="ve")
                w2v = E("vector", lambda v, var=var: v.tensor_tensor(
                    strow[0:1, 0:TC], var,
                    strow[0:1, 0:TC], ALU.mult), waits=[("ve", w2r)],
                    inc="ve")
                s3 = E("scalar", lambda s: s.activation(
                    strow[0:1, 0:TC], strow[0:1, 0:TC], AF.Ln,
                    bias=c_misc[0:1, 1:2]), waits=[("ve", w2v)], inc="se")
                s4 = E("scalar", lambda s: s.activation(
                    strow[0:1, 0:TC], strow[0:1, 0:TC], AF.Exp, scale=-0.5),
                    waits=[("se", s3)], inc="se")
                out["rq_row"] = ("se", s4)
                pool_edge[pool.index(prow)] = out["r"]
                pool_edge[pool.index(prow2)] = ("se", s3)
            elif prow is not None:
                pool_edge[pool.index(prow)] = out["r"]
                pool_edge[pool.index(prow2)] = ("se", s1)
            elif kind == "rms":
                p7 = stats_banks["s2"]
                v2 = E("vector", lambda v, p7=p7: v.tensor_scalar(
                    strow[0:1, 0:TC], p7[0:1, :], 1.0 / D, None, ALU.mult),
                    waits=[stats_edge[0]], inc="ve")
                pm_edge[stats_banks["i1"]] = stats_edge[0]
                pm_edge[stats_banks["i2"]] = ("ve", v2)
                s1 = E("scalar", lambda s: s.activation(
                    strow[0:1, 0:TC], strow[0:1, 0:TC], AF.Ln,
                    bias=c_misc[0:1, 1:2]), waits=[("ve", v2)], inc="se")
                s2_ = E("scalar", lambda s: s.activation(
                    strow[0:1, 0:TC], strow[0:1, 0:TC], AF.Exp, scale=-0.5),
                    waits=[("se", s1)], inc="se")
                out["rho"] = bcast_row(0, bcB, [("se", s2_)])
            return out

        front_state = {}

        def emit_front1(c):
            st = {}
            front_state[c] = st
            cpar = c % 2
            ln1_sq = stats(x0b[cpar], MD, True,
                           [[x0_ready[c]] + edges.get(f"x0_{cpar}", [])
                            for _ in range(MD)])
            st["ln1_sq"] = ln1_sq
            yield
            rows = ln_rows("ln1")
            yield
            a_v = None
            for kt in range(KD):
                xs = x0b[cpar][:, kt * TC:(kt + 1) * TC]
                t1s = t1b[cpar][:, kt * TC:(kt + 1) * TC]
                E("vector", lambda v, a=t1s, b=xs: v.tensor_tensor(
                    a, b, bcA[:], ALU.subtract),
                    waits=[rows["mu"], x0_ready[c]]
                    + edges.get(f"t1_{cpar}", []), inc="ve")
                a_v = E("vector", lambda v, a=t1s: v.tensor_tensor(
                    a, a, bcB[:], ALU.mult), waits=[rows["r"]], inc="ve")
                yield
            rq_done = bcast_row(0, bcB, [rows["rq_row"], ("ve", a_v)])
            h1_v = None
            for kt in range(KD):
                t1s = t1b[cpar][:, kt * TC:(kt + 1) * TC]
                h1s = h1b[:, kt * TC:(kt + 1) * TC]
                h1_v = E("vector", lambda v, o=h1s, a=t1s, kt=kt:
                         v.scalar_tensor_tensor(
                             o, a, c_norm1[:, kt:kt + 1], bcB[:], ALU.mult,
                             ALU.mult),
                         waits=[rq_done] + edges.get("h1b", []),
                         inc="ve")
            st["h1"] = ("ve", h1_v)
            yield
            tap("a", c, t1b[cpar][:], [128, MD * TC], F32, ("ve", a_v))
            tap("h1", c, h1b[:], [128, MD * TC], BF16, ("ve", h1_v))
            conv_done = [None] * MI
            last_pe = None
            for hs in range(8):
                src = ipw_bf.rearrange("(kt p) m -> p kt m", p=128)[
                    :, :, hs * 256:(hs + 1) * 256]
                slot, wl_v = ring_load("ipw", src)
                for mt_in in range(2):
                    mi = hs * 2 + mt_in
                    pi, pmt, pmw = next_pm()
                    for kt in range(KD):
                        lhs = wring[slot][:, kt * 256 + mt_in * 128:
                                          kt * 256 + mt_in * 128 + 128]
                        rhs = h1b[:, kt * TC:(kt + 1) * TC]
                        last_pe = E("tensor", lambda t, o=pmt, l=lhs, r=rhs,
                                    st_=(kt == 0), sp=(kt == KD - 1):
                                    t.matmul(o[:], l, r, start=st_, stop=sp),
                                    waits=[wl_v, ("ve", h1_v)] + pmw,
                                    inc="pe" if kt == KD - 1 else None)
                    _, acc, accw = next_pool()
                    w_ = [("pe", last_pe), ("glc", consts_gl),
                          edges["tail0"]] + accw
                    E("vector", lambda v, a=acc, p=pmt, mi=mi:
                      v.tensor_scalar(
                          a[:], p[:], c_convw[:, mi * 4 + 3:mi * 4 + 4],
                          None, ALU.mult), waits=w_, inc="ve")
                    for k in (2, 1, 0):
                        n = 3 - k
                        E("vector", lambda v, a=acc, p=pmt, mi=mi, k=k, n=n:
                          v.scalar_tensor_tensor(
                              a[:, n:TC], p[:, 0:TC - n],
                              c_convw[:, mi * 4 + k:mi * 4 + k + 1],
                              a[:, n:TC], ALU.mult, ALU.add), inc="ve")
                        E("vector", lambda v, a=acc, mi=mi, k=k, n=n:
                          v.scalar_tensor_tensor(
                              a[:, 0:n], tailb[:, mi * 3 + k:mi * 3 + 3],
                              c_convw[:, mi * 4 + k:mi * 4 + k + 1],
                              a[:, 0:n], ALU.mult, ALU.add), inc="ve")
                    tl_v = E("vector", lambda v, p=pmt, mi=mi: v.tensor_copy(
                        tailb[:, mi * 3:mi * 3 + 3], p[:, TC - 3:TC]),
                        inc="ve")
                    cd = E("scalar", lambda s, a=acc, mi=mi: s.activation(
                        xcb[cpar][:, mi * TC:(mi + 1) * TC], a[:], AF.Silu,
                        bias=c_convb[:, mi:mi + 1]),
                        waits=[("ve", tl_v)] + edges.get(f"xc_{cpar}", []),
                        inc="se")
                    conv_done[mi] = cd
                    pm_edge[pi] = ("se", cd)
                    pool_edge[pool.index(acc)] = ("se", cd)
                ring_release(slot, ("pe", last_pe))
                yield
            edges["tail0"] = ("ve", pb.now("ve"))
            st["conv"] = conv_done
            tap("xc", c, xcb[cpar][:], [128, MI * TC], BF16,
                ("se", max(conv_done)))

        def emit_front2(c):
            st = front_state[c]
            cpar = c % 2
            z_done = [None] * MI
            last_pe = None
            for hs in range(8, 16):
                src = ipw_bf.rearrange("(kt p) m -> p kt m", p=128)[
                    :, :, hs * 256:(hs + 1) * 256]
                slot, wl_v = ring_load("ipw", src)
                for mt_in in range(2):
                    mi = (hs - 8) * 2 + mt_in
                    pi, pmt, pmw = next_pm()
                    for kt in range(KD):
                        lhs = wring[slot][:, kt * 256 + mt_in * 128:
                                          kt * 256 + mt_in * 128 + 128]
                        rhs = h1b[:, kt * TC:(kt + 1) * TC]
                        last_pe = E("tensor", lambda t, o=pmt, l=lhs, r=rhs,
                                    st_=(kt == 0), sp=(kt == KD - 1):
                                    t.matmul(o[:], l, r, start=st_, stop=sp),
                                    waits=[wl_v, st["h1"]] + pmw,
                                    inc="pe" if kt == KD - 1 else None)
                    zd = E("scalar", lambda s, p=pmt, mi=mi: s.activation(
                        zb[:, mi * TC:(mi + 1) * TC], p[:], AF.Silu),
                        waits=[("pe", last_pe)] + edges.get("zb", []),
                        inc="se")
                    z_done[mi] = zd
                    pm_edge[pi] = ("se", zd)
                ring_release(slot, ("pe", last_pe))
            st["z"] = z_done
            edges["h1b"] = [("pe", last_pe)]
            tap("z", c, zb[:], [128, MI * TC], BF16, ("se", max(z_done)))

            def xp_part(col0, mc, evict):
                pi, pmt, pmw = next_pm()
                last = None
                for kt in range(KI):
                    lhs = r_xpw[:, kt * 96 + col0:kt * 96 + col0 + mc]
                    rhs = xcb[cpar][:, kt * TC:(kt + 1) * TC]
                    last = E("tensor", lambda t, o=pmt, l=lhs, r=rhs,
                             st_=(kt == 0), sp=(kt == KI - 1), mc=mc:
                             t.matmul(o[0:mc, :], l, r, start=st_, stop=sp),
                             waits=[("se", st["conv"][kt]),
                                    ("glc", consts_gl)] + pmw,
                             inc="pe" if kt == KI - 1 else None)
                ev = evict(pmt, last)
                pm_edge[pi] = ev
                return ev
            st["dtr"] = xp_part(0, DTR, lambda p, l: (
                "se", E("scalar", lambda s, p=p: s.activation(
                    dtrb[:], p[0:DTR, :], AF.Copy),
                    waits=[("pe", l)] + edges.get("dtrb", []), inc="se")))
            st["B"] = xp_part(DTR, NS, lambda p, l: (
                "se", E("scalar", lambda s, p=p: s.activation(
                    Bsb[:], p[0:NS, :], AF.Copy),
                    waits=[("pe", l)] + edges.get("Bsb", []), inc="se")))
            st["C"] = xp_part(DTR + NS, NS, lambda p, l: (
                "se", E("scalar", lambda s, p=p: s.activation(
                    Csb[:], p[0:NS, :], AF.Copy),
                    waits=[("pe", l)] + edges.get("Bsb", []), inc="se")))
            tap("B", c, Bsb[:], [NS, TC], BF16, st["B"])
            tap("C", c, Csb[:], [NS, TC], BF16, st["C"])

        def emit_scan(c, pull):
            st = front_state[c]
            cpar = c % 2
            yz_done = [None] * MI
            for mi in range(MI):
                dts = dtb2[:, 0:TC]
                pi, pmt, pmw = next_pm()
                last = E("tensor", lambda t, o=pmt, mi=mi: t.matmul(
                    o[:], r_dtw[:, mi * 128:(mi + 1) * 128], dtrb[:],
                    start=True, stop=True),
                    waits=[st["dtr"], ("glc", consts_gl)] + pmw, inc="pe")
                _, p_av, w_av = next_pool()
                _, p_m, w_m = next_pool()
                E("vector", lambda v, a=p_av, p=pmt, mi=mi: v.tensor_scalar(
                    a[:], p[:], c_dtb[:, mi:mi + 1], 0.0, ALU.add, ALU.min),
                    waits=[("pe", last)] + w_av + w_m, inc="ve")
                E("vector", lambda v, a=p_m, p=pmt, mi=mi: v.tensor_scalar(
                    a[:], p[:], c_dtb[:, mi:mi + 1], 0.0, ALU.add, ALU.max),
                    inc="ve")
                E("vector", lambda v, a=p_av, b=p_m: v.tensor_tensor(
                    a[:], a[:], b[:], ALU.subtract), inc="ve")
                pm_edge[pi] = ("ve", pb.now("ve"))
                s_e = E("scalar", lambda s, a=p_av: s.activation(
                    a[:], a[:], AF.Exp), waits=[("ve", pb.now("ve"))],
                    inc="se")
                s_l = E("scalar", lambda s, a=p_av: s.activation(
                    a[:], a[:], AF.Ln, bias=c_misc[:, 2:3]),
                    waits=[("se", s_e)], inc="se")
                w_dt = [("se", s_l)]
                if dt_edge[0]:
                    w_dt.append(dt_edge[0])
                dt_v = E("vector", lambda v, o=dts, a=p_m, b=p_av:
                         v.tensor_tensor(o, a[:], b[:], ALU.add),
                         waits=w_dt, inc="ve")
                pool_edge[pool.index(p_av)] = ("ve", dt_v)
                pool_edge[pool.index(p_m)] = ("ve", dt_v)
                if mi == 0:
                    tap("dt0", c, dts, [128, TC], F32, ("ve", dt_v))
                w_v = E("vector", lambda v, o=wbuf, a=dts, mi=mi:
                        v.tensor_tensor(
                            o[:], a, xcb[cpar][:, mi * TC:(mi + 1) * TC],
                            ALU.mult),
                        waits=[("se", st["conv"][mi])], inc="ve")
                y_v = None
                last_dA = None
                for s_i in range(NS):
                    spar = 0
                    dAs = dAb[:, 0:TC]
                    wA = [pbA_edge[0]] if pbA_edge[0] else []
                    wB = [pbB_edge[0]] if pbB_edge[0] else []
                    bA = E("tensor", lambda t, s_i=s_i: t.matmul(
                        pbA[:], c_sel16[:, s_i * 128:(s_i + 1) * 128],
                        Bsb[:], start=True, stop=True),
                        waits=[st["B"]] + wA, inc="pe")
                    bB = E("tensor", lambda t, s_i=s_i: t.matmul(
                        pbB[:], c_sel16[:, s_i * 128:(s_i + 1) * 128],
                        Csb[:], start=True, stop=True),
                        waits=[st["C"]] + wB, inc="pe")
                    w_da = [("ve", dt_v), ("ve", vA2)]
                    if dA_edge[spar]:
                        w_da.append(dA_edge[spar])
                    dA_v = E("scalar", lambda s, o=dAs, i=dts, mi=mi,
                             s_i=s_i: s.activation(
                                 o, i, AF.Exp,
                                 scale=c_Alog[:, mi * NS + s_i:
                                              mi * NS + s_i + 1]),
                             waits=w_da, inc="se")
                    last_dA = dA_v
                    u_v = E("vector", lambda v: v.tensor_tensor(
                        ubuf[:], wbuf[:], pbA[:], ALU.mult),
                        waits=[("pe", bA), ("ve", w_v)], inc="ve")
                    pbA_edge[0] = ("ve", u_v)
                    idx = mi * NS + s_i
                    sc_v = E("vector", lambda v, a=dAs, idx=idx:
                             v.tensor_tensor_scan(
                                 hbuf[:], a, ubuf[:], carry[:, idx:idx + 1],
                                 ALU.mult, ALU.add),
                             waits=[("se", dA_v)], inc="ve")
                    dA_edge[spar] = ("ve", sc_v)
                    E("vector", lambda v, idx=idx: v.tensor_copy(
                        carry[:, idx:idx + 1], hbuf[:, TC - 1:TC]),
                        waits=[("ve", sc_v)], inc="ve")
                    if s_i == 0:
                        y_v = E("vector", lambda v: v.tensor_tensor(
                            ybuf[:], hbuf[:], pbB[:], ALU.mult),
                            waits=[("pe", bB), ("ve", sc_v)], inc="ve")
                        pbB_edge[0] = ("ve", y_v)
                    else:
                        p_v = E("vector", lambda v: v.tensor_tensor(
                            hbuf[:], hbuf[:], pbB[:], ALU.mult),
                            waits=[("pe", bB), ("ve", sc_v)], inc="ve")
                        pbB_edge[0] = ("ve", p_v)
                        y_v = E("vector", lambda v: v.tensor_tensor(
                            ybuf[:], ybuf[:], hbuf[:], ALU.add), inc="ve")
                    pull(2)
                dt_edge[0] = ("se", last_dA)
                if mi == 0:
                    tap("y0", c, ybuf[:], [128, TC], F32, ("ve", y_v))
                _, p_t, w_t = next_pool()
                E("vector", lambda v, a=p_t, mi=mi: v.scalar_tensor_tensor(
                    a[:], xcb[cpar][:, mi * TC:(mi + 1) * TC],
                    c_Dp[:, mi:mi + 1], ybuf[:], ALU.mult, ALU.add),
                    waits=w_t, inc="ve")
                yzv = E("vector", lambda v, a=p_t, mi=mi: v.tensor_tensor(
                    xcb[cpar][:, mi * TC:(mi + 1) * TC], a[:],
                    zb[:, mi * TC:(mi + 1) * TC], ALU.mult),
                    waits=[("se", st["z"][mi])], inc="ve")
                yz_done[mi] = yzv
                pool_edge[pool.index(p_t)] = ("ve", yzv)
            tap("carry", c, carry[:], [128, MI * NS], F32,
                ("ve", pb.now("ve")))
            st["yz"] = yz_done
            edges["zb"] = [("ve", max(yz_done))]
            edges["dtrb"] = [("pe", pb.now("pe"))]
            edges["Bsb"] = [("ve", pb.now("ve"))]
            tap("yz", c, xcb[cpar][:], [128, MI * TC], BF16,
                ("ve", max(yz_done)))

        def router_gates(c, rhs_buf, rhs_done, wtile, out_et, eps, tapname):
            pi, pmt, pmw = next_pm()
            last = None
            for kt in range(KD):
                lhs = wtile[:, kt * NE:(kt + 1) * NE]
                rhs = rhs_buf[:, kt * TC:(kt + 1) * TC]
                last = E("tensor", lambda t, o=pmt, l=lhs, r=rhs,
                         st_=(kt == 0), sp=(kt == KD - 1): t.matmul(
                             o[0:NE, :], l, r, start=st_, stop=sp),
                         waits=[rhs_done, ("glc", consts_gl)] + pmw,
                         inc="pe" if kt == KD - 1 else None)
            w_rl = [("pe", last)]
            if router_edge[0]:
                w_rl.append(router_edge[0])
            rl_v = E("scalar", lambda s, p=pmt: s.activation(
                RLb[:], p[0:NE, :], AF.Copy), waits=w_rl, inc="se")
            pm_edge[pi] = ("se", rl_v)
            yield
            wA = [pbA_edge[0]] if pbA_edge[0] else []
            tp = None
            for ti in range(4):
                tp = E("tensor", lambda t, ti=ti: t.transpose(
                    pbA[:, ti * 8:(ti + 1) * 8],
                    RLb[:, ti * 128:(ti + 1) * 128], c_ident[0:8, 0:8]),
                    waits=[("se", rl_v)] + wA, inc="pe")
            gt_v = E("vector", lambda v: v.tensor_copy(
                Gtb[:], pbA[:, 0:32]), waits=[("pe", tp)], inc="ve")
            pbA_edge[0] = ("ve", gt_v)
            yield
            gv = gt_v
            for ti in range(4):
                g8 = Gtb[:, ti * 8:(ti + 1) * 8]
                m1 = rsc[:, 0:1]
                m2 = rsc[:, 1:2]
                sm = rsc[:, 8:16]
                p2 = rsc[:, 16:24]
                def VS(build, extra=()):
                    nonlocal gv
                    gv = E("vector", build,
                           waits=[("ve", gv)] + list(extra), inc="ve")
                    return gv
                VS(lambda v, g8=g8, m1=m1: v.tensor_reduce(
                    m1, g8, AX.X, ALU.max))
                VS(lambda v, g8=g8, m1=m1, sm=sm: v.tensor_scalar(
                    sm, g8, m1, None, ALU.subtract))
                se_v = E("scalar", lambda s, sm=sm: s.activation(
                    sm, sm, AF.Exp), waits=[("ve", gv)], inc="se")
                VS(lambda v, sm=sm, m1=m1: v.tensor_reduce(
                    m1, sm, AX.X, ALU.add), extra=[("se", se_v)])
                VS(lambda v, m1=m1: v.reciprocal(m1, m1))
                VS(lambda v, g8=g8, sm=sm, m1=m1: v.tensor_scalar(
                    g8, sm, m1, None, ALU.mult))
                VS(lambda v, g8=g8, m1=m1: v.tensor_reduce(
                    m1, g8, AX.X, ALU.max))
                VS(lambda v, g8=g8, m1=m1, sm=sm: v.tensor_scalar(
                    sm, g8, m1, None, ALU.is_ge))
                VS(lambda v, g8=g8, sm=sm, p2=p2: v.tensor_tensor(
                    p2, g8, sm, ALU.mult))
                VS(lambda v, g8=g8, p2=p2: v.tensor_tensor(
                    p2, g8, p2, ALU.subtract))
                VS(lambda v, p2=p2, m2=m2: v.tensor_reduce(
                    m2, p2, AX.X, ALU.max))
                VS(lambda v, p2=p2, m2=m2: v.tensor_scalar(
                    p2, p2, m2, None, ALU.is_ge))
                VS(lambda v, sm=sm, p2=p2: v.tensor_tensor(
                    sm, sm, p2, ALU.add))
                VS(lambda v, m1=m1, m2=m2: v.tensor_tensor(
                    m1, m1, m2, ALU.add))
                if eps:
                    VS(lambda v, m1=m1: v.tensor_scalar(
                        m1, m1, eps, None, ALU.add))
                VS(lambda v, g8=g8, sm=sm: v.tensor_tensor(
                    g8, g8, sm, ALU.mult))
                VS(lambda v, m1=m1: v.reciprocal(m1, m1))
                VS(lambda v, g8=g8, m1=m1: v.tensor_scalar(
                    g8, g8, m1, None, ALU.mult))
            yield
            wB = [pbB_edge[0]] if pbB_edge[0] else []
            tp2 = None
            for ti in range(4):
                tp2 = E("tensor", lambda t, ti=ti: t.transpose(
                    pbB[0:8, ti * 128:(ti + 1) * 128],
                    Gtb[:, ti * 8:(ti + 1) * 8], c_ident[:]),
                    waits=[("ve", gv)] + wB, inc="pe")
            et_v = E("vector", lambda v, o=out_et: v.tensor_copy(
                o[:], pbB[0:8, :]),
                waits=[("pe", tp2)] + edges.get(out_et.name, []), inc="ve")
            pbB_edge[0] = ("ve", et_v)
            router_edge[0] = ("pe", tp2)
            tap(tapname, c, out_et[:], [NE, TC], BF16, ("ve", et_v))
            front_state[c][tapname] = ("ve", et_v)

        def tail_units(c):
            st = front_state[c]
            cpar = c % 2
            xa_done = [None] * MD
            for hs in range(8):
                src = opw_bf.rearrange("(kt p) m -> p kt m", p=128)[
                    :, :, hs * 128:(hs + 1) * 128]
                slot, wl_v = ring_load("opw", src)
                m = hs
                pi, pmt, pmw = next_pm()
                last_pe = None
                for kt in range(KI):
                    lhs = wring[slot][:, kt * 128:kt * 128 + 128]
                    rhs = xcb[cpar][:, kt * TC:(kt + 1) * TC]
                    last_pe = E("tensor", lambda t, o=pmt, l=lhs, r=rhs,
                                st_=(kt == 0), sp=(kt == KI - 1): t.matmul(
                                    o[:], l, r, start=st_, stop=sp),
                                waits=[wl_v, ("ve", st["yz"][kt])] + pmw,
                                inc="pe" if kt == KI - 1 else None)
                xv = E("vector", lambda v, p=pmt, m=m: v.tensor_tensor(
                    t1b[cpar][:, m * TC:(m + 1) * TC],
                    t1b[cpar][:, m * TC:(m + 1) * TC], p[:], ALU.add),
                    waits=[("pe", last_pe)], inc="ve")
                xa_done[m] = xv
                pm_edge[pi] = ("ve", xv)
                ring_release(slot, ("pe", last_pe))
                yield
            edges[f"xc_{cpar}"] = [("pe", pb.now("pe"))]
            tap("xa", c, t1b[cpar][:], [128, MD * TC], F32,
                ("ve", max(xa_done)))
            acc_done = [None] * MD
            for m in range(MD):
                acc_done[m] = E("gpsimd", lambda g,
                                a=x0b[cpar][:, m * TC:(m + 1) * TC],
                                b=t1b[cpar][:, m * TC:(m + 1) * TC]:
                                g.tensor_tensor(a, a, b, ALU.add),
                                waits=[("ve", xa_done[m]), x0_ready[c],
                                       ("se", st["ln1_sq"])], inc="ge")
            yield
            stats(t1b[cpar], MD, False, [[("ve", xa_done[kt])]
                                         for kt in range(MD)])
            rows = ln_rows("rms")
            ht_v = None
            for kt in range(KD):
                ht_v = E("vector", lambda v, kt=kt: v.scalar_tensor_tensor(
                    htb[:, kt * TC:(kt + 1) * TC],
                    t1b[cpar][:, kt * TC:(kt + 1) * TC],
                    c_norm2[:, kt:kt + 1], bcB[:], ALU.mult, ALU.mult),
                    waits=[rows["rho"], ("ve", xa_done[kt])]
                    + edges.get("htb", []), inc="ve")
            st["ht"] = ("ve", ht_v)
            edges[f"t1_{cpar}"] = [("se", pb.now("se")), ("ve", ht_v),
                                   ("ge", acc_done[-1])]
            tap("ht", c, htb[:], [128, MD * TC], BF16, ("ve", ht_v))
            yield
            def w1_stage(e, key, wsrc, rhs_done, bias_c):
                hid_done = [None] * KI
                for hs in range(8):
                    src = wsrc.rearrange("e (kt p) m -> e p kt m", p=128)[
                        e][:, :, hs * 256:(hs + 1) * 256]
                    slot, wl_v = ring_load((key, e), src)
                    last_pe = None
                    for mt_in in range(2):
                        hi = hs * 2 + mt_in
                        pi, pmt, pmw = next_pm()
                        for kt in range(KD):
                            lhs = wring[slot][:, kt * 256 + mt_in * 128:
                                              kt * 256 + mt_in * 128 + 128]
                            rhs = htb[:, kt * TC:(kt + 1) * TC]
                            last_pe = E("tensor", lambda t, o=pmt, l=lhs,
                                        r=rhs, st_=(kt == 0),
                                        sp=(kt == KD - 1): t.matmul(
                                            o[:], l, r, start=st_, stop=sp),
                                        waits=[wl_v, rhs_done] + pmw,
                                        inc="pe" if kt == KD - 1 else None)
                        hv = E("scalar", lambda s, p=pmt, hi=hi, e=e:
                               s.activation(
                                   hidb[:, hi * TC:(hi + 1) * TC], p[:],
                                   AF.Identity,
                                   bias=bias_c[:, e * MI + hi:
                                               e * MI + hi + 1]),
                               waits=[("pe", last_pe)]
                               + edges.get("hidb", []), inc="se")
                        hid_done[hi] = hv
                        pm_edge[pi] = ("se", hv)
                    ring_release(slot, ("pe", last_pe))
                    yield
                hid_ret[0] = hid_done

            hid_ret = [None]
            w1_e0 = w1_stage(0, "w1", w1_bf, ("ve", ht_v), c_b1)
            for _ in w1_e0:
                yield
            hid_e0 = hid_ret[0]
            for _ in router_gates(c, htb, ("ve", ht_v), r_rtw, Get, 0.0,
                                  "gates"):
                yield
            get_done = st["gates"]
            for e in range(NE):
                wA = [pbA_edge[0]] if pbA_edge[0] else []
                gb_v = E("tensor", lambda t, e=e: t.matmul(
                    pbA[:], c_sel16[0:8, e * 128:(e + 1) * 128], Get[:],
                    start=True, stop=True), waits=[get_done] + wA, inc="pe")
                gs_v = E("vector", lambda v: v.tensor_copy(Gsb[:], pbA[:]),
                         waits=[("pe", gb_v)] + edges.get("Gsb", []),
                         inc="ve")
                pbA_edge[0] = ("ve", gs_v)
                edges["Gsb"] = [("ve", gs_v)]
                if e == 0:
                    hid_done = hid_e0
                else:
                    for _ in w1_stage(e, "w1", w1_bf, ("ve", ht_v), c_b1):
                        yield
                    hid_done = hid_ret[0]
                gelu_h1 = E("scalar", lambda s: s.activation(
                    hidb[:, 0:8 * TC], hidb[:, 0:8 * TC],
                    AF.Gelu_apprx_tanh),
                    waits=[("se", max(hid_done[0:8]))], inc="se")
                gelu_v = E("scalar", lambda s: s.activation(
                    hidb[:, 8 * TC:16 * TC], hidb[:, 8 * TC:16 * TC],
                    AF.Gelu_apprx_tanh),
                    waits=[("se", max(hid_done))], inc="se")
                yield
                for hs in range(8):
                    src = w2_bf.rearrange("e (kt p) m -> e p kt m", p=128)[
                        e][:, :, hs * 128:(hs + 1) * 128]
                    slot, wl_v = ring_load(("w2", e), src)
                    m = hs
                    pi, pmt, pmw = next_pm()
                    last_pe = None
                    for kt in range(KI):
                        lhs = wring[slot][:, kt * 128:kt * 128 + 128]
                        rhs = hidb[:, kt * TC:(kt + 1) * TC]
                        gw_ = ("se", gelu_h1 if kt < 8 else gelu_v)
                        last_pe = E("tensor", lambda t, o=pmt, l=lhs, r=rhs,
                                    st_=(kt == 0), sp=(kt == KI - 1):
                                    t.matmul(o[:], l, r, start=st_, stop=sp),
                                    waits=[wl_v, gw_] + pmw,
                                    inc="pe" if kt == KI - 1 else None)
                    _, p_t, w_t = next_pool()
                    sv = E("scalar", lambda s, a=p_t, p=pmt, m=m, e=e:
                           s.activation(
                               a[:], p[:], AF.Identity,
                               bias=c_b2[:, e * MD + m:e * MD + m + 1]),
                           waits=[("pe", last_pe)] + w_t, inc="se")
                    pm_edge[pi] = ("se", sv)
                    gm = E("gpsimd", lambda g, a=p_t: g.tensor_tensor(
                        a[:], a[:], Gsb[:], ALU.mult),
                        waits=[("se", sv), ("ve", gs_v)], inc="ge")
                    gv2 = E("gpsimd", lambda g, a=p_t,
                            o=x0b[cpar][:, m * TC:(m + 1) * TC]:
                            g.tensor_tensor(o, o, a[:], ALU.add),
                            waits=[("ge", acc_done[m])],
                            inc="ge")
                    acc_done[m] = gv2
                    pool_edge[pool.index(p_t)] = ("ge", gv2)
                    ring_release(slot, ("pe", last_pe))
                    yield
                edges["Gsb"] = [("ge", pb.now("ge"))]
            edges["htb"] = [("pe", pb.now("pe"))]
            edges["hidb"] = [("pe", pb.now("pe"))]
            x1_done = [("ge", v) for v in acc_done]
            tap("x1", c, x0b[cpar][:], [128, MD * TC], F32, x1_done[-1])
            ln2_sq = stats(x0b[cpar], MD, True, [[x1_done[kt]]
                                                 for kt in range(MD)])
            rows2 = ln_rows("ln2")
            b2_v = None
            for kt in range(MD):
                _, p_t, w_t = next_pool()
                E("vector", lambda v, a=p_t,
                  b=x0b[cpar][:, kt * TC:(kt + 1) * TC]: v.tensor_tensor(
                      a[:], b, bcA[:], ALU.subtract),
                  waits=[rows2["mu"], x1_done[kt]] + w_t, inc="ve")
                b2_v = E("vector", lambda v, a=p_t, kt=kt: v.tensor_tensor(
                    htb[:, kt * TC:(kt + 1) * TC], a[:], bcB[:], ALU.mult),
                    waits=[rows2["r"]] + edges.get("htb", []), inc="ve")
                pool_edge[pool.index(p_t)] = ("ve", b2_v)
            tap("b2nd", c, htb[:], [128, MD * TC], BF16, ("ve", b2_v))
            yield
            sg_e0_gen = w1_stage(0, "gw", gw_bf, ("ve", b2_v), c_gb)
            for _ in sg_e0_gen:
                yield
            sg_e0 = hid_ret[0]
            for _ in router_gates(c, htb, ("ve", b2_v), r_frw, Pet, 1e-9,
                                  "pgates"):
                yield
            pet_done = st["pgates"]
            facc_done = list(x1_done)
            for e in range(NE):
                wA = [pbA_edge[0]] if pbA_edge[0] else []
                pb_v = E("tensor", lambda t, e=e: t.matmul(
                    pbA[:], c_sel16[0:8, e * 128:(e + 1) * 128], Pet[:],
                    start=True, stop=True), waits=[pet_done] + wA, inc="pe")
                gs_v = E("vector", lambda v: v.tensor_copy(Gsb[:], pbA[:]),
                         waits=[("pe", pb_v)] + edges.get("Gsb", []),
                         inc="ve")
                pbA_edge[0] = ("ve", gs_v)
                edges["Gsb"] = [("ve", gs_v)]
                if e == 0:
                    sg_done = sg_e0
                else:
                    for _ in w1_stage(e, "gw", gw_bf, ("ve", b2_v), c_gb):
                        yield
                    sg_done = hid_ret[0]
                silu_h1 = E("scalar", lambda s: s.activation(
                    hidb[:, 0:8 * TC], hidb[:, 0:8 * TC], AF.Silu),
                    waits=[("se", max(sg_done[0:8]))], inc="se")
                silu_v = E("scalar", lambda s: s.activation(
                    hidb[:, 8 * TC:16 * TC], hidb[:, 8 * TC:16 * TC],
                    AF.Silu),
                    waits=[("se", max(sg_done))], inc="se")
                yield
                h2_done = [None] * KI
                for hs in range(8):
                    src = uw_bf.rearrange("e (kt p) m -> e p kt m", p=128)[
                        e][:, :, hs * 256:(hs + 1) * 256]
                    slot, wl_v = ring_load(("uw", e), src)
                    last_pe = None
                    for mt_in in range(2):
                        fi = hs * 2 + mt_in
                        pi, pmt, pmw = next_pm()
                        for kt in range(KD):
                            lhs = wring[slot][:, kt * 256 + mt_in * 128:
                                              kt * 256 + mt_in * 128 + 128]
                            rhs = htb[:, kt * TC:(kt + 1) * TC]
                            last_pe = E("tensor", lambda t, o=pmt, l=lhs,
                                        r=rhs, st_=(kt == 0),
                                        sp=(kt == KD - 1): t.matmul(
                                            o[:], l, r, start=st_, stop=sp),
                                        waits=[wl_v, ("ve", b2_v)] + pmw,
                                        inc="pe" if kt == KD - 1 else None)
                        sw_ = ("se", silu_h1 if fi < 8 else silu_v)
                        _, p_u, w_u = next_pool()
                        uv = E("scalar", lambda s, a=p_u, p=pmt, fi=fi, e=e:
                               s.activation(
                                   a[:], p[:], AF.Identity,
                                   bias=c_ub[:, e * MI + fi:
                                             e * MI + fi + 1]),
                               waits=[("pe", last_pe)] + w_u, inc="se")
                        pm_edge[pi] = ("se", uv)
                        hv = E("gpsimd", lambda g, a=p_u, fi=fi:
                               g.tensor_tensor(
                                   hidb[:, fi * TC:(fi + 1) * TC],
                                   hidb[:, fi * TC:(fi + 1) * TC], a[:],
                                   ALU.mult),
                               waits=[("se", uv), sw_], inc="ge")
                        h2_done[fi] = hv
                        pool_edge[pool.index(p_u)] = ("ge", hv)
                    ring_release(slot, ("pe", last_pe))
                    yield
                for hs in range(8):
                    src = dw_bf.rearrange("e (kt p) m -> e p kt m", p=128)[
                        e][:, :, hs * 128:(hs + 1) * 128]
                    slot, wl_v = ring_load(("dw", e), src)
                    m = hs
                    pi, pmt, pmw = next_pm()
                    last_pe = None
                    for kt in range(KI):
                        lhs = wring[slot][:, kt * 128:kt * 128 + 128]
                        rhs = hidb[:, kt * TC:(kt + 1) * TC]
                        last_pe = E("tensor", lambda t, o=pmt, l=lhs, r=rhs,
                                    st_=(kt == 0), sp=(kt == KI - 1):
                                    t.matmul(o[:], l, r, start=st_, stop=sp),
                                    waits=[wl_v, ("ge", h2_done[kt])] + pmw,
                                    inc="pe" if kt == KI - 1 else None)
                    _, p_t, w_t = next_pool()
                    sv = E("scalar", lambda s, a=p_t, p=pmt, m=m, e=e:
                           s.activation(
                               a[:], p[:], AF.Identity,
                               bias=c_db[:, e * MD + m:e * MD + m + 1]),
                           waits=[("pe", last_pe)] + w_t, inc="se")
                    pm_edge[pi] = ("se", sv)
                    gm = E("gpsimd", lambda g, a=p_t: g.tensor_tensor(
                        a[:], a[:], Gsb[:], ALU.mult),
                        waits=[("se", sv), ("ve", gs_v)], inc="ge")
                    gv2 = E("gpsimd", lambda g, a=p_t,
                            o=x0b[cpar][:, m * TC:(m + 1) * TC]:
                            g.tensor_tensor(o, o, a[:], ALU.add),
                            waits=[facc_done[m], ("ve", b2_v),
                                   ("se", ln2_sq)], inc="ge")
                    facc_done[m] = ("ge", gv2)
                    pool_edge[pool.index(p_t)] = ("ge", gv2)
                    ring_release(slot, ("pe", last_pe))
                    yield
                edges["Gsb"] = [("ge", pb.now("ge"))]
            edges["hidb"] = [("pe", pb.now("pe"))]
            edges["htb"] = [("pe", pb.now("pe"))]
            out_done = pb.now("ge")
            dst = out.rearrange("(kt p) t -> p kt t", p=128)[
                :, :, c * TC:(c + 1) * TC]
            srcv = x0b[cpar][:].rearrange("p (kt t) -> p kt t", t=TC)
            st_sem = pb.sem(f"sl{cpar}")
            st_v = E("gpsimd", lambda g, d=dst, s=srcv: g.dma_start(d, s),
                     waits=[("ge", out_done), x0_ready[c]], inc=st_sem,
                     inck=16)
            if c + 2 < NCH:
                x0_load(c + 2, waits=[(st_sem, st_v)])
            edges[f"x0_{cpar}"] = [(st_sem, st_v)]
            edges[Get.name] = [("pe", pb.now("pe"))]
            edges[Pet.name] = [("pe", pb.now("pe"))]
            del front_state[c]
            yield

        def chain2(g1, g2):
            if g1 is not None:
                for x in g1:
                    yield x
            if g2 is not None:
                for x in g2:
                    yield x

        gens = {}
        for _ in emit_front1(0):
            pass
        for it in range(NCH + 1):
            if it < NCH:
                emit_front2(it)
                ch = chain2(gens.pop(it - 1, None),
                            emit_front1(it + 1) if it + 1 < NCH else None)
                def pull(k, ch=ch):
                    if not interleave:
                        return
                    for _ in range(k):
                        if next(ch, StopIteration) is StopIteration:
                            break
                emit_scan(it, pull)
                for _ in ch:
                    pass
                gens[it] = tail_units(it)
            else:
                for _ in gens.pop(it - 1):
                    pass

        fin = [(s, pb.now(s)) for s in ("sl0", "sl1", "tl") if s in pb.semc]
        E("gpsimd", lambda g: g.nop(), waits=fin)

        with nc.Block() as block:
            @block.sync
            def _(e):
                for f in pb.order["sync"]:
                    f(e)

            @block.tensor
            def _(e):
                for f in pb.order["tensor"]:
                    f(e)

            @block.vector
            def _(e):
                for f in pb.order["vector"]:
                    f(e)

            @block.scalar
            def _(e):
                for f in pb.order["scalar"]:
                    f(e)

            @block.gpsimd
            def _(e):
                for f in pb.order["gpsimd"]:
                    f(e)

    return nc, taps


def _host_inputs(inputs, b):
    f = np.ascontiguousarray
    ip = inputs
    m = {
        "xT": f(np.asarray(ip["L0_attn_x"][b]).T),
        "ipw": f(ip["L0_attn_in_proj_w"]),
        "opw": f(ip["L0_attn_out_proj_w"]),
        "xpw": f(ip["L0_attn_x_proj_w"]),
        "dtw": f(ip["L0_attn_dt_proj_w"]),
        "w1": f(ip["L0_attn_w1"]),
        "w2": f(ip["L0_attn_w2"]),
        "gw": f(np.asarray(ip["L0_ffn_gate_w"])),
        "uw": f(np.asarray(ip["L0_ffn_up_w"])),
        "dw": f(np.asarray(ip["L0_ffn_down_w"])),
        "rtw": f(ip["L0_attn_router_w"]),
        "frw": f(np.asarray(ip["L0_ffn_router_w"]).T),
        "norm1c": f(np.asarray(ip["L0_attn_norm1_w"]).reshape(8, 128).T),
        "norm2c": f(np.asarray(ip["L0_attn_norm2_w"]).reshape(8, 128).T),
        "convw_c": f(np.asarray(ip["L0_attn_conv_w"]).reshape(16, 128, 4)
                     .transpose(1, 0, 2).reshape(128, 64)),
        "convb_c": f(np.asarray(ip["L0_attn_conv_b"]).reshape(16, 128).T),
        "dtb_c": f(np.asarray(ip["L0_attn_dt_proj_b"]).reshape(16, 128).T),
        "Dp_c": f(np.asarray(ip["L0_attn_D_param"]).reshape(16, 128).T),
        "Alog_c": f(np.asarray(ip["L0_attn_A_log"]).reshape(16, 128, 16)
                    .transpose(1, 0, 2).reshape(128, 16 * 16)),
        "b1_c": f(np.asarray(ip["L0_attn_b1"]).reshape(8, 16, 128)
                  .transpose(2, 0, 1).reshape(128, 8 * 16)),
        "b2_c": f(np.asarray(ip["L0_attn_b2"]).reshape(8, 8, 128)
                  .transpose(2, 0, 1).reshape(128, 8 * 8)),
        "gb_c": f(np.asarray(ip["L0_ffn_gate_b"]).reshape(8, 16, 128)
                  .transpose(2, 0, 1).reshape(128, 8 * 16)),
        "ub_c": f(np.asarray(ip["L0_ffn_up_b"]).reshape(8, 16, 128)
                  .transpose(2, 0, 1).reshape(128, 8 * 16)),
        "db_c": f(np.asarray(ip["L0_ffn_down_b"]).reshape(8, 8, 128)
                  .transpose(2, 0, 1).reshape(128, 8 * 8)),
        "ident_c": np.eye(128, dtype=np.float32),
        "misc_c": np.tile(np.array([[1e-5, 1e-6, 1.0, 0.0]], np.float32),
                          (128, 1)),
    }
    import ml_dtypes
    sel = np.zeros((16, 16, 128), np.float32)
    for s in range(16):
        sel[s, s, :] = 1.0
    m["sel16_c"] = f(sel.transpose(1, 0, 2).reshape(16, 16 * 128)
                     .astype(ml_dtypes.bfloat16))
    m["onescol_c"] = np.ones((128, 1), np.float32).astype(ml_dtypes.bfloat16)
    m["onesrow_c"] = np.ones((1, 128), np.float32).astype(ml_dtypes.bfloat16)
    for k in m:
        if m[k].dtype == np.float64:
            m[k] = m[k].astype(np.float32)
    return m


def kernel(**inputs):
    nc, _ = build_program(DEBUG_CHUNKS)
    in_maps = [_host_inputs(inputs, b) for b in range(N_CORES)]
    res = run_bass_kernel_spmd(nc, in_maps, core_ids=list(range(N_CORES)))
    outs = [np.ascontiguousarray(np.asarray(r["out"]).T)
            for r in res.results]
    return np.stack(outs, 0)
